# version 10
# baseline (speedup 1.0000x reference)
import math
from contextlib import ExitStack

import numpy as np
import ml_dtypes
import concourse.bass as bass
import concourse.bass_isa as bass_isa
from concourse.library_config import all_libraries
LIB_ATTN = all_libraries[1]
import concourse.mybir as mybir
from concourse.bass_utils import run_bass_kernel_spmd

F32 = mybir.dt.float32
BF16 = mybir.dt.bfloat16
I32 = mybir.dt.int32
AX = mybir.AxisListType.X
OP = mybir.AluOpType
AF = mybir.ActivationFunctionType

H = 2048
T = 4096
CH = 512
NCH = T // CH
NAH = 16
AHD = 128
NFH = 4
FHD = 512
WIN = 1024
FFI = 8192
LN2 = math.log(2.0)
BASE_LR_INV = 0.001 + float(np.log(-np.expm1(-0.001)))
ATT_SCALE = 1.0 / math.sqrt(AHD)

STREAMS = ("pe", "dve", "act", "gp", "sync")
NPOOL = 8
DMA_QUEUES = ("sd", "ad", "gd")
SEMS = ("pe", "dve", "act", "gp") + tuple(
    f"{q}{i}" for q in DMA_QUEUES for i in range(NPOOL))
STREAM_SEM = {"pe": "pe", "dve": "dve", "act": "act", "gp": "gp"}


class Sched:

    def __init__(self):
        self.ops = {s: [] for s in STREAMS}
        self.cnt = {s: 0 for s in SEMS}
        self.water = {s: {p: 0 for p in SEMS} for s in STREAMS}
        self.bufs = {}
        self.rr = {q: 0 for q in DMA_QUEUES}

    def _buf(self, key):
        b = self.bufs.get(key)
        if b is None:
            b = {"w": None, "r": {}}
            self.bufs[key] = b
        return b

    def op(self, stream, emit, ins=(), outs=(), sem=None, inc=None):
        isdma = sem in DMA_QUEUES
        if isdma:
            q = sem
            sem = f"{q}{self.rr[q] % NPOOL}"
            self.rr[q] += 1
        elif sem is None:
            sem = STREAM_SEM[stream]
        if inc is None:
            inc = 16 if isdma else 1
        waits = {}

        def need(psem, pcnt):
            if stream == "pe" and psem == "pe":
                return
            if pcnt > self.water[stream][psem]:
                waits[psem] = max(waits.get(psem, 0), pcnt)

        if isdma and self.cnt[sem] > 0:
            need(sem, self.cnt[sem])
        for k in ins:
            b = self._buf(k)
            if b["w"] is not None:
                need(*b["w"])
        for k in outs:
            b = self._buf(k)
            if b["w"] is not None:
                need(*b["w"])
            for psem, pcnt in b["r"].items():
                need(psem, pcnt)
        for psem, pcnt in waits.items():
            self.water[stream][psem] = pcnt
        self.cnt[sem] += inc
        mycnt = self.cnt[sem]
        self.ops[stream].append((sorted(waits.items()), emit, sem if inc else None, inc))
        for k in ins:
            self._buf(k)["r"][sem] = mycnt
        for k in outs:
            b = self._buf(k)
            b["w"] = (sem, mycnt)
            b["r"] = {}

    def barrier(self):
        snap = dict(self.cnt)
        for s in STREAMS:
            for p in SEMS:
                if p == STREAM_SEM.get(s):
                    continue
                if snap[p] > self.water[s][p]:
                    self.water[s][p] = snap[p]
                    self.ops[s].append(([(p, snap[p])], None, None, 0))


class Arena:

    def __init__(self, ap, cols, name):
        self.ap = ap
        self.cols = cols
        self.off = 0
        self.name = name
        self.marks = []

    def tile(self, n):
        assert self.off + n <= self.cols, f"{self.name} arena overflow: {self.off}+{n}>{self.cols}"
        v = self.ap[:, self.off:self.off + n]
        self.off += n
        return v

    def push(self):
        self.marks.append(self.off)

    def pop(self):
        self.off = self.marks.pop()

    def reset(self):
        self.off = 0
        self.marks = []


def cdiv(a, b):
    return (a + b - 1) // b



_ROPE_CACHE = {}


def rope_tables():
    if "C" in _ROPE_CACHE:
        return _ROPE_CACHE["C"], _ROPE_CACHE["S"]
    import os
    import subprocess
    import sys
    code = (
        "import numpy as np\n"
        "import jax\n"
        "jax.config.update('jax_platforms', 'cpu')\n"
        "import jax.numpy as jnp\n"
        f"AHD={AHD}; T={T}\n"
        "inv_freq = 1.0 / 500000.0 ** (jnp.arange(0, AHD, 2, dtype=jnp.float32) / AHD)\n"
        "t = jnp.arange(T, dtype=jnp.float32)\n"
        "freqs = jnp.outer(t, inv_freq)\n"
        "cos = np.asarray(jnp.cos(freqs))\n"
        "sin = np.asarray(jnp.sin(freqs))\n"
        "import sys\n"
        "sys.stdout.buffer.write(cos.astype(np.float32).tobytes())\n"
        "sys.stdout.buffer.write(sin.astype(np.float32).tobytes())\n"
    )
    env = dict(os.environ)
    env["JAX_PLATFORMS"] = "cpu"
    r = subprocess.run([sys.executable, "-c", code], check=True, env=env,
                       capture_output=True)
    n = T * (AHD // 2) * 4
    cos = np.frombuffer(r.stdout[:n], dtype=np.float32).reshape(T, AHD // 2)
    sin = np.frombuffer(r.stdout[n:2 * n], dtype=np.float32).reshape(T, AHD // 2)
    Ct = np.concatenate([cos.T, cos.T], axis=0).astype(np.float32)
    St = np.concatenate([-sin.T, sin.T], axis=0).astype(np.float32)
    _ROPE_CACHE["C"] = np.ascontiguousarray(Ct)
    _ROPE_CACHE["S"] = np.ascontiguousarray(St)
    return _ROPE_CACHE["C"], _ROPE_CACHE["S"]


def attn_masks():
    kk = np.arange(128)[:, None]
    qq = np.arange(128)[None, :]
    edge = (kk > qq).astype(ml_dtypes.bfloat16)
    diag = (kk <= qq).astype(ml_dtypes.bfloat16)
    return np.ascontiguousarray(edge), np.ascontiguousarray(diag)


def host_inputs(z):
    x0 = np.asarray(z["L0_attn_hidden_states"])
    TR = lambda a: np.ascontiguousarray(np.asarray(a, np.float32).T)
    colv = lambda a: np.ascontiguousarray(
        np.asarray(a, np.float32).reshape(16, 128).T)
    Ct, St = rope_tables()
    edge, diag = attn_masks()
    w0 = np.asarray(z["L0_attn_w0"], np.float32)
    w1 = np.asarray(z["L0_attn_w1"], np.float32)
    w2 = np.asarray(z["L0_attn_w2"], np.float32)
    sel = np.zeros((128, 16, 16), dtype=np.float32)
    for m in range(16):
        sel[:, m, m] = 1.0
    shared = {
        "wqkvT": TR(z["L0_attn_attn_qkv_w"]),
        "woT": TR(z["L0_attn_attn_o_w"]),
        "wmlpT": TR(z["L0_attn_mlp_qkv_w"]),
        "wmloT": TR(z["L0_attn_mlp_o_w"]),
        "wlrT": TR(z["L0_attn_lr_w"]),
        "wgT": TR(z["L0_ffn_gate_w"]),
        "wuT": TR(z["L0_ffn_up_w"]),
        "wdT": TR(z["L0_ffn_down_w"]),
        "w0T": np.ascontiguousarray(w0.transpose(0, 2, 1)),
        "w2T": np.ascontiguousarray(w2.transpose(0, 2, 1)),
        "w1o": np.ascontiguousarray(w1),
        "w1e": np.ascontiguousarray(w1.transpose(0, 2, 1)),
        "qnw": colv(z["L0_attn_q_norm_w"]),
        "knw": colv(z["L0_attn_k_norm_w"]),
        "qnw_s": np.ascontiguousarray(np.roll(colv(z["L0_attn_q_norm_w"]), 64, axis=0)),
        "knw_s": np.ascontiguousarray(np.roll(colv(z["L0_attn_k_norm_w"]), 64, axis=0)),
        "mnw": colv(z["L0_attn_mlp_norm_w"]),
        "tnw": np.ascontiguousarray(
            np.tile(np.asarray(z["L0_attn_ttt_norm_w"], np.float32).reshape(4, 128).T,
                    (1, 1))),
        "lrb": np.ascontiguousarray(np.asarray(z["L0_attn_lr_b"], np.float32).reshape(12, 1)),
        "lrb_row": np.ascontiguousarray(np.asarray(z["L0_attn_lr_b"], np.float32).reshape(1, 12)),
        "wmod": np.ascontiguousarray(np.asarray(z["L0_wmod"], np.float32)),
        "ropeC": Ct, "ropeS": St,
        "mask_edge": edge, "mask_diag": diag,
        "sel16": np.ascontiguousarray(sel.reshape(128, 256)),
    }
    in_maps = []
    for core in range(8):
        b, s = core // 2, core % 2
        m = dict(shared)
        m["x0fm"] = np.ascontiguousarray(x0[b].T.astype(np.float32))
        m["x0own"] = np.ascontiguousarray(x0[b, 2048 * s:2048 * (s + 1)].T.astype(np.float32))
        m["c_col"] = np.ascontiguousarray(
            np.asarray(z["L0_c"], np.float32)[b].reshape(16, 128).T)
        m["selmask"] = np.full((128, 1), float(s), dtype=np.float32)
        in_maps.append(m)
    return in_maps



class KB:

    def __init__(self, nc, upto="all"):
        self.nc = nc
        self.S = Sched()
        self.upto = upto
        self.st = ExitStack()
        self._declare_io()

    def dma(self, q, dst, src, ins=(), outs=(), transpose=False):
        stream = {"sd": "sync", "ad": "act", "gd": "gp"}[q]
        if transpose:
            self.S.op(stream, lambda e: e.dma_start_transpose(dst, src),
                      ins=ins, outs=outs, sem=q)
        else:
            self.S.op(stream, lambda e: e.dma_start(dst, src),
                      ins=ins, outs=outs, sem=q)

    def pe(self, emit, ins=(), outs=()):
        self.S.op("pe", emit, ins=ins, outs=outs)

    def dve(self, emit, ins=(), outs=()):
        self.S.op("dve", emit, ins=ins, outs=outs)

    def act(self, emit, ins=(), outs=()):
        self.S.op("act", emit, ins=ins, outs=outs)

    def gp(self, emit, ins=(), outs=()):
        self.S.op("gp", emit, ins=ins, outs=outs)

    def mm(self, out, lhsT, rhs, start, stop, ins, outs):
        self.pe(lambda e: e.matmul(out, lhsT, rhs, start=start, stop=stop,
                                   skip_group_check=True),
                ins=ins, outs=outs)

    def _declare_io(self):
        nc = self.nc
        D = {}

        def inp(name, shape, dt=F32):
            D[name] = nc.dram_tensor(name, list(shape), dt, kind="ExternalInput").ap()

        inp("x0fm", (H, T)); inp("x0own", (H, 2048))
        inp("wqkvT", (H, 3 * H)); inp("woT", (H, H))
        inp("wmlpT", (H, 3 * H)); inp("wmloT", (H, H)); inp("wlrT", (H, 12))
        inp("wgT", (H, FFI)); inp("wuT", (H, FFI)); inp("wdT", (FFI, H))
        inp("w0T", (NFH, FHD, FHD)); inp("w2T", (NFH, FHD, FHD))
        inp("w1o", (NFH, FHD, FHD)); inp("w1e", (NFH, FHD, FHD))
        inp("qnw", (128, 16)); inp("knw", (128, 16)); inp("mnw", (128, 16))
        inp("qnw_s", (128, 16)); inp("knw_s", (128, 16))
        inp("tnw", (128, 4)); inp("lrb", (12, 1))
        inp("wmod", (H, 6 * H)); inp("c_col", (128, 16))
        inp("lrb_row", (1, 12))
        inp("ropeC", (128, T)); inp("ropeS", (128, T))
        inp("mask_edge", (128, 128), BF16); inp("mask_diag", (128, 128), BF16)
        inp("sel16", (128, 256)); inp("selmask", (128, 1))
        D["out"] = nc.dram_tensor("out", [H, 2048], F32, kind="ExternalOutput").ap()

        import os as _os
        _dbg = "KDBG" in _os.environ

        def scratch(name, shape, dt=BF16):
            kind = "ExternalOutput" if (_dbg and name in (
                "q_s", "ao_s", "xm_s", "fq_s", "oc_s", "x1_s", "h2_s", "wgq_s")) else "Internal"
            D[name] = nc.dram_tensor(name, list(shape), dt, kind=kind).ap()

        scratch("h_s", (H, T))
        scratch("q_s", (H, T)); scratch("k_s", (H, T))
        scratch("qr_s", (H, T)); scratch("kr_s", (H, T))
        scratch("v_s", (T, H))
        scratch("o_s", (H, T)); scratch("ao_s", (H, T)); scratch("xm_s", (H, T))
        scratch("fq_s", (H, T))
        scratch("fk_s", (T, H)); scratch("fv_s", (T, H))
        scratch("oc_s", (H, T))
        scratch("x1_s", (H, 2048), F32)
        scratch("h2_s", (H, 2048))
        scratch("wgq_s", (H, FFI)); scratch("wuq_s", (H, FFI))
        scratch("wdq_s", (FFI, H)); scratch("hq_s", (FFI, 2048))
        scratch("dbg", (128, 4096), F32)
        self.D = D

    def alloc_all(self):
        nc, st = self.nc, self.st
        self.RBF_t = st.enter_context(nc.sbuf_tensor("RBF", [128, 43 * 1024], BF16))
        self.RF4_t = st.enter_context(nc.sbuf_tensor("RF4", [128, 21 * 1024], F32))
        self.RBF = Arena(self.RBF_t.ap(), 43 * 1024, "RBF")
        self.RF4 = Arena(self.RF4_t.ap(), 21 * 1024, "RF4")
        self.ITI_t = st.enter_context(nc.sbuf_tensor("ITI", [128, 2048], mybir.dt.int32))
        self.ITI = Arena(self.ITI_t.ap(), 2048, "ITI")
        P = {}
        P["consts"] = st.enter_context(nc.sbuf_tensor("consts", [128, 768], F32))
        P["onesr"] = st.enter_context(nc.sbuf_tensor("onesr", [1, 128], F32))
        P["rows"] = st.enter_context(nc.sbuf_tensor("rows", [16, T], BF16))
        P["rows2"] = st.enter_context(nc.sbuf_tensor("rows2", [16, T], BF16))
        P["lr_t"] = st.enter_context(nc.sbuf_tensor("lr_t", [128, 384], F32))
        P["mrow"] = st.enter_context(nc.sbuf_tensor("mrow", [1, 2048], F32))
        P["maskE"] = st.enter_context(nc.sbuf_tensor("maskE", [128, 128], BF16))
        P["maskD"] = st.enter_context(nc.sbuf_tensor("maskD", [128, 128], BF16))
        self.P = P
        cst = P["consts"].ap()
        self.kconst_slots = {}
        vals = [-1.0, 1.0, 0.5, 1e-5, 1e-6, 67.0, 2.0, -2.0, float(BASE_LR_INV)]
        for i, v in enumerate(vals):
            ap = cst[:, 428 + i:429 + i]
            nc.const_aps.register(F32, float(v), ap)
            self.kconst_slots[float(v)] = ap
        self.PS = [st.enter_context(nc.psum_tensor(f"ps{i}", [128, 512], F32)).ap()
                   for i in range(8)]

    def C(self, a, b):
        return self.P["consts"].ap()[:, a:b]

    def f0(self):
        import os
        SKIP = os.environ.get("KSKIP", "")
        D, S = self.D, self.S
        cst = self.P["consts"].ap()
        self.S.op("gp", lambda e: e.load_library(LIB_ATTN), inc=0)
        self.dma("sd", cst[:, 0:16], D["qnw"], outs=["c_qnw"])
        self.dma("sd", cst[:, 16:32], D["knw"], outs=["c_knw"])
        self.dma("sd", cst[:, 32:48], D["mnw"], outs=["c_mnw"])
        self.dma("sd", cst[:, 640:656], D["qnw_s"], outs=["c_qnws"])
        self.dma("sd", cst[:, 656:672], D["knw_s"], outs=["c_knws"])
        self.dma("sd", cst[:, 48:52], D["tnw"], outs=["c_tnw"])
        self.dma("sd", cst[:, 150:151], D["selmask"], outs=["c_sel"])
        self.dma("sd", cst[:, 152:408], D["sel16"], outs=["c_sel16"])
        self.dma("sd", cst[:, 408:424], D["c_col"], outs=["c_ccol"])
        self.dma("sd", self.P["maskE"].ap(), D["mask_edge"], outs=["maskE"])
        self.dma("sd", self.P["maskD"].ap(), D["mask_diag"], outs=["maskD"])
        self.dma("sd", cst[0:12, 424:425], D["lrb"], outs=["lrb"])
        if "lrb" not in SKIP:
            self.dma("sd", cst[0:1, 472:484], D["lrb_row"], outs=["lrbr"])
            self.gp(lambda e: e.partition_broadcast(cst[:, 472:484], cst[0:1, 472:484], 128),
                    ins=["lrbr"], outs=["lrbb"])
        self.dve(lambda e: e.memset(cst[:, 146:147], 2.0), outs=["two_c"])
        self.act(lambda e: e.activation(cst[:, 147:148], cst[:, 146:147], AF.Ln),
                 ins=["two_c"], outs=["ln2col"])
        self.dve(lambda e: e.tensor_scalar(
            cst[:, 145:146], cst[:, 147:148], -1.0, None, OP.mult),
            ins=["ln2col"], outs=["ln2col"])
        self.dve(lambda e: e.tensor_scalar(
            cst[:, 144:145], cst[:, 147:148], 16.0, None, OP.mult),
            ins=["ln2col"], outs=["ln2col"])
        for v, ap in self.kconst_slots.items():
            self.dve(lambda e, a=ap, vv=v: e.memset(a, vv), outs=["kconst"])
        self.dve(lambda e: e.memset(cst[:, 151:152], 1.0), outs=["ones_c"])
        self.dve(lambda e: e.memset(self.P["onesr"].ap(), 1.0), outs=["ones_r"])
        self.dve(lambda e: e.memset(cst[:, 148:149], -448.0), outs=["c448"])
        self.dve(lambda e: e.reciprocal(cst[:, 149:150], cst[:, 148:149]),
                 ins=["c448"], outs=["rcp448"])
        if self.upto == "f0a":
            self.S.barrier()
            return
        self.RBF.push()
        csil = self.RF4.tile(16)
        self.act(lambda e: e.activation(csil, cst[:, 408:424], AF.Silu),
                 ins=["c_ccol"], outs=["csil"])
        wst = self.RF4.tile(2048)
        wstb = self.RF4.tile(2048)
        modv = self.RF4.tile(96)
        self.modv = modv
        wmod_r = D["wmod"].rearrange("(kt kp) m -> kp kt m", kp=128)
        for mc in range(96):
            stg = wst if mc % 2 == 0 else wstb
            key = f"modw{mc % 2}"
            self.dma("sd", stg.rearrange("p (kt x) -> p kt x", kt=16),
                     wmod_r[:, :, 128 * mc:128 * (mc + 1)], outs=[key])
            ps = self.PS[mc % 2][:, 0:1]
            pkey = f"ps{mc % 2}"
            for k in range(16):
                self.mm(ps, stg[:, 128 * k:128 * (k + 1)], csil[:, k:k + 1],
                        start=(k == 0), stop=(k == 15),
                        ins=[key, "csil"], outs=[pkey])
            mcv = mc
            self.dve(lambda e, m=mcv, p=ps: e.tensor_copy(modv[:, m:m + 1], p),
                     ins=[pkey], outs=["modv"])
        if self.upto == "f0b":
            self.RBF.pop()
            self.S.barrier()
            return
        self.dve(lambda e: e.tensor_copy(cst[:, 52:68], modv[:, 0:16]),
                 ins=["modv"], outs=["c_sa"])
        self.act(lambda e: e.activation(cst[:, 68:84], modv[:, 16:32], AF.Identity,
                                        bias=1.0), ins=["modv", "kconst"], outs=["c_ca1"])
        self.dve(lambda e: e.tensor_copy(cst[:, 100:116], modv[:, 48:64]),
                 ins=["modv"], outs=["c_sb"])
        self.act(lambda e: e.activation(cst[:, 116:132], modv[:, 64:80], AF.Identity,
                                        bias=1.0), ins=["modv", "kconst"], outs=["c_cb1"])
        if self.upto == "f0c":
            self.RBF.pop()
            self.S.barrier()
            return
        for src0, dst0, nm in ((32, 84, "c_ga"), (80, 132, "c_gb")):
            sgn = self.RF4.tile(16)
            self.dve(lambda e, s=src0, d=sgn: e.tensor_scalar(
                d, modv[:, s:s + 16], 0.0, None, OP.is_ge),
                ins=["modv"], outs=["sgn_t"])
            self.act(lambda e, d=sgn: e.activation(d, d, AF.Identity, bias=-1.0,
                                                   scale=2.0),
                     ins=["sgn_t", "kconst"], outs=["sgn_t"])
            mag = self.RF4.tile(16)
            self.act(lambda e, s=src0, d=mag: e.activation(
                d, modv[:, s:s + 16], AF.Abs), ins=["modv"], outs=["mag_t"])
            self.dve(lambda e, d=mag: e.tensor_scalar(
                d, d, 0.25, None, OP.max), ins=["mag_t"], outs=["mag_t"])
            self.dve(lambda e, d=dst0, a=sgn, b=mag: e.tensor_mul(
                cst[:, d:d + 16], a, b), ins=["sgn_t", "mag_t"], outs=[nm])
        self.RBF.pop()
        S.barrier()
        self.RF4.reset()

    def f1_ln1(self):
        D = self.D
        cst = self.P["consts"].ap()
        self.RBF.reset(); self.RF4.reset()
        xch = [self.RF4.tile(512) for _ in range(2)]
        self.RF4.reset()
        xbuf = [self.RF4.tile(16 * 512) for _ in range(2)]
        hbuf = self.RBF.tile(16 * 512)
        sqt = self.RF4.tile(512)
        x0r = D["x0fm"].rearrange("(t p) n -> p t n", p=128)
        h_sr = D["h_s"].rearrange("(t p) n -> p t n", p=128)
        for c in range(NCH):
            xb = xbuf[c % 2]
            xk = f"x0c{c % 2}"
            self.dma("sd", xb.rearrange("p (t x) -> p t x", t=16),
                     x0r[:, :, CH * c:CH * (c + 1)], outs=[xk])
            ps_s = self.PS[4][0:1, :]
            ps_q = self.PS[5][0:1, :]
            for t in range(16):
                xt = xb[:, 512 * t:512 * (t + 1)]
                self.mm(ps_s, cst[:, 151:152], xt, start=(t == 0), stop=(t == 15),
                        ins=["ones_c", xk], outs=["ps4"])
                self.act(lambda e, xt=xt: e.activation(sqt, xt, AF.Square),
                         ins=[xk], outs=["sqt"])
                self.mm(ps_q, cst[:, 151:152], sqt, start=(t == 0), stop=(t == 15),
                        ins=["ones_c", "sqt"], outs=["ps5"])
            mr = self.P["mrow"].ap()
            m_ = mr[:, 0:512]; A_ = mr[:, 512:1024]; B_ = mr[:, 1024:1536]
            v_ = mr[:, 1536:2048]
            self.dve(lambda e: e.tensor_scalar_mul(m_, ps_s, 1.0 / 2048.0),
                     ins=["ps4"], outs=["m_"])
            self.dve(lambda e: e.tensor_scalar_mul(v_, ps_q, 1.0 / 2048.0),
                     ins=["ps5"], outs=["v_"])
            self.dve(lambda e: e.scalar_tensor_tensor(
                A_, m_, 1.0, m_, OP.bypass, OP.mult), ins=["m_"], outs=["A_"])
            self.dve(lambda e: e.tensor_sub(v_, v_, A_), ins=["v_", "A_"], outs=["v_"])
            self.act(lambda e: e.activation(A_, v_, AF.Sqrt, bias=1e-5),
                     ins=["v_", "kconst"], outs=["A_"])
            self.dve(lambda e: e.reciprocal(A_, A_), ins=["A_"], outs=["A_"])
            self.dve(lambda e: e.scalar_tensor_tensor(
                B_, m_, -1.0, A_, OP.mult, OP.mult), ins=["m_", "A_"], outs=["B_"])
            psA = self.PS[6]; psB = self.PS[7]
            self.mm(psA, self.P["onesr"].ap(), A_, True, True,
                    ins=["ones_r", "A_"], outs=["ps6"])
            self.mm(psB, self.P["onesr"].ap(), B_, True, True,
                    ins=["ones_r", "B_"], outs=["ps7"])
            for t in range(16):
                xt = xb[:, 512 * t:512 * (t + 1)]
                ht = hbuf[:, 512 * t:512 * (t + 1)]
                tmp = sqt
                self.dve(lambda e, xt=xt: e.tensor_mul(tmp, xt, psA),
                         ins=[xk, "ps6"], outs=["sqt"])
                self.dve(lambda e: e.tensor_add(tmp, tmp, psB),
                         ins=["sqt", "ps7"], outs=["sqt"])
                self.dve(lambda e, t=t, ht=ht: e.tensor_scalar(
                    ht, tmp, cst[:, 68 + t:69 + t], cst[:, 52 + t:53 + t],
                    OP.mult, OP.add), ins=["sqt", "c_ca1", "c_sa"], outs=["hbuf"])
            self.dma("ad", h_sr[:, :, CH * c:CH * (c + 1)],
                     hbuf.rearrange("p (t x) -> p t x", t=16), ins=["hbuf"],
                     outs=[f"h_s{c}"])
        self.S.barrier()
        self.RBF.reset(); self.RF4.reset()

    def f1_qk(self):
        D = self.D
        cst = self.P["consts"].ap()
        self.RBF.reset(); self.RF4.reset()
        wres = self.RBF.tile(8 * 16 * 128)
        hch = [self.RBF.tile(16 * 512) for _ in range(2)]
        evac = self.RBF.tile(8 * 512)
        sqt = self.RF4.tile(512)
        h_sr = D["h_s"].rearrange("(t p) n -> p t n", p=128)
        wq_r = D["wqkvT"].rearrange("(kt kp) m -> kp kt m", kp=128)
        for half, (w_off, out_s, nrm_rows) in enumerate(
                ((0, "q_s", self.P["rows"].ap()), (2048, "k_s", self.P["rows2"].ap()))):
            out_r = D[out_s].rearrange("(t p) n -> p t n", p=128)
            for quart in range(2):
                mo = 8 * quart
                for mc in range(8):
                    co = w_off + 128 * (mo + mc)
                    dst = wres[:, 2048 * mc:2048 * (mc + 1)]
                    self.dma("gd", dst.rearrange("p (t x) -> p t x", t=16),
                             wq_r[:, :, co:co + 128], outs=["wres"])
                for c in range(NCH):
                    hb = hch[c % 2]
                    hk = f"hch{c % 2}"
                    self.dma("sd", hb.rearrange("p (t x) -> p t x", t=16),
                             h_sr[:, :, CH * c:CH * (c + 1)], outs=[hk])
                    ps_r = self.PS[4][0:16, :]
                    for m in range(8):
                        bank = self.PS[m % 4]
                        bkey = f"ps{m % 4}"
                        for k in range(16):
                            lhsT = wres[:, 2048 * m + 128 * k:2048 * m + 128 * (k + 1)]
                            self.mm(bank, lhsT, hb[:, 512 * k:512 * (k + 1)],
                                    start=(k == 0), stop=(k == 15),
                                    ins=["wres", hk], outs=[bkey])
                        et = evac[:, 512 * m:512 * (m + 1)]
                        self.act(lambda e, d=et, s=bank: e.activation(d, s, AF.Copy),
                                 ins=[bkey], outs=["evac"])
                        self.act(lambda e, s=et: e.activation(sqt, s, AF.Square),
                                 ins=["evac"], outs=["sqt"])
                        gm = mo + m
                        self.mm(ps_r, cst[:, 152 + 16 * gm:152 + 16 * (gm + 1)], sqt,
                                start=(m == 0), stop=(m == 7),
                                ins=["c_sel16", "sqt"], outs=["ps4"])
                    self.dma("ad", out_r[:, mo:mo + 8, CH * c:CH * (c + 1)],
                             evac.rearrange("p (t x) -> p t x", t=8), ins=["evac"],
                             outs=[f"{out_s}{c}_{quart}"])
                    nr = nrm_rows[:, CH * c:CH * (c + 1)]
                    if quart == 0:
                        self.dve(lambda e, nr=nr: e.tensor_copy(nr, ps_r),
                                 ins=["ps4"], outs=[f"nrm{half}"])
                    else:
                        self.dve(lambda e, nr=nr: e.tensor_add(nr, nr, ps_r),
                                 ins=["ps4", f"nrm{half}"], outs=[f"nrm{half}"])
        self.S.barrier()
        self.RBF.reset(); self.RF4.reset()

    def f1_v(self):
        D = self.D
        self.RBF.reset(); self.RF4.reset()
        wres = self.RBF.tile(8 * 16 * 128)
        hch = [self.RBF.tile(16 * 512) for _ in range(2)]
        evac = self.RBF.tile(2 * 512)
        h_sr = D["h_s"].rearrange("(t p) n -> p t n", p=128)
        wv_r = D["wqkvT"].rearrange("(kt kp) m -> kp kt m", kp=128)
        v_r = D["v_s"].rearrange("(n p) t -> p n t", p=128)
        for vh in range(2):
            for mc in range(8):
                co = 4096 + 1024 * vh + 128 * mc
                dst = wres[:, 2048 * mc:2048 * (mc + 1)]
                self.dma("gd", dst.rearrange("p (t x) -> p t x", t=16),
                         wv_r[:, :, co:co + 128], outs=["wres"])
            wres_r = wres.rearrange("p (m x) -> p m x", m=8)
            for c in range(NCH):
                hb = hch[c % 2]
                hk = f"hch{c % 2}"
                self.dma("sd", hb.rearrange("p (t x) -> p t x", t=16),
                         h_sr[:, :, CH * c:CH * (c + 1)], outs=[hk])
                for mt in range(4):
                    for nc2 in range(2):
                        bank = self.PS[nc2]
                        bkey = f"ps{nc2}"
                        for k in range(16):
                            lhsT = hb[:, 512 * k + 128 * mt:512 * k + 128 * (mt + 1)]
                            rhs = wres_r[:, 4 * nc2:4 * nc2 + 4,
                                         128 * k:128 * k + 128]
                            self.mm(bank, lhsT, rhs, start=(k == 0), stop=(k == 15),
                                    ins=["wres", hk], outs=[bkey])
                        et = evac[:, 512 * nc2:512 * (nc2 + 1)]
                        self.act(lambda e, d=et, s=bank: e.activation(d, s, AF.Copy),
                                 ins=[bkey], outs=["evac"])
                    self.dma("ad", v_r[:, 4 * c + mt, 1024 * vh:1024 * (vh + 1)],
                             evac.rearrange("p (n x) -> p n x", n=2), ins=["evac"],
                             outs=[f"v_s{c}_{mt}_{vh}"])
        self.S.barrier()
        self.RBF.reset(); self.RF4.reset()

    def f2_rope(self):
        D = self.D
        cst = self.P["consts"].ap()
        for half, (in_s, out_s, rows_t, w_a, w_b, bnd_dst) in enumerate((
                ("q_s", "qr_s", self.P["rows"].ap(), 0, 640, 0),
                ("k_s", "kr_s", self.P["rows2"].ap(), 16, 656, 8))):
            wkeys = ["c_qnw", "c_qnws"] if half == 0 else ["c_knw", "c_knws"]
            self.RBF.reset(); self.RF4.reset()
            tot = self.RF4.tile(T)[0:16, :]
            self.dve(lambda e, r=rows_t, t=tot: e.tensor_copy(t, r),
                    ins=[f"nrm{half}"], outs=["tot"])
            self.gp(lambda e, t=tot: e.partition_all_reduce(
                t, t, 16, bass.bass_isa.ReduceOp.add), ins=["tot"], outs=["tot"])
            inv = self.RF4.tile(T)[0:1, :]
            self.act(lambda e, t=tot, i=inv: e.activation(
                i, t[0:1, :], AF.Sqrt, bias=1e-6, scale=1.0 / 2048.0),
                ins=["tot", "kconst"], outs=["invr"])
            self.dve(lambda e, i=inv: e.reciprocal(i, i), ins=["invr"], outs=["invr"])
            inv2 = self.RF4.tile(T)[0:16, :]
            self.dve(lambda e, i=inv, o=inv2: e.scalar_tensor_tensor(
                o[0:1, :], i, 1.0, i, OP.bypass, OP.mult), ins=["invr"], outs=["inv2"])
            self.gp(lambda e, o=inv2: e.partition_broadcast(o, o[0:1, :], 16),
                    ins=["inv2"], outs=["inv2"])
            u = self.RF4.tile(T)[0:16, :]
            self.dve(lambda e, r=rows_t, i=inv2, u=u: e.tensor_mul(u, r, i),
                     ins=[f"nrm{half}", "inv2"], outs=["u_"])
            bnd = cst[0:16, 672 + bnd_dst:672 + bnd_dst + 8]
            self.dve(lambda e, u=u, b=bnd: e.tensor_reduce(
                b, u.rearrange("p (c x) -> p c x", c=8), AX, OP.max),
                ins=["u_"], outs=[f"bnd{half}"])
            combC = self.RBF.tile(T)
            combS = self.RBF.tile(T)
            rtile = self.RF4.tile(512)
            for c in range(NCH):
                ps = self.PS[6]
                self.mm(ps, self.P["onesr"].ap(), inv[:, CH * c:CH * (c + 1)],
                        True, True, ins=["ones_r", "invr"], outs=["ps6"])
                for tbl, comb, keyo in (("ropeC", combC, "combC"), ("ropeS", combS, "combS")):
                    self.dma("sd", rtile, D[tbl][:, CH * c:CH * (c + 1)], outs=["rtile"])
                    self.dve(lambda e, co=comb, c=c, rt=rtile, ps=ps: e.tensor_mul(
                        co[:, CH * c:CH * (c + 1)], rt, ps),
                        ins=["rtile", "ps6"], outs=[keyo])
            in_r = D[in_s].rearrange("(t p) n -> p t n", p=128)
            out_r = D[out_s].rearrange("(t p) n -> p t n", p=128)
            qa = self.RBF.tile(T)
            qb = self.RBF.tile(T)
            t1 = self.RBF.tile(T)
            t2 = self.RBF.tile(T)
            for t in range(16):
                self.dma("sd", qa, in_r[:, t, :], outs=["qa"])
                self.dma("sd", qb[0:64, :], D[in_s].rearrange(
                    "(t p) n -> t p n", t=16)[t, 64:128, :], outs=["qb"])
                self.dma("sd", qb[64:128, :], D[in_s].rearrange(
                    "(t p) n -> t p n", t=16)[t, 0:64, :], outs=["qb"])
                wa = cst[:, w_a + t:w_a + t + 1]
                wb = cst[:, w_b + t:w_b + t + 1]
                self.dve(lambda e, wa=wa: e.scalar_tensor_tensor(
                    t1, qa, wa, combC, OP.mult, OP.mult),
                    ins=["qa", "combC", wkeys[0]], outs=["t1", "t1f"])
                self.dve(lambda e, wb=wb: e.scalar_tensor_tensor(
                    t2, qb, wb, combS, OP.mult, OP.mult),
                    ins=["qb", "combS", wkeys[1]], outs=["t2"])
                self.dve(lambda e: e.tensor_add(t1, t1, t2),
                         ins=["t1", "t2"], outs=["t1f"])
                self.dma("ad", out_r[:, t, :], t1, ins=["t1f"], outs=[f"{out_s}{t}"])
        self.S.barrier()
        self.RBF.reset(); self.RF4.reset()

    def f2_attn(self):
        D = self.D
        cst = self.P["consts"].ap()
        for hg in range(2):
            self.RBF.reset(); self.RF4.reset()
            kwin = [self.RBF.tile(8 * 512) for _ in range(3)]
            vwin = [self.RBF.tile(8 * 512) for _ in range(3)]
            qch = self.RBF.tile(8 * 512)
            invb = self.RBF.tile(128)
            pt = self.RBF.tile(9 * 128)
            och = self.RBF.tile(8 * 512)
            gps = self.RBF.tile(9 * 128)
            srow = self.RF4.tile(128)[0:1, :]
            sinv = self.RF4.tile(128)[0:1, :]
            mbias = self.RF4.tile(128)
            kr_r = D["kr_s"].rearrange("(t p) n -> p t n", p=128)
            v_r = D["v_s"].rearrange("(n p) t -> p n t", p=128)
            qr_r = D["qr_s"].rearrange("(t p) n -> p t n", p=128)
            bq = cst[0:16, 672:680]
            bk = cst[0:16, 680:688]
            wq2 = self.RF4.tile(8)[0:16, 0:1]
            wk2 = self.RF4.tile(8)[0:16, 0:1]
            if hg == 0:
                for wsl, dst, key in ((0, wq2, "wq2"), (16, wk2, "wk2")):
                    tmp = self.RF4.tile(16)
                    self.gp(lambda e, a=wsl, t=tmp: e.partition_all_reduce(
                        t, cst[:, a:a + 16], 128, bass.bass_isa.ReduceOp.absmax),
                        ins=["c_qnw", "c_knw"], outs=["tmpw"])
                    ps = self.PS[6][0:16, 0:1]
                    self.mm(ps, tmp[0:1, :], self.P["onesr"].ap()[0:1, 0:1],
                            True, True, ins=["tmpw", "ones_r"], outs=["ps6"])
                    self.act(lambda e, d=dst, s=ps: e.activation(d, s, AF.Square),
                             ins=["ps6"], outs=[key])
                bk3 = self.RF4.tile(8)[0:16, :]
                self.dve(lambda e, b=bk3: e.tensor_copy(b, bk), ins=["bnd1"], outs=["bk3"])
                for sh in (1, 2):
                    self.dve(lambda e, b=bk3, sh=sh: e.tensor_tensor(
                        b[:, sh:8], b[:, sh:8], bk[:, 0:8 - sh], OP.max),
                        ins=["bk3", "bnd1"], outs=["bk3"])
                prod = self.RF4.tile(8)[0:16, :]
                self.dve(lambda e, p=prod, b=bk3: e.tensor_mul(p, bq, b),
                         ins=["bnd0", "bk3"], outs=["prod"])
                self.dve(lambda e, p=prod: e.tensor_scalar(
                    p, p, wq2, None, OP.mult), ins=["prod", "wq2"], outs=["prod"])
                self.dve(lambda e, p=prod: e.tensor_scalar(
                    p, p, wk2, None, OP.mult), ins=["prod", "wk2"], outs=["prod"])
                self.act(lambda e, p=prod: e.activation(
                    p, p, AF.Sqrt, scale=ATT_SCALE * ATT_SCALE),
                    ins=["prod"], outs=["prod"])
                self.dve(lambda e, p=prod: e.tensor_scalar(
                    p, p, 100.0, None, OP.min), ins=["prod"], outs=["prod"])
                self.dve(lambda e, p=prod: e.tensor_scalar(
                    p, p, -1.0, 44.0, OP.mult, OP.add), ins=["prod"], outs=["prod"])
                mb = cst[0:16, 688:696]
                self.dve(lambda e, p=prod, m=mb: e.tensor_copy(m, p),
                         ins=["prod"], outs=["mb"])
                self.dma("sd", self.P["mrow"].ap()[0:1, 1024:1152],
                         mb, ins=["mb"], outs=["mbrow"])
            mbr = self.P["mrow"].ap()
            for h8 in range(8):
                h = 8 * hg + h8
                for c in range(NCH):
                    self.gp(lambda e, h=h, c=c, h8=h8: e.partition_broadcast(
                        mbias[:, 8 * h8 + c:8 * h8 + c + 1],
                        mbr[0:1, 1024 + 8 * h + c:1025 + 8 * h + c], 128),
                        ins=["mbrow"], outs=["mbias"])
            for c in range(NCH):
                sl = c % 3
                self.dma("sd", kwin[sl].rearrange("p (t x) -> p t x", t=8),
                         kr_r[:, 8 * hg:8 * hg + 8, CH * c:CH * (c + 1)],
                         outs=[f"kw{sl}"])
                self.dma("sd", vwin[sl].rearrange("p (n x) -> p n x", n=4),
                         v_r[:, 4 * c:4 * c + 4, 1024 * hg:1024 * (hg + 1)],
                         outs=[f"vw{sl}"])
                self.dma("sd", qch.rearrange("p (t x) -> p t x", t=8),
                         qr_r[:, 8 * hg:8 * hg + 8, CH * c:CH * (c + 1)],
                         outs=["qch"])
                for h8 in range(8):
                    h = 8 * hg + h8
                    for qt in range(4):
                        q0 = CH * c + 128 * qt
                        j0 = max(0, (WIN - q0) // 128)
                        njt = 9 - j0
                        qap = qch[:, 512 * h8 + 128 * qt:512 * h8 + 128 * (qt + 1)]
                        for jj in range(j0, 9):
                            kpos = q0 - WIN + 128 * jj
                            ksl = (kpos // CH) % 3
                            koff = kpos % CH
                            kap = kwin[ksl][:, 512 * h8 + koff:512 * h8 + koff + 128]
                            bi = (jj - j0) // 4
                            bo = ((jj - j0) % 4) * 128
                            bank = self.PS[bi]
                            self.mm(bank[:, bo:bo + 128], kap, qap,
                                    start=(bo == 0), stop=True,
                                    ins=[f"kw{ksl}", "qch"], outs=[f"ps{bi}"])
                        nb = cdiv(njt, 4)
                        for bi in range(nb):
                            w = min(4, njt - 4 * bi) * 128
                            bank = self.PS[bi]
                            self.act(lambda e, bi=bi, w=w, h8=h8, c=c, bank=bank:
                                     e.activation(
                                         pt[:, 512 * bi:512 * bi + w],
                                         bank[:, 0:w], AF.Exp,
                                         bias=mbias[:, 8 * h8 + c:8 * h8 + c + 1],
                                         scale=ATT_SCALE),
                                     ins=[f"ps{bi}", "mbias"], outs=["pt"])
                        if q0 >= WIN:
                            self.dve(lambda e: e.tensor_mul(
                                pt[:, 0:128], pt[:, 0:128], self.P["maskE"].ap()),
                                ins=["pt", "maskE"], outs=["pt"])
                        dof = 128 * (njt - 1)
                        self.dve(lambda e, dof=dof: e.tensor_mul(
                            pt[:, dof:dof + 128], pt[:, dof:dof + 128],
                            self.P["maskD"].ap()), ins=["pt", "maskD"], outs=["pt"])
                        self.gp(lambda e, n=njt: e.partition_all_reduce(
                            gps[:, 0:128 * n], pt[:, 0:128 * n], 128,
                            bass.bass_isa.ReduceOp.add), ins=["pt"], outs=["gps"])
                        self.dve(lambda e, n=njt: e.tensor_reduce(
                            srow, gps[0:1, 0:128 * n].rearrange(
                                "p (j x) -> p x j", j=n), AX, OP.add),
                            ins=["gps"], outs=["srow"])
                        self.dve(lambda e: e.reciprocal(sinv, srow),
                                 ins=["srow"], outs=["sinv"])
                        ob = self.PS[4][:, 0:128]
                        for jj in range(j0, 9):
                            kpos = q0 - WIN + 128 * jj
                            ksl = (kpos // CH) % 3
                            ktt = (kpos % CH) // 128
                            vap = vwin[ksl][:, 1024 * ktt + 128 * h8:
                                            1024 * ktt + 128 * (h8 + 1)]
                            self.mm(ob, vap, pt[:, 128 * (jj - j0):128 * (jj - j0 + 1)],
                                    start=(jj == j0), stop=(jj == 8),
                                    ins=[f"vw{ksl}", "pt"], outs=["ps4"])
                        bb = self.PS[5][:, 0:128]
                        self.mm(bb, self.P["onesr"].ap(), sinv, True, True,
                                ins=["ones_r", "sinv"], outs=["ps5"])
                        self.act(lambda e: e.activation(invb, bb, AF.Copy),
                                 ins=["ps5"], outs=["invb128"])
                        oap = och[:, 512 * h8 + 128 * qt:512 * h8 + 128 * (qt + 1)]
                        self.dve(lambda e, o=oap: e.tensor_mul(o, ob, invb),
                                 ins=["ps4", "invb128"], outs=["och"])
                self.dma("ad", D["o_s"].rearrange("(t p) n -> p t n", p=128)[
                    :, 8 * hg:8 * hg + 8, CH * c:CH * (c + 1)],
                    och.rearrange("p (t x) -> p t x", t=8),
                    ins=["och"], outs=[f"o_s{hg}_{c}"])
            self.S.barrier()
        self.S.barrier()
        self.RBF.reset(); self.RF4.reset()

    def f2_oproj(self):
        D = self.D
        cst = self.P["consts"].ap()
        self.RBF.reset(); self.RF4.reset()
        och = [self.RBF.tile(16 * 512) for _ in range(2)]
        wocol = [self.RBF.tile(2048) for _ in range(2)]
        aoch = self.RBF.tile(16 * 512)
        xmch = self.RBF.tile(16 * 512)
        aof = self.RF4.tile(16 * 512)
        sqt = self.RF4.tile(512)
        o_r = D["o_s"].rearrange("(t p) n -> p t n", p=128)
        wo_r = D["woT"].rearrange("(kt kp) m -> kp kt m", kp=128)
        ao_r = D["ao_s"].rearrange("(t p) n -> p t n", p=128)
        xm_r = D["xm_s"].rearrange("(t p) n -> p t n", p=128)
        mr = self.P["mrow"].ap()
        for c in range(NCH):
            ob = och[c % 2]
            okey = f"och{c % 2}"
            self.dma("sd", ob.rearrange("p (t x) -> p t x", t=16),
                     o_r[:, :, CH * c:CH * (c + 1)], outs=[okey])
            ps_r = self.PS[5][0:1, :]
            for m in range(16):
                wc = wocol[m % 2]
                wkey = f"woc{m % 2}"
                self.dma("gd", wc.rearrange("p (t x) -> p t x", t=16),
                         wo_r[:, :, 128 * m:128 * (m + 1)], outs=[wkey])
                bank = self.PS[m % 4]
                bkey = f"ps{m % 4}"
                for k in range(16):
                    self.mm(bank, wc[:, 128 * k:128 * (k + 1)],
                            ob[:, 512 * k:512 * (k + 1)],
                            start=(k == 0), stop=(k == 15),
                            ins=[wkey, okey], outs=[bkey])
                at = aof[:, 512 * m:512 * (m + 1)]
                self.dve(lambda e, d=at, s=bank: e.tensor_copy(d, s),
                         ins=[bkey], outs=["aof"])
                self.act(lambda e, s=at: e.activation(sqt, s, AF.Square),
                         ins=["aof"], outs=["sqt"])
                self.mm(ps_r, cst[:, 151:152], sqt, start=(m == 0), stop=(m == 15),
                        ins=["ones_c", "sqt"], outs=["ps5"])
            iv = mr[:, 0:512]
            self.act(lambda e: e.activation(iv, ps_r, AF.Sqrt, bias=1e-6,
                                            scale=1.0 / 2048.0),
                     ins=["ps5", "kconst"], outs=["iv"])
            self.dve(lambda e: e.reciprocal(iv, iv), ins=["iv"], outs=["iv"])
            bb = self.PS[6]
            self.mm(bb, self.P["onesr"].ap(), iv, True, True,
                    ins=["ones_r", "iv"], outs=["ps6"])
            for m in range(16):
                at = aof[:, 512 * m:512 * (m + 1)]
                ac = aoch[:, 512 * m:512 * (m + 1)]
                xc = xmch[:, 512 * m:512 * (m + 1)]
                self.dve(lambda e, d=ac, s=at: e.tensor_copy(d, s),
                         ins=["aof"], outs=["aoch"])
                self.dve(lambda e, x=xc, s=at, m=m: e.scalar_tensor_tensor(
                    x, s, cst[:, 32 + m:33 + m], bb, OP.mult, OP.mult),
                    ins=["aof", "c_mnw", "ps6"], outs=["xmch"])
            self.dma("ad", ao_r[:, :, CH * c:CH * (c + 1)],
                     aoch.rearrange("p (t x) -> p t x", t=16),
                     ins=["aoch"], outs=[f"ao{c}"])
            self.dma("ad", xm_r[:, :, CH * c:CH * (c + 1)],
                     xmch.rearrange("p (t x) -> p t x", t=16),
                     ins=["xmch"], outs=[f"xm{c}"])
        self.S.barrier()
        self.RBF.reset(); self.RF4.reset()

    def f3_mlpqkv(self):
        D = self.D
        cst = self.P["consts"].ap()
        self.RBF.reset(); self.RF4.reset()
        wres = self.RBF.tile(8 * 16 * 128)
        xch = [self.RBF.tile(16 * 512) for _ in range(2)]
        evac = self.RBF.tile(8 * 512)
        sqt = self.RF4.tile(512)
        nrm = self.RF4.tile(512)
        nrm2 = self.RF4.tile(512)
        nrk = self.RF4.tile(8)
        xm_r = D["xm_s"].rearrange("(t p) n -> p t n", p=128)
        w_r = D["wmlpT"].rearrange("(kt kp) m -> kp kt m", kp=128)
        fq_r = D["fq_s"].rearrange("(t p) n -> p t n", p=128)
        for quart in range(2):
            mo = 8 * quart
            for mc in range(8):
                co = 128 * (mo + mc)
                dst = wres[:, 2048 * mc:2048 * (mc + 1)]
                self.dma("gd", dst.rearrange("p (t x) -> p t x", t=16),
                         w_r[:, :, co:co + 128], outs=["wres"])
            for c in range(NCH):
                xb = xch[c % 2]
                xk = f"xch{c % 2}"
                self.dma("sd", xb.rearrange("p (t x) -> p t x", t=16),
                         xm_r[:, :, CH * c:CH * (c + 1)], outs=[xk])
                for hl in range(2):
                    ps_r = self.PS[4][0:16, :]
                    for mt in range(4):
                        m = 4 * hl + mt
                        bank = self.PS[mt % 4]
                        bkey = f"ps{mt % 4}"
                        for k in range(16):
                            lhsT = wres[:, 2048 * m + 128 * k:2048 * m + 128 * (k + 1)]
                            self.mm(bank, lhsT, xb[:, 512 * k:512 * (k + 1)],
                                    start=(k == 0), stop=(k == 15),
                                    ins=["wres", xk], outs=[bkey])
                        et = evac[:, 512 * m:512 * (m + 1)]
                        self.act(lambda e, d=et, s=bank: e.activation(d, s, AF.Silu),
                                 ins=[bkey], outs=["evac"])
                        self.act(lambda e, s=et: e.activation(sqt, s, AF.Square),
                                 ins=["evac"], outs=["sqt"])
                        self.mm(ps_r, cst[:, 152 + 16 * mt:152 + 16 * (mt + 1)], sqt,
                                start=(mt == 0), stop=(mt == 3),
                                ins=["c_sel16", "sqt"], outs=["ps4"])
                    nr = nrm[0:4, :]
                    self.dve(lambda e, nr=nr: e.tensor_copy(nr, self.PS[4][0:4, :]),
                             ins=["ps4"], outs=["nr"])
                    self.gp(lambda e, nr=nr: e.partition_all_reduce(
                        nr, nr, 4, bass_isa.ReduceOp.add), ins=["nr"], outs=["nr"])
                    iv = nrm2[0:1, :]
                    self.act(lambda e, nr=nr, iv=iv: e.activation(
                        iv, nr[0:1, :], AF.Sqrt), ins=["nr"], outs=["iv2"])
                    self.dve(lambda e, iv=iv: e.tensor_scalar_max(iv, iv, 1e-6),
                             ins=["iv2"], outs=["iv2"])
                    self.dve(lambda e, iv=iv: e.reciprocal(iv, iv),
                             ins=["iv2"], outs=["iv2"])
                    bb = self.PS[5]
                    self.mm(bb, self.P["onesr"].ap(), iv, True, True,
                            ins=["ones_r", "iv2"], outs=["ps5"])
                    for mt in range(4):
                        m = 4 * hl + mt
                        et = evac[:, 512 * m:512 * (m + 1)]
                        self.dve(lambda e, et=et: e.tensor_mul(et, et, bb),
                                 ins=["evac", "ps5"], outs=["evac"])
                self.dma("ad", fq_r[:, mo:mo + 8, CH * c:CH * (c + 1)],
                         evac.rearrange("p (t x) -> p t x", t=8), ins=["evac"],
                         outs=[f"fq{c}_{quart}"])
        wres2 = wres
        fk_r = D["fk_s"].rearrange("(n p) t -> p n t", p=128)
        fv_r = D["fv_s"].rearrange("(n p) t -> p n t", p=128)
        evac2 = evac
        for out_s, woff, outr in (("fk_s", 2048, fk_r), ("fv_s", 4096, fv_r)):
            for vh in range(2):
                for mc in range(8):
                    co = woff + 1024 * vh + 128 * mc
                    dst = wres2[:, 2048 * mc:2048 * (mc + 1)]
                    self.dma("gd", dst.rearrange("p (t x) -> p t x", t=16),
                             w_r[:, :, co:co + 128], outs=["wres"])
                wrr = wres2.rearrange("p (m x) -> p m x", m=8)
                for c in range(NCH):
                    xb = xch[c % 2]
                    xk = f"xch{c % 2}"
                    self.dma("sd", xb.rearrange("p (t x) -> p t x", t=16),
                             xm_r[:, :, CH * c:CH * (c + 1)], outs=[xk])
                    for mt in range(4):
                        for nc2 in range(2):
                            bank = self.PS[nc2]
                            bkey = f"ps{nc2}"
                            for k in range(16):
                                lhsT = xb[:, 512 * k + 128 * mt:512 * k + 128 * (mt + 1)]
                                rhs = wrr[:, 4 * nc2:4 * nc2 + 4, 128 * k:128 * k + 128]
                                self.mm(bank, lhsT, rhs, start=(k == 0), stop=(k == 15),
                                        ins=["wres", xk], outs=[bkey])
                            et = evac2[:, 512 * nc2:512 * (nc2 + 1)]
                            self.act(lambda e, d=et, s=bank: e.activation(d, s, AF.Silu),
                                     ins=[bkey], outs=["evac"])
                        if out_s == "fk_s":
                            sq2 = nrm2
                            self.act(lambda e: e.activation(nrm, evac2[:, 0:512],
                                                            AF.Square),
                                     ins=["evac"], outs=["sqa"])
                            self.act(lambda e: e.activation(nrm2, evac2[:, 512:1024],
                                                            AF.Square),
                                     ins=["evac"], outs=["sqb"])
                            nr = nrk[:, 0:2]
                            self.dve(lambda e, nr=nr: e.tensor_reduce(
                                nr[:, 0:1], nrm, AX, OP.add),
                                ins=["sqa"], outs=["nr2"])
                            self.dve(lambda e, nr=nr: e.tensor_reduce(
                                nr[:, 1:2], nrm2, AX, OP.add),
                                ins=["sqb"], outs=["nr2"])
                            self.act(lambda e, nr=nr: e.activation(nr, nr, AF.Sqrt),
                                     ins=["nr2"], outs=["nr2"])
                            self.dve(lambda e, nr=nr: e.tensor_scalar_max(
                                nr, nr, 1e-6), ins=["nr2"], outs=["nr2"])
                            self.dve(lambda e, nr=nr: e.reciprocal(nr, nr),
                                     ins=["nr2"], outs=["nr2"])
                            for bq2 in range(2):
                                self.dve(lambda e, b=bq2, nr=nr: e.tensor_scalar(
                                    evac2[:, 512 * b:512 * (b + 1)],
                                    evac2[:, 512 * b:512 * (b + 1)],
                                    nr[:, b:b + 1], None, OP.mult),
                                    ins=["evac", "nr2"], outs=["evac"])
                        self.dma("ad", outr[:, 4 * c + mt, 1024 * vh:1024 * (vh + 1)],
                                 evac2[:, 0:1024].rearrange("p (n x) -> p n x", n=2),
                                 ins=["evac"], outs=[f"{out_s}{c}_{mt}_{vh}"])
        lrt = self.P["lr_t"].ap()
        wlr = self.RBF.tile(16 * 12)
        self.dma("gd", wlr.rearrange("p (t x) -> p t x", t=16),
                 D["wlrT"].rearrange("(kt kp) m -> kp kt m", kp=128),
                 outs=["wlr"])
        for c in range(NCH):
            xb = xch[c % 2]
            xk = f"xch{c % 2}"
            self.dma("sd", xb.rearrange("p (t x) -> p t x", t=16),
                     xm_r[:, :, CH * c:CH * (c + 1)], outs=[xk])
            for mt in range(4):
                bank = self.PS[6][:, 0:12]
                for k in range(16):
                    lhsT = xb[:, 512 * k + 128 * mt:512 * k + 128 * (mt + 1)]
                    self.mm(bank, lhsT, wlr[:, 12 * k:12 * (k + 1)],
                            start=(k == 0), stop=(k == 15),
                            ins=[xk, "wlr"], outs=["ps6"])
                ti = 12 * (4 * c + mt)
                lap = lrt[:, ti:ti + 12]
                self.dve(lambda e, lap=lap, bank=bank: e.tensor_scalar(
                    lap, bank, float(BASE_LR_INV), None, OP.add),
                    ins=["ps6", "kconst"], outs=["lrt"])
                self.dve(lambda e, lap=lap: e.tensor_add(
                    lap, lap, cst[:, 472:484]), ins=["lrt", "lrbb"], outs=["lrt"])
                self.act(lambda e, lap=lap: e.activation(lap, lap, AF.Exp),
                         ins=["lrt"], outs=["lrt"])
                self.act(lambda e, lap=lap: e.activation(lap, lap, AF.Ln, bias=1.0),
                         ins=["lrt", "kconst"], outs=["lrt"])
        self.S.barrier()
        self.RBF.reset(); self.RF4.reset()

    def f4_ttt(self):
        D = self.D
        self.RBF.reset(); self.RF4.reset()
        M0 = self.RF4.tile(2048); M2 = self.RF4.tile(2048)
        M1o = self.RF4.tile(2048); M1e = self.RF4.tile(2048)
        B0 = self.RBF.tile(2048); B2 = self.RBF.tile(2048)
        B1o = self.RBF.tile(2048); B1e = self.RBF.tile(2048)
        qcf = self.RBF.tile(2048)
        kcf = self.RBF.tile(2048); kct = self.RBF.tile(2048)
        vcf = self.RBF.tile(2048); vct = self.RBF.tile(2048)
        gk = self.RBF.tile(2048); hbm = self.RBF.tile(2048)
        dhid = self.RBF.tile(2048); sg = self.RBF.tile(2048)
        u_ = self.RBF.tile(2048); hid = self.RBF.tile(2048)
        dgp = self.RBF.tile(2048); dhb = self.RBF.tile(2048)
        kl0 = self.RBF.tile(2048); kl2 = self.RBF.tile(2048)
        vl1 = self.RBF.tile(2048)
        gq = gk; t2q = hbm
        hq = dhid
        ocb = self.RBF.tile(2048)
        sqt = self.RF4.tile(512)
        nrm = self.RF4.tile(512)
        lrt = self.P["lr_t"].ap()
        fq_r = D["fq_s"].rearrange("(t p) n -> p t n", p=128)
        fk_r = D["fk_s"].rearrange("(n p) t -> p n t", p=128)
        fv_r = D["fv_s"].rearrange("(n p) t -> p n t", p=128)
        oc_r = D["oc_s"].rearrange("(t p) n -> p t n", p=128)
        cst = self.P["consts"].ap()

        def load_state(h):
            for nm, dst, key in (("w0T", M0, "M0"), ("w2T", M2, "M2"),
                                 ("w1o", M1o, "M1o"), ("w1e", M1e, "M1e")):
                self.dma("sd", dst.rearrange("p (t x) -> p t x", t=4),
                         D[nm].rearrange("h (t p) x -> h p t x", p=128)[h],
                         outs=[key])

        def prod16(outbank_fn, lhsT_tile, rhs_tile, lkey, rkey, okey, fold=None):
            for mt in range(4):
                bank, bkey = outbank_fn(mt)
                for kt in range(4):
                    lhsT = lhsT_tile[:, 512 * kt + 128 * mt:512 * kt + 128 * (mt + 1)]
                    rhs = rhs_tile[:, 512 * kt:512 * (kt + 1)]
                    self.mm(bank, lhsT, rhs, start=(kt == 0), stop=(kt == 3),
                            ins=[lkey, rkey], outs=[bkey])
                if fold is not None:
                    fold(mt, bank, bkey)

        for h in range(NFH):
            load_state(h)
            for nm, mst, bw, km, kb in (("M0", M0, B0, "M0", "B0"),
                                        ("M2", M2, B2, "M2", "B2"),
                                        ("M1o", M1o, B1o, "M1o", "B1o"),
                                        ("M1e", M1e, B1e, "M1e", "B1e")):
                self.dve(lambda e, d=bw, s=mst: e.tensor_copy(d, s),
                         ins=[km], outs=[kb])
            for c in range(NCH):
                self.dma("sd", qcf.rearrange("p (t x) -> p t x", t=4),
                         fq_r[:, 4 * h:4 * h + 4, CH * c:CH * (c + 1)], outs=["qcf"])
                self.dma("sd", kct.rearrange("p (n x) -> p n x", n=4),
                         fk_r[:, 4 * c:4 * c + 4, FHD * h:FHD * (h + 1)], outs=["kct"])
                self.dma("sd", vct.rearrange("p (n x) -> p n x", n=4),
                         fv_r[:, 4 * c:4 * c + 4, FHD * h:FHD * (h + 1)], outs=["vct"])
                import os as _os
                if "tdma" in _os.environ.get("KSKIP", ""):
                    self.dma("sd", kcf.rearrange("p (t x) -> p t x", t=4),
                             fq_r[:, 4 * h:4 * h + 4, CH * c:CH * (c + 1)], outs=["kcf"])
                    self.dma("sd", vcf.rearrange("p (t x) -> p t x", t=4),
                             fq_r[:, 4 * h:4 * h + 4, CH * c:CH * (c + 1)], outs=["vcf"])
                else:
                 for dt4 in range(4):
                    self.dma("sd", kcf[:, 512 * dt4:512 * (dt4 + 1)],
                             D["fk_s"][CH * c:CH * (c + 1),
                                       FHD * h + 128 * dt4:FHD * h + 128 * (dt4 + 1)],
                             outs=["kcf"], transpose=True)
                    self.dma("sd", vcf[:, 512 * dt4:512 * (dt4 + 1)],
                             D["fv_s"][CH * c:CH * (c + 1),
                                       FHD * h + 128 * dt4:FHD * h + 128 * (dt4 + 1)],
                             outs=["vcf"], transpose=True)

                import os as _os
                _SK = _os.environ.get("KSKIP", "")
                do_upd = (c < NCH - 1) and ("upd" not in _SK)
                if "qp" in _SK and "keyp" in _SK:
                    continue
                def foldq(dst, dkey):
                    def f(mt, bank, bkey):
                        et = dst[:, 512 * mt:512 * (mt + 1)]
                        self.dve(lambda e, d=et, s=bank: e.tensor_copy(d, s),
                                 ins=[bkey], outs=[dkey])
                    return f
                prod16(lambda mt: (self.PS[mt], f"ps{mt}"), B0, qcf,
                       "B0", "qcf", None, foldq(gq, "gk"))
                if "t2" not in _SK:
                    prod16(lambda mt: (self.PS[4 + (mt % 2)], f"ps{4 + (mt % 2)}"),
                           B2, qcf, "B2", "qcf", None, foldq(t2q, "hbm"))
                for mt in range(4) if "t2" not in _SK else []:
                    a = gq[:, 512 * mt:512 * (mt + 1)]
                    b = t2q[:, 512 * mt:512 * (mt + 1)]
                    d = hq[:, 512 * mt:512 * (mt + 1)]
                    self.act(lambda e, a=a, d=d: e.activation(d, a, AF.Silu),
                             ins=["gk"], outs=["dhid"])
                    self.dve(lambda e, d=d, b=b: e.tensor_mul(d, d, b),
                             ins=["dhid", "hbm"], outs=["dhid"])
                if "oc" in _SK:
                    continue
                def foldoc(mt, bank, bkey):
                    et = ocb[:, 512 * mt:512 * (mt + 1)]
                    self.dve(lambda e, d=et, s=bank: e.tensor_copy(d, s),
                             ins=[bkey], outs=["ocb"])
                    if ("rms" not in _SK) and ("rmsA" not in _SK):
                        self.act(lambda e, s=et: e.activation(sqt, s, AF.Square),
                                 ins=["ocb"], outs=["sqt"])
                        self.mm(self.PS[6][0:16, :], cst[:, 152 + 16 * mt:152 + 16 * (mt + 1)],
                                sqt, start=(mt == 0), stop=(mt == 3),
                                ins=["sqt", "c_sel16"], outs=["ps6"])
                prod16(lambda mt: (self.PS[mt], f"ps{mt}"), B1e, hq,
                       "B1e", "dhid", None, foldoc)
                if "rms" in _SK:
                    self.dma("ad", oc_r[:, 4 * h:4 * h + 4, CH * c:CH * (c + 1)],
                             ocb.rearrange("p (t x) -> p t x", t=4), ins=["ocb"],
                             outs=[f"oc{h}_{c}"])
                    continue
                nr = nrm[0:16, :]
                self.dve(lambda e, nr=nr: e.tensor_copy(nr, self.PS[6][0:16, :]),
                         ins=["ps6"], outs=["nrf"])
                self.gp(lambda e, nr=nr: e.partition_all_reduce(
                    nr[0:4, :], nr[0:4, :], 4, bass_isa.ReduceOp.add),
                    ins=["nrf"], outs=["nrf"])
                iv = nrm[0:1, :]
                self.act(lambda e, nr=nr, iv=iv: e.activation(
                    iv, nr[0:1, :], AF.Sqrt, bias=1e-6, scale=1.0 / 512.0),
                    ins=["nrf", "kconst"], outs=["ivt"])
                self.dve(lambda e, iv=iv: e.reciprocal(iv, iv),
                         ins=["ivt"], outs=["ivt"])
                bb = self.PS[7]
                self.mm(bb, self.P["onesr"].ap(), iv, True, True,
                        ins=["ones_r", "ivt"], outs=["ps7"])
                for mt in range(4):
                    et = ocb[:, 512 * mt:512 * (mt + 1)]
                    self.dve(lambda e, et=et, mt=mt: e.scalar_tensor_tensor(
                        et, et, cst[:, 48 + mt:49 + mt], bb, OP.mult, OP.mult),
                        ins=["ocb", "c_tnw", "ps7"], outs=["ocb"])
                self.dma("ad", oc_r[:, 4 * h:4 * h + 4, CH * c:CH * (c + 1)],
                         ocb.rearrange("p (t x) -> p t x", t=4), ins=["ocb"],
                         outs=[f"oc{h}_{c}"])
                if "keyp" in _SK:
                    continue
                def foldtm(dst, dkey, actf=None):
                    def f(mt, bank, bkey):
                        et = dst[:, 512 * mt:512 * (mt + 1)]
                        self.dve(lambda e, d=et, s=bank: e.tensor_copy(d, s),
                                 ins=[bkey], outs=[dkey])
                    return f
                prod16(lambda mt: (self.PS[mt], f"ps{mt}"), kcf, B0,
                       "kcf", "B0", None, foldtm(gk, "gk"))
                prod16(lambda mt: (self.PS[4 + (mt % 2)], f"ps{4 + (mt % 2)}"),
                       kcf, B2, "kcf", "B2", None, foldtm(hbm, "hbm"))
                prod16(lambda mt: (self.PS[6 + (mt % 2)], f"ps{6 + (mt % 2)}"),
                       vcf, B1o, "vcf", "B1o", None, foldtm(dhid, "dhid"))
                self.act(lambda e: e.activation(sg, gk, AF.Sigmoid),
                         ins=["gk"], outs=["sg"])
                self.dve(lambda e: e.tensor_mul(u_, gk, sg), ins=["gk", "sg"],
                         outs=["u_t"])
                self.dve(lambda e: e.tensor_mul(hid, u_, hbm), ins=["u_t", "hbm"],
                         outs=["hid"])
                self.dve(lambda e: e.tensor_mul(dgp, dhid, hbm), ins=["dhid", "hbm"],
                         outs=["dgp"])
                self.dve(lambda e: e.tensor_mul(dgp, dgp, sg), ins=["dgp", "sg"],
                         outs=["dgp"])
                self.dve(lambda e: e.tensor_sub(dhb, gk, u_), ins=["gk", "u_t"],
                         outs=["dhb"])
                self.dve(lambda e: e.scalar_tensor_tensor(
                    dhb, dhb, 1.0, dgp, OP.bypass, OP.mult), ins=["dhb", "dgp"],
                    outs=["dhb"])
                self.dve(lambda e: e.tensor_add(dgp, dgp, dhb), ins=["dgp", "dhb"],
                         outs=["dgp"])
                self.dve(lambda e: e.tensor_mul(dhb, dhid, u_), ins=["dhid", "u_t"],
                         outs=["dhb"])
                if do_upd:
                    for mt in range(4):
                        ti = 12 * (4 * c + mt)
                        for src, dst, lri, sk, dk in (
                                (kct, kl0, 0, "kct", "kl0"),
                                (kct, kl2, 8, "kct", "kl2"),
                                (vct, vl1, 4, "vct", "vl1")):
                            a = src[:, 512 * mt:512 * (mt + 1)]
                            d = dst[:, 512 * mt:512 * (mt + 1)]
                            lap = lrt[:, ti + lri + h:ti + lri + h + 1]
                            self.dve(lambda e, a=a, d=d, lap=lap: e.tensor_scalar(
                                d, a, lap, None, OP.mult),
                                ins=[sk, "lrt"], outs=[dk])
                    def foldupd(mst, bw, km, kb):
                        def f(mt, bank, bkey):
                            ms = mst[:, 512 * mt:512 * (mt + 1)]
                            bs = bw[:, 512 * mt:512 * (mt + 1)]
                            self.dve(lambda e, ms=ms, s=bank: e.tensor_add(ms, ms, s),
                                     ins=[bkey, km], outs=[km])
                            self.dve(lambda e, bs=bs, ms=ms: e.tensor_copy(bs, ms),
                                     ins=[km], outs=[kb])
                        return f
                    prod16(lambda mt: (self.PS[mt], f"ps{mt}"), kl0, dgp,
                           "kl0", "dgp", None, foldupd(M0, B0, "M0", "B0"))
                    prod16(lambda mt: (self.PS[4 + (mt % 2)], f"ps{4 + (mt % 2)}"),
                           kl2, dhb, "kl2", "dhb", None, foldupd(M2, B2, "M2", "B2"))
                    prod16(lambda mt: (self.PS[mt], f"ps{mt}"), vl1, hid,
                           "vl1", "hid", None, foldupd(M1o, B1o, "M1o", "B1o"))
                    prod16(lambda mt: (self.PS[4 + (mt % 2)], f"ps{4 + (mt % 2)}"),
                           hid, vl1, "hid", "vl1", None, foldupd(M1e, B1e, "M1e", "B1e"))
        self.S.barrier()
        self.RBF.reset(); self.RF4.reset()

    def emit_quant(self, x_ap, out_ap, keyx, keyo, W, qt):
        cst = self.P["consts"].ap()
        vt, dqT = qt["vt"], qt["dqT"]
        red, scn, inv = qt["red"], qt["scn"], qt["inv"]
        r_, st_, y_, w_ = qt["r"], qt["st"], qt["y"], qt["w"]
        it = qt["it"]
        B = W // 32
        def b3(ap):
            return ap[:, 0:W].rearrange("p (b x) -> p b x", x=32)
        def rep(small):
            return small[:, 0:B].rearrange("p (b o) -> p b o", o=1).broadcast_to([128, B, 32])
        self.dve(lambda e: e.transpose(vt[:, 0:W], x_ap), ins=[keyx], outs=["vt"])
        self.dve(lambda e: e.tensor_reduce(
            red[:, 0:B], b3(vt), AX, OP.max, apply_absolute_value=True),
            ins=["vt"], outs=["red"])
        self.dve(lambda e: e.tensor_scalar(
            scn[:, 0:B], red[:, 0:B], cst[:, 149:150], None, OP.mult),
            ins=["red", "rcp448"], outs=["scq"])
        self.dve(lambda e: e.tensor_scalar(
            scn[:, 0:B], scn[:, 0:B], -1e-30, None, OP.min),
            ins=["scq"], outs=["scq"])
        self.dve(lambda e: e.reciprocal(inv[:, 0:B], scn[:, 0:B]),
                 ins=["scq"], outs=["invq"])
        self.dve(lambda e: e.tensor_tensor(b3(r_), b3(vt), rep(inv), OP.mult),
                 ins=["vt", "invq"], outs=["r_"])
        self.dve(lambda e: e.tensor_scalar(
            it[:, 0:W], r_[:, 0:W].bitcast(I32), 0x7F800000, None,
            OP.bitwise_and), ins=["r_"], outs=["itq"])
        self.dve(lambda e: e.tensor_scalar(
            it[:, 0:W], it[:, 0:W], 0x3C800000, None, OP.max),
            ins=["itq"], outs=["itq"])
        self.dve(lambda e: e.tensor_scalar(
            it[:, 0:W], it[:, 0:W], 3 << 23, None, OP.subtract),
            ins=["itq"], outs=["itq"])
        self.dve(lambda e: e.tensor_tensor(
            b3(st_), b3(it.bitcast(F32)), rep(scn), OP.mult),
            ins=["itq", "scq"], outs=["stsq"])
        self.dve(lambda e: e.tensor_scalar(
            it[:, 0:W], it[:, 0:W], -1, None, OP.mult),
            ins=["itq"], outs=["itq"])
        self.dve(lambda e: e.tensor_scalar(
            it[:, 0:W], it[:, 0:W], 254 << 23, None, OP.add),
            ins=["itq"], outs=["itq"])
        self.dve(lambda e: e.tensor_tensor(
            y_[:, 0:W], r_[:, 0:W], it[:, 0:W].bitcast(F32), OP.mult),
            ins=["r_", "itq"], outs=["yq"])
        self.dve(lambda e: e.tensor_scalar(
            w_[:, 0:W], y_[:, 0:W], 12582912.0, None, OP.add),
            ins=["yq"], outs=["wq"])
        self.dve(lambda e: e.tensor_scalar(
            w_[:, 0:W], w_[:, 0:W], 12582912.0, None, OP.subtract),
            ins=["wq"], outs=["wq"])
        self.dve(lambda e: e.tensor_mul(dqT[:, 0:W], w_[:, 0:W], st_[:, 0:W]),
                 ins=["wq", "stsq"], outs=["dqT"])
        self.dve(lambda e: e.transpose(out_ap, dqT[:, 0:W]), ins=["dqT"], outs=[keyo])

    def quant_tiles(self, W):
        qt = {}
        for nm in ("scn", "inv", "red"):
            qt[nm] = self.RF4.tile(max(W // 32, 1))
        for nm in ("r", "st", "y", "w"):
            qt[nm] = self.RF4.tile(W)
        qt["vt"] = self.RBF.tile(W)
        qt["dqT"] = self.RBF.tile(W)
        qt["it"] = self.ITI.tile(W)
        return qt

    def f_wquant(self):
        import os as _os
        NT = int(_os.environ.get("KNT", "100000"))
        _done = 0
        D = self.D
        self.RBF.reset(); self.RF4.reset(); self.ITI.reset()
        W = 1024
        qt = self.quant_tiles(W)
        xin = self.RBF.tile(W)
        xout = self.RBF.tile(W)
        for src, dst, kt_n, m_n in (("wgT", "wgq_s", 16, FFI), ("wuT", "wuq_s", 16, FFI),
                                    ("wdT", "wdq_s", 64, H)):
            s_r = D[src].rearrange("(kt kp) m -> kp kt m", kp=128)
            d_r = D[dst].rearrange("(kt kp) m -> kp kt m", kp=128)
            for kt in range(kt_n):
                for m0 in range(0, m_n, W):
                    if _done >= NT:
                        continue
                    _done += 1
                    self.dma("gd", xin, s_r[:, kt, m0:m0 + W], outs=["qxin"])
                    self.emit_quant(xin, xout, "qxin", "qxout", W, qt)
                    self.dma("ad", d_r[:, kt, m0:m0 + W], xout,
                             ins=["qxout"], outs=["qso"])
        self.S.barrier()
        self.RBF.reset(); self.RF4.reset()

    def f5_mlpo(self):
        D = self.D
        cst = self.P["consts"].ap()
        self.RBF.reset(); self.RF4.reset()
        oc_r = D["oc_s"].rearrange("(t p) n -> p t n", p=128)
        ao_r = D["ao_s"].rearrange("(t p) n -> p t n", p=128)
        x0_r = D["x0own"].rearrange("(t p) n -> p t n", p=128)
        x1_r = D["x1_s"].rearrange("(t p) n -> p t n", p=128)
        osel = self.RBF.tile(16 * 512)
        asel = self.RBF.tile(16 * 512)
        bounce = self.RBF.tile(512)
        h2ch = self.RBF.tile(16 * 512)
        wcol = [self.RBF.tile(2048) for _ in range(2)]
        x1b = self.RF4.tile(16 * 512)
        sqt = self.RF4.tile(512)
        x0tile = self.RF4.tile(512)
        w_r = D["wmloT"].rearrange("(kt kp) m -> kp kt m", kp=128)
        mr = self.P["mrow"].ap()
        selc = cst[:, 150:151]
        for j in range(4):
            for (buf, srcr, k1) in ((osel, oc_r, "osel"), (asel, ao_r, "asel")):
                for t in range(16):
                    at = buf[:, 512 * t:512 * (t + 1)]
                    self.dma("sd", at, srcr[:, t, CH * j:CH * (j + 1)], outs=[k1])
                    self.dma("sd", bounce, srcr[:, t, CH * (j + 4):CH * (j + 5)],
                             outs=["bnc"])
                    self.dve(lambda e, at=at: e.tensor_sub(bounce, bounce, at),
                             ins=[k1, "bnc"], outs=["bnc"])
                    self.dve(lambda e: e.tensor_scalar(
                        bounce, bounce, selc, None, OP.mult),
                        ins=["bnc", "c_sel"], outs=["bnc"])
                    self.dve(lambda e, at=at: e.tensor_add(at, at, bounce),
                             ins=[k1, "bnc"], outs=[k1])
            for m in range(16):
                wc = wcol[m % 2]
                wkey = f"wc{m % 2}"
                self.dma("gd", wc.rearrange("p (t x) -> p t x", t=16),
                         w_r[:, :, 128 * m:128 * (m + 1)], outs=[wkey])
                bank = self.PS[m % 4]
                bkey = f"ps{m % 4}"
                for k in range(16):
                    self.mm(bank, wc[:, 128 * k:128 * (k + 1)],
                            osel[:, 512 * k:512 * (k + 1)],
                            start=(k == 0), stop=(k == 15),
                            ins=[wkey, "osel"], outs=[bkey])
                xt = x1b[:, 512 * m:512 * (m + 1)]
                at = asel[:, 512 * m:512 * (m + 1)]
                self.dve(lambda e, xt=xt, s=bank, at=at: e.tensor_add(xt, s, at),
                         ins=[bkey, "asel"], outs=["x1b"])
                self.dve(lambda e, xt=xt, m=m: e.tensor_scalar(
                    xt, xt, cst[:, 84 + m:85 + m], None, OP.mult),
                    ins=["x1b", "c_ga"], outs=["x1b"])
                self.dma("sd", x0tile, x0_r[:, m, CH * j:CH * (j + 1)], outs=["x0t"])
                self.dve(lambda e, xt=xt: e.tensor_add(xt, xt, x0tile),
                         ins=["x1b", "x0t"], outs=["x1b"])
                ps_s = self.PS[6][0:1, :]
                ps_q = self.PS[7][0:1, :]
                self.mm(ps_s, cst[:, 151:152], xt, start=(m == 0), stop=(m == 15),
                        ins=["ones_c", "x1b"], outs=["ps6"])
                self.act(lambda e, xt=xt: e.activation(sqt, xt, AF.Square),
                         ins=["x1b"], outs=["sq2"])
                self.mm(ps_q, cst[:, 151:152], sqt, start=(m == 0), stop=(m == 15),
                        ins=["ones_c", "sq2"], outs=["ps7"])
            self.dma("ad", x1_r[:, :, CH * j:CH * (j + 1)],
                     x1b.rearrange("p (t x) -> p t x", t=16), ins=["x1b"],
                     outs=[f"x1{j}"])
            m_ = mr[:, 0:512]; A_ = mr[:, 512:1024]; B_ = mr[:, 1024:1536]
            v_ = mr[:, 1536:2048]
            self.dve(lambda e: e.tensor_scalar_mul(m_, self.PS[6][0:1, :], 1.0 / 2048.0),
                     ins=["ps6"], outs=["m2_"])
            self.dve(lambda e: e.tensor_scalar_mul(v_, self.PS[7][0:1, :], 1.0 / 2048.0),
                     ins=["ps7"], outs=["v2_"])
            self.dve(lambda e: e.scalar_tensor_tensor(
                A_, m_, 1.0, m_, OP.bypass, OP.mult), ins=["m2_"], outs=["A2_"])
            self.dve(lambda e: e.tensor_sub(v_, v_, A_), ins=["v2_", "A2_"], outs=["v2_"])
            self.act(lambda e: e.activation(A_, v_, AF.Sqrt, bias=1e-5),
                     ins=["v2_", "kconst"], outs=["A2_"])
            self.dve(lambda e: e.reciprocal(A_, A_), ins=["A2_"], outs=["A2_"])
            self.dve(lambda e: e.scalar_tensor_tensor(
                B_, m_, -1.0, A_, OP.mult, OP.mult), ins=["m2_", "A2_"], outs=["B2_"])
            psA = self.PS[4]; psB = self.PS[5]
            self.mm(psA, self.P["onesr"].ap(), A_, True, True,
                    ins=["ones_r", "A2_"], outs=["ps4"])
            self.mm(psB, self.P["onesr"].ap(), B_, True, True,
                    ins=["ones_r", "B2_"], outs=["ps5"])
            for m in range(16):
                xt = x1b[:, 512 * m:512 * (m + 1)]
                ht = h2ch[:, 512 * m:512 * (m + 1)]
                self.dve(lambda e, xt=xt: e.tensor_mul(sqt, xt, psA),
                         ins=["x1b", "ps4"], outs=["sq2"])
                self.dve(lambda e: e.tensor_add(sqt, sqt, psB),
                         ins=["sq2", "ps5"], outs=["sq2"])
                self.dve(lambda e, ht=ht, m=m: e.tensor_scalar(
                    ht, sqt, cst[:, 116 + m:117 + m], cst[:, 100 + m:101 + m],
                    OP.mult, OP.add), ins=["sq2", "c_cb1", "c_sb"], outs=["h2ch"])
            self.dma("ad", D["h2_s"].rearrange("(t p) n -> p t n", p=128)[
                :, :, CH * j:CH * (j + 1)],
                h2ch.rearrange("p (t x) -> p t x", t=16), ins=["h2ch"],
                outs=[f"h2s{j}"])
        self.S.barrier()
        self.RBF.reset(); self.RF4.reset()

    def f6_ffn(self):
        D = self.D
        cst = self.P["consts"].ap()
        self.RF4.reset()
        self.RBF.reset(); self.ITI.reset()
        qt = self.quant_tiles(1024)
        self.h2res = self.RBF.tile(16 * 2048)
        bounce = self.RBF.tile(1024)
        h2_r = D["h2_s"].rearrange("(t p) n -> p t n", p=128)
        for t in range(16):
            for half in range(2):
                xa = self.h2res[:, 2048 * t + 1024 * half:2048 * t + 1024 * (half + 1)]
                self.dma("sd", bounce, h2_r[:, t, 1024 * half:1024 * (half + 1)],
                         outs=["h2b"])
                self.emit_quant(bounce, xa, "h2b", "h2res", 1024, qt)
        wcol = [self.RBF.tile(2048) for _ in range(2)]
        hkbuf = self.RBF.tile(1024)
        hrow = self.RBF.tile(512)
        hqt = self.RBF.tile(512)
        qt2 = self.quant_tiles(512)
        wg_r = D["wgq_s"].rearrange("(kt kp) m -> kp kt m", kp=128)
        wu_r = D["wuq_s"].rearrange("(kt kp) m -> kp kt m", kp=128)
        wd_r = D["wdq_s"].rearrange("(kt kp) m -> kp kt m", kp=128)
        hq_r = D["hq_s"].rearrange("(kt kp) m -> kp kt m", kp=128)
        x1_r = D["x1_s"].rearrange("(t p) n -> p t n", p=128)
        out_r = D["out"].rearrange("(t p) n -> p t n", p=128)
        x1t = self.RF4.tile(512)
        hrow2 = self.RF4.tile(512)
        for tg in range(4):
            for ht in range(64):
                for gu, wr in ((0, wg_r), (1, wu_r)):
                    wc = wcol[gu]
                    wkey = f"wq{gu}"
                    self.dma("sd", wc.rearrange("p (t x) -> p t x", t=16),
                             wr[:, :, 128 * ht:128 * (ht + 1)], outs=[wkey])
                    bank = self.PS[gu]
                    bkey = f"ps{gu}"
                    for k in range(16):
                        self.mm(bank, wc[:, 128 * k:128 * (k + 1)],
                                self.h2res[:, 2048 * k + 512 * tg:
                                           2048 * k + 512 * (tg + 1)],
                                start=(k == 0), stop=(k == 15),
                                ins=[wkey, "h2res"], outs=[bkey])
                self.act(lambda e: e.activation(hrow, self.PS[0], AF.Silu),
                         ins=["ps0"], outs=["hrow"])
                self.dve(lambda e: e.tensor_mul(hrow, hrow, self.PS[1]),
                         ins=["hrow", "ps1"], outs=["hrow"])
                self.emit_quant(hrow, hqt, "hrow", "hqt", 512, qt2)
                self.dma("ad", hq_r[:, ht, 512 * tg:512 * (tg + 1)], hqt,
                         ins=["hqt"], outs=["hq_s"])
            for mg in range(4):
                for kt in range(64):
                    hk = hkbuf[:, 0:512]
                    hkk = "hk0"
                    self.dma("sd", hk[:, 0:512], hq_r[:, kt, 512 * tg:512 * (tg + 1)],
                             ins=["hq_s"], outs=[hkk])
                    wdc = hkbuf[:, 512:1024]
                    wdk = "wd0"
                    self.dma("sd", wdc.rearrange("p (t x) -> p t x", t=4),
                             wd_r[:, kt, 512 * mg:512 * (mg + 1)].rearrange(
                                 "p (t x) -> p t x", t=4), outs=[wdk])
                    for mi in range(4):
                        bank = self.PS[4 + mi]
                        bkey = f"ps{4 + mi}"
                        self.mm(bank, wdc[:, 128 * mi:128 * (mi + 1)],
                                hk[:, 0:512], start=(kt == 0), stop=(kt == 63),
                                ins=[hkk, wdk], outs=[bkey])
                for mi in range(4):
                    m = 4 * mg + mi
                    bank = self.PS[4 + mi]
                    bkey = f"ps{4 + mi}"
                    self.dma("sd", x1t, x1_r[:, m, CH * tg:CH * (tg + 1)],
                             outs=["x1t"])
                    self.dve(lambda e, bank=bank, m=m: e.tensor_scalar(
                        hrow2, bank, cst[:, 132 + m:133 + m], None, OP.mult),
                        ins=[bkey, "c_gb"], outs=["x2t"])
                    self.dve(lambda e: e.tensor_add(hrow2, hrow2, x1t),
                             ins=["x2t", "x1t"], outs=["x2t"])
                    self.dma("ad", out_r[:, m, CH * tg:CH * (tg + 1)], hrow2,
                             ins=["x2t"], outs=["outd"])
        self.S.barrier()

    def finish(self):
        self._finish_inner()
        mybir.codegen_inst_isa_subclasses(self.nc)

    def _finish_inner(self):
        nc = self.nc
        sems = {k: self.st.enter_context(nc.semaphore(k)) for k in SEMS}
        blk = self.st.enter_context(nc.Block())
        S = self.S

        def replay(stream):
            def run(eng):
                for waits, emit, sem, inc in S.ops[stream]:
                    for psem, pcnt in waits:
                        eng.wait_ge(sems[psem], pcnt)
                    if emit is not None:
                        ins = emit(eng)
                        if sem is not None:
                            ins.then_inc(sems[sem], inc)
            return run

        blk.tensor(replay("pe"))
        blk.vector(replay("dve"))
        blk.scalar(replay("act"))
        blk.gpsimd(replay("gp"))
        blk.sync(replay("sync"))

    def close(self):
        self.st.close()


def build_nc(upto="all"):
    nc = bass.Bass("TRN2", target_bir_lowering=False)
    kb = KB(nc, upto)
    kb.alloc_all()
    kb.f0()
    if upto in ("f0", "f0a", "f0b", "f0c"):
        kb.finish(); return nc, kb
    kb.f1_ln1()
    if upto in ("f1a",):
        kb.finish(); return nc, kb
    kb.f1_qk()
    if upto in ("f1b",):
        kb.finish(); return nc, kb
    kb.f1_v()
    if upto in ("f1v",):
        kb.finish(); return nc, kb
    kb.f2_rope()
    if upto in ("f2a",):
        kb.finish(); return nc, kb
    kb.f2_attn()
    if upto in ("f2b",):
        kb.finish(); return nc, kb
    kb.f2_oproj()
    if upto in ("f2c",):
        kb.finish(); return nc, kb
    kb.f3_mlpqkv()
    if upto in ("f3",):
        kb.finish(); return nc, kb
    kb.f4_ttt()
    if upto in ("f4",):
        kb.finish(); return nc, kb
    kb.f_wquant()
    if upto in ("f4b",):
        kb.finish(); return nc, kb
    kb.f5_mlpo()
    if upto in ("f5",):
        kb.finish(); return nc, kb
    kb.f6_ffn()
    kb.finish()
    return nc, kb


def kernel(**inputs):
    in_maps = host_inputs(inputs)
    nc, kb = build_nc()
    res = run_bass_kernel_spmd(nc, in_maps, core_ids=list(range(8)))
    outs = []
    for core in range(8):
        outs.append(np.asarray(res.results[core]["out"]).T)
    full = np.empty((4, T, H), dtype=np.float32)
    for core in range(8):
        b, s = core // 2, core % 2
        full[b, 2048 * s:2048 * (s + 1)] = outs[core]
    return full


# version 11
# speedup vs baseline: 1.2179x; 1.2179x over previous
import math
from contextlib import ExitStack

import numpy as np
import ml_dtypes
import concourse.bass as bass
import concourse.bass_isa as bass_isa
from concourse.library_config import all_libraries
LIB_ATTN = all_libraries[1]
import concourse.mybir as mybir
from concourse.bass_utils import run_bass_kernel_spmd

F32 = mybir.dt.float32
BF16 = mybir.dt.bfloat16
I32 = mybir.dt.int32
AX = mybir.AxisListType.X
OP = mybir.AluOpType
AF = mybir.ActivationFunctionType

H = 2048
T = 4096
CH = 512
NCH = T // CH
NAH = 16
AHD = 128
NFH = 4
FHD = 512
WIN = 1024
FFI = 8192
LN2 = math.log(2.0)
BASE_LR_INV = 0.001 + float(np.log(-np.expm1(-0.001)))
ATT_SCALE = 1.0 / math.sqrt(AHD)

STREAMS = ("pe", "dve", "act", "gp", "sync")
NPOOL = 8
DMA_QUEUES = ("sd", "ad", "gd")
SEMS = ("pe", "dve", "act", "gp") + tuple(
    f"{q}{i}" for q in DMA_QUEUES for i in range(NPOOL))
STREAM_SEM = {"pe": "pe", "dve": "dve", "act": "act", "gp": "gp"}


class Sched:

    def __init__(self):
        self.ops = {s: [] for s in STREAMS}
        self.cnt = {s: 0 for s in SEMS}
        self.water = {s: {p: 0 for p in SEMS} for s in STREAMS}
        self.bufs = {}
        self.rr = {q: 0 for q in DMA_QUEUES}

    def _buf(self, key):
        b = self.bufs.get(key)
        if b is None:
            b = {"w": None, "r": {}}
            self.bufs[key] = b
        return b

    def op(self, stream, emit, ins=(), outs=(), sem=None, inc=None):
        isdma = sem in DMA_QUEUES
        if isdma:
            q = sem
            sem = f"{q}{self.rr[q] % NPOOL}"
            self.rr[q] += 1
        elif sem is None:
            sem = STREAM_SEM[stream]
        if inc is None:
            inc = 16 if isdma else 1
        waits = {}

        def need(psem, pcnt):
            if stream == "pe" and psem == "pe":
                return
            if pcnt > self.water[stream][psem]:
                waits[psem] = max(waits.get(psem, 0), pcnt)

        if isdma and self.cnt[sem] > 0:
            need(sem, self.cnt[sem])
        for k in ins:
            b = self._buf(k)
            if b["w"] is not None:
                need(*b["w"])
        for k in outs:
            b = self._buf(k)
            if b["w"] is not None:
                need(*b["w"])
            for psem, pcnt in b["r"].items():
                need(psem, pcnt)
        for psem, pcnt in waits.items():
            self.water[stream][psem] = pcnt
        self.cnt[sem] += inc
        mycnt = self.cnt[sem]
        self.ops[stream].append((sorted(waits.items()), emit, sem if inc else None, inc))
        for k in ins:
            self._buf(k)["r"][sem] = mycnt
        for k in outs:
            b = self._buf(k)
            b["w"] = (sem, mycnt)
            b["r"] = {}

    def barrier(self):
        snap = dict(self.cnt)
        for s in STREAMS:
            for p in SEMS:
                if p == STREAM_SEM.get(s):
                    continue
                if snap[p] > self.water[s][p]:
                    self.water[s][p] = snap[p]
                    self.ops[s].append(([(p, snap[p])], None, None, 0))


class Arena:

    def __init__(self, ap, cols, name):
        self.ap = ap
        self.cols = cols
        self.off = 0
        self.name = name
        self.marks = []

    def tile(self, n):
        assert self.off + n <= self.cols, f"{self.name} arena overflow: {self.off}+{n}>{self.cols}"
        v = self.ap[:, self.off:self.off + n]
        self.off += n
        return v

    def push(self):
        self.marks.append(self.off)

    def pop(self):
        self.off = self.marks.pop()

    def reset(self):
        self.off = 0
        self.marks = []


def cdiv(a, b):
    return (a + b - 1) // b



_ROPE_CACHE = {}


def rope_tables():
    if "C" in _ROPE_CACHE:
        return _ROPE_CACHE["C"], _ROPE_CACHE["S"]
    import os
    import subprocess
    import sys
    code = (
        "import numpy as np\n"
        "import jax\n"
        "jax.config.update('jax_platforms', 'cpu')\n"
        "import jax.numpy as jnp\n"
        f"AHD={AHD}; T={T}\n"
        "inv_freq = 1.0 / 500000.0 ** (jnp.arange(0, AHD, 2, dtype=jnp.float32) / AHD)\n"
        "t = jnp.arange(T, dtype=jnp.float32)\n"
        "freqs = jnp.outer(t, inv_freq)\n"
        "cos = np.asarray(jnp.cos(freqs))\n"
        "sin = np.asarray(jnp.sin(freqs))\n"
        "import sys\n"
        "sys.stdout.buffer.write(cos.astype(np.float32).tobytes())\n"
        "sys.stdout.buffer.write(sin.astype(np.float32).tobytes())\n"
    )
    env = dict(os.environ)
    env["JAX_PLATFORMS"] = "cpu"
    r = subprocess.run([sys.executable, "-c", code], check=True, env=env,
                       capture_output=True)
    n = T * (AHD // 2) * 4
    cos = np.frombuffer(r.stdout[:n], dtype=np.float32).reshape(T, AHD // 2)
    sin = np.frombuffer(r.stdout[n:2 * n], dtype=np.float32).reshape(T, AHD // 2)
    Ct = np.concatenate([cos.T, cos.T], axis=0).astype(np.float32)
    St = np.concatenate([-sin.T, sin.T], axis=0).astype(np.float32)
    _ROPE_CACHE["C"] = np.ascontiguousarray(Ct)
    _ROPE_CACHE["S"] = np.ascontiguousarray(St)
    return _ROPE_CACHE["C"], _ROPE_CACHE["S"]


def attn_masks():
    kk = np.arange(128)[:, None]
    qq = np.arange(128)[None, :]
    edge = (kk > qq).astype(ml_dtypes.bfloat16)
    diag = (kk <= qq).astype(ml_dtypes.bfloat16)
    return np.ascontiguousarray(edge), np.ascontiguousarray(diag)


def host_inputs(z):
    x0 = np.asarray(z["L0_attn_hidden_states"])
    TR = lambda a: np.ascontiguousarray(np.asarray(a, np.float32).T)
    colv = lambda a: np.ascontiguousarray(
        np.asarray(a, np.float32).reshape(16, 128).T)
    Ct, St = rope_tables()
    edge, diag = attn_masks()
    w0 = np.asarray(z["L0_attn_w0"], np.float32)
    w1 = np.asarray(z["L0_attn_w1"], np.float32)
    w2 = np.asarray(z["L0_attn_w2"], np.float32)
    sel = np.zeros((128, 16, 16), dtype=np.float32)
    for m in range(16):
        sel[:, m, m] = 1.0
    shared = {
        "wqkvT": TR(z["L0_attn_attn_qkv_w"]),
        "woT": TR(z["L0_attn_attn_o_w"]),
        "wmlpT": TR(z["L0_attn_mlp_qkv_w"]),
        "wmloT": TR(z["L0_attn_mlp_o_w"]),
        "wlrT": TR(z["L0_attn_lr_w"]),
        "wgT": TR(z["L0_ffn_gate_w"]),
        "wuT": TR(z["L0_ffn_up_w"]),
        "wdT": TR(z["L0_ffn_down_w"]),
        "w0T": np.ascontiguousarray(w0.transpose(0, 2, 1)),
        "w2T": np.ascontiguousarray(w2.transpose(0, 2, 1)),
        "w1o": np.ascontiguousarray(w1),
        "w1e": np.ascontiguousarray(w1.transpose(0, 2, 1)),
        "qnw": colv(z["L0_attn_q_norm_w"]),
        "knw": colv(z["L0_attn_k_norm_w"]),
        "qnw_s": np.ascontiguousarray(np.roll(colv(z["L0_attn_q_norm_w"]), 64, axis=0)),
        "knw_s": np.ascontiguousarray(np.roll(colv(z["L0_attn_k_norm_w"]), 64, axis=0)),
        "mnw": colv(z["L0_attn_mlp_norm_w"]),
        "tnw": np.ascontiguousarray(
            np.tile(np.asarray(z["L0_attn_ttt_norm_w"], np.float32).reshape(4, 128).T,
                    (1, 1))),
        "lrb": np.ascontiguousarray(np.asarray(z["L0_attn_lr_b"], np.float32).reshape(12, 1)),
        "lrb_row": np.ascontiguousarray(np.asarray(z["L0_attn_lr_b"], np.float32).reshape(1, 12)),
        "wmod": np.ascontiguousarray(np.asarray(z["L0_wmod"], np.float32)),
        "ropeC": Ct, "ropeS": St,
        "mask_edge": edge, "mask_diag": diag,
        "sel16": np.ascontiguousarray(sel.reshape(128, 256)),
    }
    in_maps = []
    for core in range(8):
        b, s = core // 2, core % 2
        m = dict(shared)
        m["x0fm"] = np.ascontiguousarray(x0[b].T.astype(np.float32))
        m["x0own"] = np.ascontiguousarray(x0[b, 2048 * s:2048 * (s + 1)].T.astype(np.float32))
        m["c_col"] = np.ascontiguousarray(
            np.asarray(z["L0_c"], np.float32)[b].reshape(16, 128).T)
        m["selmask"] = np.full((128, 1), float(s), dtype=np.float32)
        in_maps.append(m)
    return in_maps



class KB:

    def __init__(self, nc, upto="all"):
        self.nc = nc
        self.S = Sched()
        self.upto = upto
        self.st = ExitStack()
        self._declare_io()

    def dma(self, q, dst, src, ins=(), outs=(), transpose=False):
        stream = {"sd": "sync", "ad": "act", "gd": "gp"}[q]
        if transpose:
            self.S.op(stream, lambda e: e.dma_start_transpose(dst, src),
                      ins=ins, outs=outs, sem=q)
        else:
            self.S.op(stream, lambda e: e.dma_start(dst, src),
                      ins=ins, outs=outs, sem=q)

    def pe(self, emit, ins=(), outs=()):
        self.S.op("pe", emit, ins=ins, outs=outs)

    def dve(self, emit, ins=(), outs=()):
        self.S.op("dve", emit, ins=ins, outs=outs)

    def act(self, emit, ins=(), outs=()):
        self.S.op("act", emit, ins=ins, outs=outs)

    def gp(self, emit, ins=(), outs=()):
        self.S.op("gp", emit, ins=ins, outs=outs)

    def mm(self, out, lhsT, rhs, start, stop, ins, outs):
        self.pe(lambda e: e.matmul(out, lhsT, rhs, start=start, stop=stop,
                                   skip_group_check=True),
                ins=ins, outs=outs)

    def _declare_io(self):
        nc = self.nc
        D = {}

        def inp(name, shape, dt=F32):
            D[name] = nc.dram_tensor(name, list(shape), dt, kind="ExternalInput").ap()

        inp("x0fm", (H, T)); inp("x0own", (H, 2048))
        inp("wqkvT", (H, 3 * H)); inp("woT", (H, H))
        inp("wmlpT", (H, 3 * H)); inp("wmloT", (H, H)); inp("wlrT", (H, 12))
        inp("wgT", (H, FFI)); inp("wuT", (H, FFI)); inp("wdT", (FFI, H))
        inp("w0T", (NFH, FHD, FHD)); inp("w2T", (NFH, FHD, FHD))
        inp("w1o", (NFH, FHD, FHD)); inp("w1e", (NFH, FHD, FHD))
        inp("qnw", (128, 16)); inp("knw", (128, 16)); inp("mnw", (128, 16))
        inp("qnw_s", (128, 16)); inp("knw_s", (128, 16))
        inp("tnw", (128, 4)); inp("lrb", (12, 1))
        inp("wmod", (H, 6 * H)); inp("c_col", (128, 16))
        inp("lrb_row", (1, 12))
        inp("ropeC", (128, T)); inp("ropeS", (128, T))
        inp("mask_edge", (128, 128), BF16); inp("mask_diag", (128, 128), BF16)
        inp("sel16", (128, 256)); inp("selmask", (128, 1))
        D["out"] = nc.dram_tensor("out", [H, 2048], F32, kind="ExternalOutput").ap()

        import os as _os
        _dbg = "KDBG" in _os.environ

        def scratch(name, shape, dt=BF16):
            kind = "ExternalOutput" if (_dbg and name in (
                "q_s", "ao_s", "xm_s", "fq_s", "oc_s", "x1_s", "h2_s", "wgq_s")) else "Internal"
            D[name] = nc.dram_tensor(name, list(shape), dt, kind=kind).ap()

        scratch("h_s", (H, T))
        scratch("q_s", (H, T)); scratch("k_s", (H, T))
        scratch("qr_s", (H, T)); scratch("kr_s", (H, T))
        scratch("v_s", (T, H))
        scratch("o_s", (H, T)); scratch("ao_s", (H, T)); scratch("xm_s", (H, T))
        scratch("fq_s", (H, T))
        scratch("fk_s", (T, H)); scratch("fv_s", (T, H))
        scratch("oc_s", (H, T))
        scratch("x1_s", (H, 2048), F32)
        scratch("h2_s", (H, 2048))
        scratch("wgq_s", (H, FFI)); scratch("wuq_s", (H, FFI))
        scratch("wdq_s", (FFI, H)); scratch("hq_s", (FFI, 2048))
        scratch("dbg", (128, 4096), F32)
        self.D = D

    def alloc_all(self):
        nc, st = self.nc, self.st
        self.RBF_t = st.enter_context(nc.sbuf_tensor("RBF", [128, 43 * 1024], BF16))
        self.RF4_t = st.enter_context(nc.sbuf_tensor("RF4", [128, 21 * 1024], F32))
        self.RBF = Arena(self.RBF_t.ap(), 43 * 1024, "RBF")
        self.RF4 = Arena(self.RF4_t.ap(), 21 * 1024, "RF4")
        self.ITI_t = st.enter_context(nc.sbuf_tensor("ITI", [128, 2048], mybir.dt.int32))
        self.ITI = Arena(self.ITI_t.ap(), 2048, "ITI")
        P = {}
        P["consts"] = st.enter_context(nc.sbuf_tensor("consts", [128, 768], F32))
        P["onesr"] = st.enter_context(nc.sbuf_tensor("onesr", [1, 128], F32))
        P["onesb"] = st.enter_context(nc.sbuf_tensor("onesb", [128, 1], BF16))
        P["rows"] = st.enter_context(nc.sbuf_tensor("rows", [16, T], BF16))
        P["rows2"] = st.enter_context(nc.sbuf_tensor("rows2", [16, T], BF16))
        P["lr_t"] = st.enter_context(nc.sbuf_tensor("lr_t", [128, 384], F32))
        P["mrow"] = st.enter_context(nc.sbuf_tensor("mrow", [1, 2048], F32))
        P["maskE"] = st.enter_context(nc.sbuf_tensor("maskE", [128, 128], BF16))
        P["maskD"] = st.enter_context(nc.sbuf_tensor("maskD", [128, 128], BF16))
        self.P = P
        cst = P["consts"].ap()
        self.kconst_slots = {}
        vals = [-1.0, 1.0, 0.5, 1e-5, 1e-6, 67.0, 2.0, -2.0, float(BASE_LR_INV)]
        for i, v in enumerate(vals):
            ap = cst[:, 428 + i:429 + i]
            nc.const_aps.register(F32, float(v), ap)
            self.kconst_slots[float(v)] = ap
        self.PS = [st.enter_context(nc.psum_tensor(f"ps{i}", [128, 512], F32)).ap()
                   for i in range(8)]

    def C(self, a, b):
        return self.P["consts"].ap()[:, a:b]

    def f0(self):
        import os
        SKIP = os.environ.get("KSKIP", "")
        D, S = self.D, self.S
        cst = self.P["consts"].ap()
        self.S.op("gp", lambda e: e.load_library(LIB_ATTN), inc=0)
        self.dma("sd", cst[:, 0:16], D["qnw"], outs=["c_qnw"])
        self.dma("sd", cst[:, 16:32], D["knw"], outs=["c_knw"])
        self.dma("sd", cst[:, 32:48], D["mnw"], outs=["c_mnw"])
        self.dma("sd", cst[:, 640:656], D["qnw_s"], outs=["c_qnws"])
        self.dma("sd", cst[:, 656:672], D["knw_s"], outs=["c_knws"])
        self.dma("sd", cst[:, 48:52], D["tnw"], outs=["c_tnw"])
        self.dma("sd", cst[:, 150:151], D["selmask"], outs=["c_sel"])
        self.dma("sd", cst[:, 152:408], D["sel16"], outs=["c_sel16"])
        self.dma("sd", cst[:, 408:424], D["c_col"], outs=["c_ccol"])
        self.dma("sd", self.P["maskE"].ap(), D["mask_edge"], outs=["maskE"])
        self.dma("sd", self.P["maskD"].ap(), D["mask_diag"], outs=["maskD"])
        self.dma("sd", cst[0:12, 424:425], D["lrb"], outs=["lrb"])
        if "lrb" not in SKIP:
            self.dma("sd", cst[0:1, 472:484], D["lrb_row"], outs=["lrbr"])
            self.gp(lambda e: e.partition_broadcast(cst[:, 472:484], cst[0:1, 472:484], 128),
                    ins=["lrbr"], outs=["lrbb"])
        self.dve(lambda e: e.memset(cst[:, 146:147], 2.0), outs=["two_c"])
        self.act(lambda e: e.activation(cst[:, 147:148], cst[:, 146:147], AF.Ln),
                 ins=["two_c"], outs=["ln2col"])
        self.dve(lambda e: e.tensor_scalar(
            cst[:, 145:146], cst[:, 147:148], -1.0, None, OP.mult),
            ins=["ln2col"], outs=["ln2col"])
        self.dve(lambda e: e.tensor_scalar(
            cst[:, 144:145], cst[:, 147:148], 16.0, None, OP.mult),
            ins=["ln2col"], outs=["ln2col"])
        for v, ap in self.kconst_slots.items():
            self.dve(lambda e, a=ap, vv=v: e.memset(a, vv), outs=["kconst"])
        self.dve(lambda e: e.memset(cst[:, 151:152], 1.0), outs=["ones_c"])
        self.dve(lambda e: e.memset(self.P["onesr"].ap(), 1.0), outs=["ones_r"])
        self.dve(lambda e: e.memset(self.P["onesb"].ap(), 1.0), outs=["ones_b"])
        self.dve(lambda e: e.memset(cst[:, 148:149], -448.0), outs=["c448"])
        self.dve(lambda e: e.reciprocal(cst[:, 149:150], cst[:, 148:149]),
                 ins=["c448"], outs=["rcp448"])
        if self.upto == "f0a":
            self.S.barrier()
            return
        self.RBF.push()
        csil = self.RF4.tile(16)
        self.act(lambda e: e.activation(csil, cst[:, 408:424], AF.Silu),
                 ins=["c_ccol"], outs=["csil"])
        wst = self.RF4.tile(2048)
        wstb = self.RF4.tile(2048)
        modv = self.RF4.tile(96)
        self.modv = modv
        wmod_r = D["wmod"].rearrange("(kt kp) m -> kp kt m", kp=128)
        for mc in range(96):
            stg = wst if mc % 2 == 0 else wstb
            key = f"modw{mc % 2}"
            self.dma("sd", stg.rearrange("p (kt x) -> p kt x", kt=16),
                     wmod_r[:, :, 128 * mc:128 * (mc + 1)], outs=[key])
            ps = self.PS[mc % 2][:, 0:1]
            pkey = f"ps{mc % 2}"
            for k in range(16):
                self.mm(ps, stg[:, 128 * k:128 * (k + 1)], csil[:, k:k + 1],
                        start=(k == 0), stop=(k == 15),
                        ins=[key, "csil"], outs=[pkey])
            mcv = mc
            self.dve(lambda e, m=mcv, p=ps: e.tensor_copy(modv[:, m:m + 1], p),
                     ins=[pkey], outs=["modv"])
        if self.upto == "f0b":
            self.RBF.pop()
            self.S.barrier()
            return
        self.dve(lambda e: e.tensor_copy(cst[:, 52:68], modv[:, 0:16]),
                 ins=["modv"], outs=["c_sa"])
        self.act(lambda e: e.activation(cst[:, 68:84], modv[:, 16:32], AF.Identity,
                                        bias=1.0), ins=["modv", "kconst"], outs=["c_ca1"])
        self.dve(lambda e: e.tensor_copy(cst[:, 100:116], modv[:, 48:64]),
                 ins=["modv"], outs=["c_sb"])
        self.act(lambda e: e.activation(cst[:, 116:132], modv[:, 64:80], AF.Identity,
                                        bias=1.0), ins=["modv", "kconst"], outs=["c_cb1"])
        if self.upto == "f0c":
            self.RBF.pop()
            self.S.barrier()
            return
        for src0, dst0, nm in ((32, 84, "c_ga"), (80, 132, "c_gb")):
            sgn = self.RF4.tile(16)
            self.dve(lambda e, s=src0, d=sgn: e.tensor_scalar(
                d, modv[:, s:s + 16], 0.0, None, OP.is_ge),
                ins=["modv"], outs=["sgn_t"])
            self.act(lambda e, d=sgn: e.activation(d, d, AF.Identity, bias=-1.0,
                                                   scale=2.0),
                     ins=["sgn_t", "kconst"], outs=["sgn_t"])
            mag = self.RF4.tile(16)
            self.act(lambda e, s=src0, d=mag: e.activation(
                d, modv[:, s:s + 16], AF.Abs), ins=["modv"], outs=["mag_t"])
            self.dve(lambda e, d=mag: e.tensor_scalar(
                d, d, 0.25, None, OP.max), ins=["mag_t"], outs=["mag_t"])
            self.dve(lambda e, d=dst0, a=sgn, b=mag: e.tensor_mul(
                cst[:, d:d + 16], a, b), ins=["sgn_t", "mag_t"], outs=[nm])
        self.RBF.pop()
        S.barrier()
        self.RF4.reset()

    def f1_ln1(self):
        D = self.D
        cst = self.P["consts"].ap()
        self.RBF.reset(); self.RF4.reset()
        xch = [self.RF4.tile(512) for _ in range(2)]
        self.RF4.reset()
        xbuf = [self.RF4.tile(16 * 512) for _ in range(2)]
        hbuf = self.RBF.tile(16 * 512)
        sqt = self.RF4.tile(512)
        x0r = D["x0fm"].rearrange("(t p) n -> p t n", p=128)
        h_sr = D["h_s"].rearrange("(t p) n -> p t n", p=128)
        for c in range(NCH):
            xb = xbuf[c % 2]
            xk = f"x0c{c % 2}"
            self.dma("sd", xb.rearrange("p (t x) -> p t x", t=16),
                     x0r[:, :, CH * c:CH * (c + 1)], outs=[xk])
            ps_s = self.PS[4][0:1, :]
            ps_q = self.PS[5][0:1, :]
            for t in range(16):
                xt = xb[:, 512 * t:512 * (t + 1)]
                self.mm(ps_s, cst[:, 151:152], xt, start=(t == 0), stop=(t == 15),
                        ins=["ones_c", xk], outs=["ps4"])
                self.act(lambda e, xt=xt: e.activation(sqt, xt, AF.Square),
                         ins=[xk], outs=["sqt"])
                self.mm(ps_q, cst[:, 151:152], sqt, start=(t == 0), stop=(t == 15),
                        ins=["ones_c", "sqt"], outs=["ps5"])
            mr = self.P["mrow"].ap()
            m_ = mr[:, 0:512]; A_ = mr[:, 512:1024]; B_ = mr[:, 1024:1536]
            v_ = mr[:, 1536:2048]
            self.dve(lambda e: e.tensor_scalar_mul(m_, ps_s, 1.0 / 2048.0),
                     ins=["ps4"], outs=["m_"])
            self.dve(lambda e: e.tensor_scalar_mul(v_, ps_q, 1.0 / 2048.0),
                     ins=["ps5"], outs=["v_"])
            self.dve(lambda e: e.scalar_tensor_tensor(
                A_, m_, 1.0, m_, OP.bypass, OP.mult), ins=["m_"], outs=["A_"])
            self.dve(lambda e: e.tensor_sub(v_, v_, A_), ins=["v_", "A_"], outs=["v_"])
            self.act(lambda e: e.activation(A_, v_, AF.Sqrt, bias=1e-5),
                     ins=["v_", "kconst"], outs=["A_"])
            self.dve(lambda e: e.reciprocal(A_, A_), ins=["A_"], outs=["A_"])
            self.dve(lambda e: e.scalar_tensor_tensor(
                B_, m_, -1.0, A_, OP.mult, OP.mult), ins=["m_", "A_"], outs=["B_"])
            psA = self.PS[6]; psB = self.PS[7]
            self.mm(psA, self.P["onesr"].ap(), A_, True, True,
                    ins=["ones_r", "A_"], outs=["ps6"])
            self.mm(psB, self.P["onesr"].ap(), B_, True, True,
                    ins=["ones_r", "B_"], outs=["ps7"])
            for t in range(16):
                xt = xb[:, 512 * t:512 * (t + 1)]
                ht = hbuf[:, 512 * t:512 * (t + 1)]
                tmp = sqt
                self.dve(lambda e, xt=xt: e.tensor_mul(tmp, xt, psA),
                         ins=[xk, "ps6"], outs=["sqt"])
                self.dve(lambda e: e.tensor_add(tmp, tmp, psB),
                         ins=["sqt", "ps7"], outs=["sqt"])
                self.dve(lambda e, t=t, ht=ht: e.tensor_scalar(
                    ht, tmp, cst[:, 68 + t:69 + t], cst[:, 52 + t:53 + t],
                    OP.mult, OP.add), ins=["sqt", "c_ca1", "c_sa"], outs=["hbuf"])
            self.dma("ad", h_sr[:, :, CH * c:CH * (c + 1)],
                     hbuf.rearrange("p (t x) -> p t x", t=16), ins=["hbuf"],
                     outs=[f"h_s{c}"])
        self.S.barrier()
        self.RBF.reset(); self.RF4.reset()

    def f1_qk(self):
        D = self.D
        cst = self.P["consts"].ap()
        self.RBF.reset(); self.RF4.reset()
        wres = self.RBF.tile(8 * 16 * 128)
        hch = [self.RBF.tile(16 * 512) for _ in range(2)]
        evac = self.RBF.tile(8 * 512)
        sqt = self.RF4.tile(512)
        h_sr = D["h_s"].rearrange("(t p) n -> p t n", p=128)
        wq_r = D["wqkvT"].rearrange("(kt kp) m -> kp kt m", kp=128)
        for half, (w_off, out_s, nrm_rows) in enumerate(
                ((0, "q_s", self.P["rows"].ap()), (2048, "k_s", self.P["rows2"].ap()))):
            out_r = D[out_s].rearrange("(t p) n -> p t n", p=128)
            for quart in range(2):
                mo = 8 * quart
                for mc in range(8):
                    co = w_off + 128 * (mo + mc)
                    dst = wres[:, 2048 * mc:2048 * (mc + 1)]
                    self.dma("gd", dst.rearrange("p (t x) -> p t x", t=16),
                             wq_r[:, :, co:co + 128], outs=["wres"])
                for c in range(NCH):
                    hb = hch[c % 2]
                    hk = f"hch{c % 2}"
                    self.dma("sd", hb.rearrange("p (t x) -> p t x", t=16),
                             h_sr[:, :, CH * c:CH * (c + 1)], outs=[hk])
                    ps_r = self.PS[4][0:16, :]
                    for m in range(8):
                        bank = self.PS[m % 4]
                        bkey = f"ps{m % 4}"
                        for k in range(16):
                            lhsT = wres[:, 2048 * m + 128 * k:2048 * m + 128 * (k + 1)]
                            self.mm(bank, lhsT, hb[:, 512 * k:512 * (k + 1)],
                                    start=(k == 0), stop=(k == 15),
                                    ins=["wres", hk], outs=[bkey])
                        et = evac[:, 512 * m:512 * (m + 1)]
                        self.act(lambda e, d=et, s=bank: e.activation(d, s, AF.Copy),
                                 ins=[bkey], outs=["evac"])
                        self.act(lambda e, s=et: e.activation(sqt, s, AF.Square),
                                 ins=["evac"], outs=["sqt"])
                        gm = mo + m
                        self.mm(ps_r, cst[:, 152 + 16 * gm:152 + 16 * (gm + 1)], sqt,
                                start=(m == 0), stop=(m == 7),
                                ins=["c_sel16", "sqt"], outs=["ps4"])
                    self.dma("ad", out_r[:, mo:mo + 8, CH * c:CH * (c + 1)],
                             evac.rearrange("p (t x) -> p t x", t=8), ins=["evac"],
                             outs=[f"{out_s}{c}_{quart}"])
                    nr = nrm_rows[:, CH * c:CH * (c + 1)]
                    if quart == 0:
                        self.dve(lambda e, nr=nr: e.tensor_copy(nr, ps_r),
                                 ins=["ps4"], outs=[f"nrm{half}"])
                    else:
                        self.dve(lambda e, nr=nr: e.tensor_add(nr, nr, ps_r),
                                 ins=["ps4", f"nrm{half}"], outs=[f"nrm{half}"])
        self.S.barrier()
        self.RBF.reset(); self.RF4.reset()

    def f1_v(self):
        D = self.D
        self.RBF.reset(); self.RF4.reset()
        wres = self.RBF.tile(8 * 16 * 128)
        hch = [self.RBF.tile(16 * 512) for _ in range(2)]
        evac = self.RBF.tile(2 * 512)
        h_sr = D["h_s"].rearrange("(t p) n -> p t n", p=128)
        wv_r = D["wqkvT"].rearrange("(kt kp) m -> kp kt m", kp=128)
        v_r = D["v_s"].rearrange("(n p) t -> p n t", p=128)
        for vh in range(2):
            for mc in range(8):
                co = 4096 + 1024 * vh + 128 * mc
                dst = wres[:, 2048 * mc:2048 * (mc + 1)]
                self.dma("gd", dst.rearrange("p (t x) -> p t x", t=16),
                         wv_r[:, :, co:co + 128], outs=["wres"])
            wres_r = wres.rearrange("p (m x) -> p m x", m=8)
            for c in range(NCH):
                hb = hch[c % 2]
                hk = f"hch{c % 2}"
                self.dma("sd", hb.rearrange("p (t x) -> p t x", t=16),
                         h_sr[:, :, CH * c:CH * (c + 1)], outs=[hk])
                for mt in range(4):
                    for nc2 in range(2):
                        bank = self.PS[nc2]
                        bkey = f"ps{nc2}"
                        for k in range(16):
                            lhsT = hb[:, 512 * k + 128 * mt:512 * k + 128 * (mt + 1)]
                            rhs = wres_r[:, 4 * nc2:4 * nc2 + 4,
                                         128 * k:128 * k + 128]
                            self.mm(bank, lhsT, rhs, start=(k == 0), stop=(k == 15),
                                    ins=["wres", hk], outs=[bkey])
                        et = evac[:, 512 * nc2:512 * (nc2 + 1)]
                        self.act(lambda e, d=et, s=bank: e.activation(d, s, AF.Copy),
                                 ins=[bkey], outs=["evac"])
                    self.dma("ad", v_r[:, 4 * c + mt, 1024 * vh:1024 * (vh + 1)],
                             evac.rearrange("p (n x) -> p n x", n=2), ins=["evac"],
                             outs=[f"v_s{c}_{mt}_{vh}"])
        self.S.barrier()
        self.RBF.reset(); self.RF4.reset()

    def f2_rope(self):
        D = self.D
        cst = self.P["consts"].ap()
        for half, (in_s, out_s, rows_t, w_a, w_b, bnd_dst) in enumerate((
                ("q_s", "qr_s", self.P["rows"].ap(), 0, 640, 0),
                ("k_s", "kr_s", self.P["rows2"].ap(), 16, 656, 8))):
            wkeys = ["c_qnw", "c_qnws"] if half == 0 else ["c_knw", "c_knws"]
            self.RBF.reset(); self.RF4.reset()
            tot = self.RF4.tile(T)[0:16, :]
            self.dve(lambda e, r=rows_t, t=tot: e.tensor_copy(t, r),
                    ins=[f"nrm{half}"], outs=["tot"])
            self.gp(lambda e, t=tot: e.partition_all_reduce(
                t, t, 16, bass.bass_isa.ReduceOp.add), ins=["tot"], outs=["tot"])
            inv = self.RF4.tile(T)[0:1, :]
            self.act(lambda e, t=tot, i=inv: e.activation(
                i, t[0:1, :], AF.Sqrt, bias=1e-6, scale=1.0 / 2048.0),
                ins=["tot", "kconst"], outs=["invr"])
            self.dve(lambda e, i=inv: e.reciprocal(i, i), ins=["invr"], outs=["invr"])
            inv2 = self.RF4.tile(T)[0:16, :]
            self.dve(lambda e, i=inv, o=inv2: e.scalar_tensor_tensor(
                o[0:1, :], i, 1.0, i, OP.bypass, OP.mult), ins=["invr"], outs=["inv2"])
            self.gp(lambda e, o=inv2: e.partition_broadcast(o, o[0:1, :], 16),
                    ins=["inv2"], outs=["inv2"])
            u = self.RF4.tile(T)[0:16, :]
            self.dve(lambda e, r=rows_t, i=inv2, u=u: e.tensor_mul(u, r, i),
                     ins=[f"nrm{half}", "inv2"], outs=["u_"])
            bnd = cst[0:16, 672 + bnd_dst:672 + bnd_dst + 8]
            self.dve(lambda e, u=u, b=bnd: e.tensor_reduce(
                b, u.rearrange("p (c x) -> p c x", c=8), AX, OP.max),
                ins=["u_"], outs=[f"bnd{half}"])
            combC = self.RBF.tile(T)
            combS = self.RBF.tile(T)
            rtile = self.RF4.tile(512)
            for c in range(NCH):
                ps = self.PS[6]
                self.mm(ps, self.P["onesr"].ap(), inv[:, CH * c:CH * (c + 1)],
                        True, True, ins=["ones_r", "invr"], outs=["ps6"])
                for tbl, comb, keyo in (("ropeC", combC, "combC"), ("ropeS", combS, "combS")):
                    self.dma("sd", rtile, D[tbl][:, CH * c:CH * (c + 1)], outs=["rtile"])
                    self.dve(lambda e, co=comb, c=c, rt=rtile, ps=ps: e.tensor_mul(
                        co[:, CH * c:CH * (c + 1)], rt, ps),
                        ins=["rtile", "ps6"], outs=[keyo])
            in_r = D[in_s].rearrange("(t p) n -> p t n", p=128)
            out_r = D[out_s].rearrange("(t p) n -> p t n", p=128)
            qa = self.RBF.tile(T)
            qb = self.RBF.tile(T)
            t1 = self.RBF.tile(T)
            t2 = self.RBF.tile(T)
            for t in range(16):
                self.dma("sd", qa, in_r[:, t, :], outs=["qa"])
                self.dma("sd", qb[0:64, :], D[in_s].rearrange(
                    "(t p) n -> t p n", t=16)[t, 64:128, :], outs=["qb"])
                self.dma("sd", qb[64:128, :], D[in_s].rearrange(
                    "(t p) n -> t p n", t=16)[t, 0:64, :], outs=["qb"])
                wa = cst[:, w_a + t:w_a + t + 1]
                wb = cst[:, w_b + t:w_b + t + 1]
                self.dve(lambda e, wa=wa: e.scalar_tensor_tensor(
                    t1, qa, wa, combC, OP.mult, OP.mult),
                    ins=["qa", "combC", wkeys[0]], outs=["t1", "t1f"])
                self.dve(lambda e, wb=wb: e.scalar_tensor_tensor(
                    t2, qb, wb, combS, OP.mult, OP.mult),
                    ins=["qb", "combS", wkeys[1]], outs=["t2"])
                self.dve(lambda e: e.tensor_add(t1, t1, t2),
                         ins=["t1", "t2"], outs=["t1f"])
                self.dma("ad", out_r[:, t, :], t1, ins=["t1f"], outs=[f"{out_s}{t}"])
        self.S.barrier()
        self.RBF.reset(); self.RF4.reset()

    def f2_attn(self):
        D = self.D
        cst = self.P["consts"].ap()
        for hg in range(2):
            self.RBF.reset(); self.RF4.reset()
            kwin = [self.RBF.tile(8 * 512) for _ in range(3)]
            vwin = [self.RBF.tile(8 * 512) for _ in range(3)]
            qch = self.RBF.tile(8 * 512)
            invb = self.RBF.tile(128)
            pt = self.RBF.tile(9 * 128)
            och = self.RBF.tile(8 * 512)
            sinv = self.RF4.tile(128)[0:1, :]
            mbias = self.RF4.tile(128)
            kr_r = D["kr_s"].rearrange("(t p) n -> p t n", p=128)
            v_r = D["v_s"].rearrange("(n p) t -> p n t", p=128)
            qr_r = D["qr_s"].rearrange("(t p) n -> p t n", p=128)
            bq = cst[0:16, 672:680]
            bk = cst[0:16, 680:688]
            wq2 = self.RF4.tile(8)[0:16, 0:1]
            wk2 = self.RF4.tile(8)[0:16, 0:1]
            if hg == 0:
                for wsl, dst, key in ((0, wq2, "wq2"), (16, wk2, "wk2")):
                    tmp = self.RF4.tile(16)
                    self.gp(lambda e, a=wsl, t=tmp: e.partition_all_reduce(
                        t, cst[:, a:a + 16], 128, bass.bass_isa.ReduceOp.absmax),
                        ins=["c_qnw", "c_knw"], outs=["tmpw"])
                    ps = self.PS[6][0:16, 0:1]
                    self.mm(ps, tmp[0:1, :], self.P["onesr"].ap()[0:1, 0:1],
                            True, True, ins=["tmpw", "ones_r"], outs=["ps6"])
                    self.act(lambda e, d=dst, s=ps: e.activation(d, s, AF.Square),
                             ins=["ps6"], outs=[key])
                bk3 = self.RF4.tile(8)[0:16, :]
                self.dve(lambda e, b=bk3: e.tensor_copy(b, bk), ins=["bnd1"], outs=["bk3"])
                for sh in (1, 2):
                    self.dve(lambda e, b=bk3, sh=sh: e.tensor_tensor(
                        b[:, sh:8], b[:, sh:8], bk[:, 0:8 - sh], OP.max),
                        ins=["bk3", "bnd1"], outs=["bk3"])
                prod = self.RF4.tile(8)[0:16, :]
                self.dve(lambda e, p=prod, b=bk3: e.tensor_mul(p, bq, b),
                         ins=["bnd0", "bk3"], outs=["prod"])
                self.dve(lambda e, p=prod: e.tensor_scalar(
                    p, p, wq2, None, OP.mult), ins=["prod", "wq2"], outs=["prod"])
                self.dve(lambda e, p=prod: e.tensor_scalar(
                    p, p, wk2, None, OP.mult), ins=["prod", "wk2"], outs=["prod"])
                self.act(lambda e, p=prod: e.activation(
                    p, p, AF.Sqrt, scale=ATT_SCALE * ATT_SCALE),
                    ins=["prod"], outs=["prod"])
                self.dve(lambda e, p=prod: e.tensor_scalar(
                    p, p, 100.0, None, OP.min), ins=["prod"], outs=["prod"])
                self.dve(lambda e, p=prod: e.tensor_scalar(
                    p, p, -1.0, 44.0, OP.mult, OP.add), ins=["prod"], outs=["prod"])
                mb = cst[0:16, 688:696]
                self.dve(lambda e, p=prod, m=mb: e.tensor_copy(m, p),
                         ins=["prod"], outs=["mb"])
                self.dma("sd", self.P["mrow"].ap()[0:1, 1024:1152],
                         mb, ins=["mb"], outs=["mbrow"])
            mbr = self.P["mrow"].ap()
            for h8 in range(8):
                h = 8 * hg + h8
                for c in range(NCH):
                    self.gp(lambda e, h=h, c=c, h8=h8: e.partition_broadcast(
                        mbias[:, 8 * h8 + c:8 * h8 + c + 1],
                        mbr[0:1, 1024 + 8 * h + c:1025 + 8 * h + c], 128),
                        ins=["mbrow"], outs=["mbias"])
            for c in range(NCH):
                sl = c % 3
                self.dma("sd", kwin[sl].rearrange("p (t x) -> p t x", t=8),
                         kr_r[:, 8 * hg:8 * hg + 8, CH * c:CH * (c + 1)],
                         outs=[f"kw{sl}"])
                self.dma("sd", vwin[sl].rearrange("p (n x) -> p n x", n=4),
                         v_r[:, 4 * c:4 * c + 4, 1024 * hg:1024 * (hg + 1)],
                         outs=[f"vw{sl}"])
                self.dma("sd", qch.rearrange("p (t x) -> p t x", t=8),
                         qr_r[:, 8 * hg:8 * hg + 8, CH * c:CH * (c + 1)],
                         outs=["qch"])
                for h8 in range(8):
                    h = 8 * hg + h8
                    for qt in range(4):
                        q0 = CH * c + 128 * qt
                        j0 = max(0, (WIN - q0) // 128)
                        njt = 9 - j0
                        qap = qch[:, 512 * h8 + 128 * qt:512 * h8 + 128 * (qt + 1)]
                        for jj in range(j0, 9):
                            kpos = q0 - WIN + 128 * jj
                            ksl = (kpos // CH) % 3
                            koff = kpos % CH
                            kap = kwin[ksl][:, 512 * h8 + koff:512 * h8 + koff + 128]
                            bi = (jj - j0) // 4
                            bo = ((jj - j0) % 4) * 128
                            bank = self.PS[bi]
                            self.mm(bank[:, bo:bo + 128], kap, qap,
                                    start=(bo == 0), stop=True,
                                    ins=[f"kw{ksl}", "qch"], outs=[f"ps{bi}"])
                        nb = cdiv(njt, 4)
                        for bi in range(nb):
                            w = min(4, njt - 4 * bi) * 128
                            bank = self.PS[bi]
                            self.act(lambda e, bi=bi, w=w, h8=h8, c=c, bank=bank:
                                     e.activation(
                                         pt[:, 512 * bi:512 * bi + w],
                                         bank[:, 0:w], AF.Exp,
                                         bias=mbias[:, 8 * h8 + c:8 * h8 + c + 1],
                                         scale=ATT_SCALE),
                                     ins=[f"ps{bi}", "mbias"], outs=["pt"])
                        if q0 >= WIN:
                            self.dve(lambda e: e.tensor_mul(
                                pt[:, 0:128], pt[:, 0:128], self.P["maskE"].ap()),
                                ins=["pt", "maskE"], outs=["pt"])
                        dof = 128 * (njt - 1)
                        self.dve(lambda e, dof=dof: e.tensor_mul(
                            pt[:, dof:dof + 128], pt[:, dof:dof + 128],
                            self.P["maskD"].ap()), ins=["pt", "maskD"], outs=["pt"])
                        srow_ps = self.PS[3][0:1, 0:128]
                        for jj in range(j0, 9):
                            self.mm(srow_ps, self.P["onesb"].ap(),
                                    pt[:, 128 * (jj - j0):128 * (jj - j0 + 1)],
                                    start=(jj == j0), stop=(jj == 8),
                                    ins=["ones_b", "pt"], outs=["ps3r"])
                        self.dve(lambda e, s=srow_ps: e.reciprocal(sinv, s),
                                 ins=["ps3r"], outs=["sinv"])
                        ob = self.PS[4][:, 0:128]
                        for jj in range(j0, 9):
                            kpos = q0 - WIN + 128 * jj
                            ksl = (kpos // CH) % 3
                            ktt = (kpos % CH) // 128
                            vap = vwin[ksl][:, 1024 * ktt + 128 * h8:
                                            1024 * ktt + 128 * (h8 + 1)]
                            self.mm(ob, vap, pt[:, 128 * (jj - j0):128 * (jj - j0 + 1)],
                                    start=(jj == j0), stop=(jj == 8),
                                    ins=[f"vw{ksl}", "pt"], outs=["ps4"])
                        bb = self.PS[5][:, 0:128]
                        self.mm(bb, self.P["onesr"].ap(), sinv, True, True,
                                ins=["ones_r", "sinv"], outs=["ps5"])
                        self.act(lambda e: e.activation(invb, bb, AF.Copy),
                                 ins=["ps5"], outs=["invb128"])
                        oap = och[:, 512 * h8 + 128 * qt:512 * h8 + 128 * (qt + 1)]
                        self.dve(lambda e, o=oap: e.tensor_mul(o, ob, invb),
                                 ins=["ps4", "invb128"], outs=["och"])
                self.dma("ad", D["o_s"].rearrange("(t p) n -> p t n", p=128)[
                    :, 8 * hg:8 * hg + 8, CH * c:CH * (c + 1)],
                    och.rearrange("p (t x) -> p t x", t=8),
                    ins=["och"], outs=[f"o_s{hg}_{c}"])
            self.S.barrier()
        self.S.barrier()
        self.RBF.reset(); self.RF4.reset()

    def f2_oproj(self):
        D = self.D
        cst = self.P["consts"].ap()
        self.RBF.reset(); self.RF4.reset()
        och = [self.RBF.tile(16 * 512) for _ in range(2)]
        wocol = [self.RBF.tile(2048) for _ in range(2)]
        aoch = self.RBF.tile(16 * 512)
        xmch = self.RBF.tile(16 * 512)
        aof = self.RF4.tile(16 * 512)
        sqt = self.RF4.tile(512)
        o_r = D["o_s"].rearrange("(t p) n -> p t n", p=128)
        wo_r = D["woT"].rearrange("(kt kp) m -> kp kt m", kp=128)
        ao_r = D["ao_s"].rearrange("(t p) n -> p t n", p=128)
        xm_r = D["xm_s"].rearrange("(t p) n -> p t n", p=128)
        mr = self.P["mrow"].ap()
        for c in range(NCH):
            ob = och[c % 2]
            okey = f"och{c % 2}"
            self.dma("sd", ob.rearrange("p (t x) -> p t x", t=16),
                     o_r[:, :, CH * c:CH * (c + 1)], outs=[okey])
            ps_r = self.PS[5][0:1, :]
            for m in range(16):
                wc = wocol[m % 2]
                wkey = f"woc{m % 2}"
                self.dma("gd", wc.rearrange("p (t x) -> p t x", t=16),
                         wo_r[:, :, 128 * m:128 * (m + 1)], outs=[wkey])
                bank = self.PS[m % 4]
                bkey = f"ps{m % 4}"
                for k in range(16):
                    self.mm(bank, wc[:, 128 * k:128 * (k + 1)],
                            ob[:, 512 * k:512 * (k + 1)],
                            start=(k == 0), stop=(k == 15),
                            ins=[wkey, okey], outs=[bkey])
                at = aof[:, 512 * m:512 * (m + 1)]
                self.dve(lambda e, d=at, s=bank: e.tensor_copy(d, s),
                         ins=[bkey], outs=["aof"])
                self.act(lambda e, s=at: e.activation(sqt, s, AF.Square),
                         ins=["aof"], outs=["sqt"])
                self.mm(ps_r, cst[:, 151:152], sqt, start=(m == 0), stop=(m == 15),
                        ins=["ones_c", "sqt"], outs=["ps5"])
            iv = mr[:, 0:512]
            self.act(lambda e: e.activation(iv, ps_r, AF.Sqrt, bias=1e-6,
                                            scale=1.0 / 2048.0),
                     ins=["ps5", "kconst"], outs=["iv"])
            self.dve(lambda e: e.reciprocal(iv, iv), ins=["iv"], outs=["iv"])
            bb = self.PS[6]
            self.mm(bb, self.P["onesr"].ap(), iv, True, True,
                    ins=["ones_r", "iv"], outs=["ps6"])
            for m in range(16):
                at = aof[:, 512 * m:512 * (m + 1)]
                ac = aoch[:, 512 * m:512 * (m + 1)]
                xc = xmch[:, 512 * m:512 * (m + 1)]
                self.dve(lambda e, d=ac, s=at: e.tensor_copy(d, s),
                         ins=["aof"], outs=["aoch"])
                self.dve(lambda e, x=xc, s=at, m=m: e.scalar_tensor_tensor(
                    x, s, cst[:, 32 + m:33 + m], bb, OP.mult, OP.mult),
                    ins=["aof", "c_mnw", "ps6"], outs=["xmch"])
            self.dma("ad", ao_r[:, :, CH * c:CH * (c + 1)],
                     aoch.rearrange("p (t x) -> p t x", t=16),
                     ins=["aoch"], outs=[f"ao{c}"])
            self.dma("ad", xm_r[:, :, CH * c:CH * (c + 1)],
                     xmch.rearrange("p (t x) -> p t x", t=16),
                     ins=["xmch"], outs=[f"xm{c}"])
        self.S.barrier()
        self.RBF.reset(); self.RF4.reset()

    def f3_mlpqkv(self):
        D = self.D
        cst = self.P["consts"].ap()
        self.RBF.reset(); self.RF4.reset()
        wres = self.RBF.tile(8 * 16 * 128)
        xch = [self.RBF.tile(16 * 512) for _ in range(2)]
        evac = self.RBF.tile(8 * 512)
        sqt = self.RF4.tile(512)
        nrm = self.RF4.tile(512)
        nrm2 = self.RF4.tile(512)
        nrk = self.RF4.tile(8)
        xm_r = D["xm_s"].rearrange("(t p) n -> p t n", p=128)
        w_r = D["wmlpT"].rearrange("(kt kp) m -> kp kt m", kp=128)
        fq_r = D["fq_s"].rearrange("(t p) n -> p t n", p=128)
        for quart in range(2):
            mo = 8 * quart
            for mc in range(8):
                co = 128 * (mo + mc)
                dst = wres[:, 2048 * mc:2048 * (mc + 1)]
                self.dma("gd", dst.rearrange("p (t x) -> p t x", t=16),
                         w_r[:, :, co:co + 128], outs=["wres"])
            for c in range(NCH):
                xb = xch[c % 2]
                xk = f"xch{c % 2}"
                self.dma("sd", xb.rearrange("p (t x) -> p t x", t=16),
                         xm_r[:, :, CH * c:CH * (c + 1)], outs=[xk])
                for hl in range(2):
                    ps_r = self.PS[4][0:16, :]
                    for mt in range(4):
                        m = 4 * hl + mt
                        bank = self.PS[mt % 4]
                        bkey = f"ps{mt % 4}"
                        for k in range(16):
                            lhsT = wres[:, 2048 * m + 128 * k:2048 * m + 128 * (k + 1)]
                            self.mm(bank, lhsT, xb[:, 512 * k:512 * (k + 1)],
                                    start=(k == 0), stop=(k == 15),
                                    ins=["wres", xk], outs=[bkey])
                        et = evac[:, 512 * m:512 * (m + 1)]
                        self.act(lambda e, d=et, s=bank: e.activation(d, s, AF.Silu),
                                 ins=[bkey], outs=["evac"])
                        self.act(lambda e, s=et: e.activation(sqt, s, AF.Square),
                                 ins=["evac"], outs=["sqt"])
                        self.mm(ps_r, cst[:, 152 + 16 * mt:152 + 16 * (mt + 1)], sqt,
                                start=(mt == 0), stop=(mt == 3),
                                ins=["c_sel16", "sqt"], outs=["ps4"])
                    nr = nrm[0:4, :]
                    self.dve(lambda e, nr=nr: e.tensor_copy(nr, self.PS[4][0:4, :]),
                             ins=["ps4"], outs=["nr"])
                    self.gp(lambda e, nr=nr: e.partition_all_reduce(
                        nr, nr, 4, bass_isa.ReduceOp.add), ins=["nr"], outs=["nr"])
                    iv = nrm2[0:1, :]
                    self.act(lambda e, nr=nr, iv=iv: e.activation(
                        iv, nr[0:1, :], AF.Sqrt), ins=["nr"], outs=["iv2"])
                    self.dve(lambda e, iv=iv: e.tensor_scalar_max(iv, iv, 1e-6),
                             ins=["iv2"], outs=["iv2"])
                    self.dve(lambda e, iv=iv: e.reciprocal(iv, iv),
                             ins=["iv2"], outs=["iv2"])
                    bb = self.PS[5]
                    self.mm(bb, self.P["onesr"].ap(), iv, True, True,
                            ins=["ones_r", "iv2"], outs=["ps5"])
                    for mt in range(4):
                        m = 4 * hl + mt
                        et = evac[:, 512 * m:512 * (m + 1)]
                        self.dve(lambda e, et=et: e.tensor_mul(et, et, bb),
                                 ins=["evac", "ps5"], outs=["evac"])
                self.dma("ad", fq_r[:, mo:mo + 8, CH * c:CH * (c + 1)],
                         evac.rearrange("p (t x) -> p t x", t=8), ins=["evac"],
                         outs=[f"fq{c}_{quart}"])
        wres2 = wres
        fk_r = D["fk_s"].rearrange("(n p) t -> p n t", p=128)
        fv_r = D["fv_s"].rearrange("(n p) t -> p n t", p=128)
        evac2 = evac
        for out_s, woff, outr in (("fk_s", 2048, fk_r), ("fv_s", 4096, fv_r)):
            for vh in range(2):
                for mc in range(8):
                    co = woff + 1024 * vh + 128 * mc
                    dst = wres2[:, 2048 * mc:2048 * (mc + 1)]
                    self.dma("gd", dst.rearrange("p (t x) -> p t x", t=16),
                             w_r[:, :, co:co + 128], outs=["wres"])
                wrr = wres2.rearrange("p (m x) -> p m x", m=8)
                for c in range(NCH):
                    xb = xch[c % 2]
                    xk = f"xch{c % 2}"
                    self.dma("sd", xb.rearrange("p (t x) -> p t x", t=16),
                             xm_r[:, :, CH * c:CH * (c + 1)], outs=[xk])
                    for mt in range(4):
                        for nc2 in range(2):
                            bank = self.PS[nc2]
                            bkey = f"ps{nc2}"
                            for k in range(16):
                                lhsT = xb[:, 512 * k + 128 * mt:512 * k + 128 * (mt + 1)]
                                rhs = wrr[:, 4 * nc2:4 * nc2 + 4, 128 * k:128 * k + 128]
                                self.mm(bank, lhsT, rhs, start=(k == 0), stop=(k == 15),
                                        ins=["wres", xk], outs=[bkey])
                            et = evac2[:, 512 * nc2:512 * (nc2 + 1)]
                            self.act(lambda e, d=et, s=bank: e.activation(d, s, AF.Silu),
                                     ins=[bkey], outs=["evac"])
                        if out_s == "fk_s":
                            sq2 = nrm2
                            self.act(lambda e: e.activation(nrm, evac2[:, 0:512],
                                                            AF.Square),
                                     ins=["evac"], outs=["sqa"])
                            self.act(lambda e: e.activation(nrm2, evac2[:, 512:1024],
                                                            AF.Square),
                                     ins=["evac"], outs=["sqb"])
                            nr = nrk[:, 0:2]
                            self.dve(lambda e, nr=nr: e.tensor_reduce(
                                nr[:, 0:1], nrm, AX, OP.add),
                                ins=["sqa"], outs=["nr2"])
                            self.dve(lambda e, nr=nr: e.tensor_reduce(
                                nr[:, 1:2], nrm2, AX, OP.add),
                                ins=["sqb"], outs=["nr2"])
                            self.act(lambda e, nr=nr: e.activation(nr, nr, AF.Sqrt),
                                     ins=["nr2"], outs=["nr2"])
                            self.dve(lambda e, nr=nr: e.tensor_scalar_max(
                                nr, nr, 1e-6), ins=["nr2"], outs=["nr2"])
                            self.dve(lambda e, nr=nr: e.reciprocal(nr, nr),
                                     ins=["nr2"], outs=["nr2"])
                            for bq2 in range(2):
                                self.dve(lambda e, b=bq2, nr=nr: e.tensor_scalar(
                                    evac2[:, 512 * b:512 * (b + 1)],
                                    evac2[:, 512 * b:512 * (b + 1)],
                                    nr[:, b:b + 1], None, OP.mult),
                                    ins=["evac", "nr2"], outs=["evac"])
                        self.dma("ad", outr[:, 4 * c + mt, 1024 * vh:1024 * (vh + 1)],
                                 evac2[:, 0:1024].rearrange("p (n x) -> p n x", n=2),
                                 ins=["evac"], outs=[f"{out_s}{c}_{mt}_{vh}"])
        lrt = self.P["lr_t"].ap()
        wlr = self.RBF.tile(16 * 12)
        self.dma("gd", wlr.rearrange("p (t x) -> p t x", t=16),
                 D["wlrT"].rearrange("(kt kp) m -> kp kt m", kp=128),
                 outs=["wlr"])
        for c in range(NCH):
            xb = xch[c % 2]
            xk = f"xch{c % 2}"
            self.dma("sd", xb.rearrange("p (t x) -> p t x", t=16),
                     xm_r[:, :, CH * c:CH * (c + 1)], outs=[xk])
            for mt in range(4):
                bank = self.PS[6][:, 0:12]
                for k in range(16):
                    lhsT = xb[:, 512 * k + 128 * mt:512 * k + 128 * (mt + 1)]
                    self.mm(bank, lhsT, wlr[:, 12 * k:12 * (k + 1)],
                            start=(k == 0), stop=(k == 15),
                            ins=[xk, "wlr"], outs=["ps6"])
                ti = 12 * (4 * c + mt)
                lap = lrt[:, ti:ti + 12]
                self.dve(lambda e, lap=lap, bank=bank: e.tensor_scalar(
                    lap, bank, float(BASE_LR_INV), None, OP.add),
                    ins=["ps6", "kconst"], outs=["lrt"])
                self.dve(lambda e, lap=lap: e.tensor_add(
                    lap, lap, cst[:, 472:484]), ins=["lrt", "lrbb"], outs=["lrt"])
                self.act(lambda e, lap=lap: e.activation(lap, lap, AF.Exp),
                         ins=["lrt"], outs=["lrt"])
                self.act(lambda e, lap=lap: e.activation(lap, lap, AF.Ln, bias=1.0),
                         ins=["lrt", "kconst"], outs=["lrt"])
        self.S.barrier()
        self.RBF.reset(); self.RF4.reset()

    def f4_ttt(self):
        D = self.D
        self.RBF.reset(); self.RF4.reset()
        M0 = self.RF4.tile(2048); M2 = self.RF4.tile(2048)
        M1o = self.RF4.tile(2048); M1e = self.RF4.tile(2048)
        B0 = self.RBF.tile(2048); B2 = self.RBF.tile(2048)
        B1o = self.RBF.tile(2048); B1e = self.RBF.tile(2048)
        qcf = self.RBF.tile(2048)
        kcf = self.RBF.tile(2048); kct = self.RBF.tile(2048)
        vcf = self.RBF.tile(2048); vct = self.RBF.tile(2048)
        gk = self.RBF.tile(2048); hbm = self.RBF.tile(2048)
        dhid = self.RBF.tile(2048); sg = self.RBF.tile(2048)
        u_ = self.RBF.tile(2048); hid = self.RBF.tile(2048)
        dgp = self.RBF.tile(2048); dhb = self.RBF.tile(2048)
        kl0 = self.RBF.tile(2048); kl2 = self.RBF.tile(2048)
        vl1 = self.RBF.tile(2048)
        gq = gk; t2q = hbm
        hq = dhid
        ocb = self.RBF.tile(2048)
        sqt = self.RF4.tile(512)
        nrm = self.RF4.tile(512)
        lrt = self.P["lr_t"].ap()
        fq_r = D["fq_s"].rearrange("(t p) n -> p t n", p=128)
        fk_r = D["fk_s"].rearrange("(n p) t -> p n t", p=128)
        fv_r = D["fv_s"].rearrange("(n p) t -> p n t", p=128)
        oc_r = D["oc_s"].rearrange("(t p) n -> p t n", p=128)
        cst = self.P["consts"].ap()

        def load_state(h):
            for nm, dst, key in (("w0T", M0, "M0"), ("w2T", M2, "M2"),
                                 ("w1o", M1o, "M1o"), ("w1e", M1e, "M1e")):
                self.dma("sd", dst.rearrange("p (t x) -> p t x", t=4),
                         D[nm].rearrange("h (t p) x -> h p t x", p=128)[h],
                         outs=[key])

        def prod16(outbank_fn, lhsT_tile, rhs_tile, lkey, rkey, okey, fold=None):
            for mt in range(4):
                bank, bkey = outbank_fn(mt)
                for kt in range(4):
                    lhsT = lhsT_tile[:, 512 * kt + 128 * mt:512 * kt + 128 * (mt + 1)]
                    rhs = rhs_tile[:, 512 * kt:512 * (kt + 1)]
                    self.mm(bank, lhsT, rhs, start=(kt == 0), stop=(kt == 3),
                            ins=[lkey, rkey], outs=[bkey])
                if fold is not None:
                    fold(mt, bank, bkey)

        for h in range(NFH):
            load_state(h)
            for nm, mst, bw, km, kb in (("M0", M0, B0, "M0", "B0"),
                                        ("M2", M2, B2, "M2", "B2"),
                                        ("M1o", M1o, B1o, "M1o", "B1o"),
                                        ("M1e", M1e, B1e, "M1e", "B1e")):
                self.dve(lambda e, d=bw, s=mst: e.tensor_copy(d, s),
                         ins=[km], outs=[kb])
            for c in range(NCH):
                self.dma("sd", qcf.rearrange("p (t x) -> p t x", t=4),
                         fq_r[:, 4 * h:4 * h + 4, CH * c:CH * (c + 1)], outs=["qcf"])
                self.dma("sd", kct.rearrange("p (n x) -> p n x", n=4),
                         fk_r[:, 4 * c:4 * c + 4, FHD * h:FHD * (h + 1)], outs=["kct"])
                self.dma("sd", vct.rearrange("p (n x) -> p n x", n=4),
                         fv_r[:, 4 * c:4 * c + 4, FHD * h:FHD * (h + 1)], outs=["vct"])
                import os as _os
                if "tdma" in _os.environ.get("KSKIP", ""):
                    self.dma("sd", kcf.rearrange("p (t x) -> p t x", t=4),
                             fq_r[:, 4 * h:4 * h + 4, CH * c:CH * (c + 1)], outs=["kcf"])
                    self.dma("sd", vcf.rearrange("p (t x) -> p t x", t=4),
                             fq_r[:, 4 * h:4 * h + 4, CH * c:CH * (c + 1)], outs=["vcf"])
                else:
                 for dt4 in range(4):
                    self.dma("sd", kcf[:, 512 * dt4:512 * (dt4 + 1)],
                             D["fk_s"][CH * c:CH * (c + 1),
                                       FHD * h + 128 * dt4:FHD * h + 128 * (dt4 + 1)],
                             outs=["kcf"], transpose=True)
                    self.dma("sd", vcf[:, 512 * dt4:512 * (dt4 + 1)],
                             D["fv_s"][CH * c:CH * (c + 1),
                                       FHD * h + 128 * dt4:FHD * h + 128 * (dt4 + 1)],
                             outs=["vcf"], transpose=True)

                import os as _os
                _SK = _os.environ.get("KSKIP", "")
                do_upd = (c < NCH - 1) and ("upd" not in _SK)
                if "qp" in _SK and "keyp" in _SK:
                    continue
                def foldq(dst, dkey):
                    def f(mt, bank, bkey):
                        et = dst[:, 512 * mt:512 * (mt + 1)]
                        self.dve(lambda e, d=et, s=bank: e.tensor_copy(d, s),
                                 ins=[bkey], outs=[dkey])
                    return f
                prod16(lambda mt: (self.PS[mt], f"ps{mt}"), B0, qcf,
                       "B0", "qcf", None, foldq(gq, "gk"))
                if "t2" not in _SK:
                    prod16(lambda mt: (self.PS[4 + (mt % 2)], f"ps{4 + (mt % 2)}"),
                           B2, qcf, "B2", "qcf", None, foldq(t2q, "hbm"))
                for mt in range(4) if "t2" not in _SK else []:
                    a = gq[:, 512 * mt:512 * (mt + 1)]
                    b = t2q[:, 512 * mt:512 * (mt + 1)]
                    d = hq[:, 512 * mt:512 * (mt + 1)]
                    self.act(lambda e, a=a, d=d: e.activation(d, a, AF.Silu),
                             ins=["gk"], outs=["dhid"])
                    self.dve(lambda e, d=d, b=b: e.tensor_mul(d, d, b),
                             ins=["dhid", "hbm"], outs=["dhid"])
                if "oc" in _SK:
                    continue
                def foldoc(mt, bank, bkey):
                    et = ocb[:, 512 * mt:512 * (mt + 1)]
                    self.dve(lambda e, d=et, s=bank: e.tensor_copy(d, s),
                             ins=[bkey], outs=["ocb"])
                    if ("rms" not in _SK) and ("rmsA" not in _SK):
                        self.act(lambda e, s=et: e.activation(sqt, s, AF.Square),
                                 ins=["ocb"], outs=["sqt"])
                        self.mm(self.PS[6][0:16, :], cst[:, 152 + 16 * mt:152 + 16 * (mt + 1)],
                                sqt, start=(mt == 0), stop=(mt == 3),
                                ins=["sqt", "c_sel16"], outs=["ps6"])
                prod16(lambda mt: (self.PS[mt], f"ps{mt}"), B1e, hq,
                       "B1e", "dhid", None, foldoc)
                if "rms" in _SK:
                    self.dma("ad", oc_r[:, 4 * h:4 * h + 4, CH * c:CH * (c + 1)],
                             ocb.rearrange("p (t x) -> p t x", t=4), ins=["ocb"],
                             outs=[f"oc{h}_{c}"])
                    continue
                nr = nrm[0:16, :]
                self.dve(lambda e, nr=nr: e.tensor_copy(nr, self.PS[6][0:16, :]),
                         ins=["ps6"], outs=["nrf"])
                self.gp(lambda e, nr=nr: e.partition_all_reduce(
                    nr[0:4, :], nr[0:4, :], 4, bass_isa.ReduceOp.add),
                    ins=["nrf"], outs=["nrf"])
                iv = nrm[0:1, :]
                self.act(lambda e, nr=nr, iv=iv: e.activation(
                    iv, nr[0:1, :], AF.Sqrt, bias=1e-6, scale=1.0 / 512.0),
                    ins=["nrf", "kconst"], outs=["ivt"])
                self.dve(lambda e, iv=iv: e.reciprocal(iv, iv),
                         ins=["ivt"], outs=["ivt"])
                bb = self.PS[7]
                self.mm(bb, self.P["onesr"].ap(), iv, True, True,
                        ins=["ones_r", "ivt"], outs=["ps7"])
                for mt in range(4):
                    et = ocb[:, 512 * mt:512 * (mt + 1)]
                    self.dve(lambda e, et=et, mt=mt: e.scalar_tensor_tensor(
                        et, et, cst[:, 48 + mt:49 + mt], bb, OP.mult, OP.mult),
                        ins=["ocb", "c_tnw", "ps7"], outs=["ocb"])
                self.dma("ad", oc_r[:, 4 * h:4 * h + 4, CH * c:CH * (c + 1)],
                         ocb.rearrange("p (t x) -> p t x", t=4), ins=["ocb"],
                         outs=[f"oc{h}_{c}"])
                if "keyp" in _SK:
                    continue
                def foldtm(dst, dkey, actf=None):
                    def f(mt, bank, bkey):
                        et = dst[:, 512 * mt:512 * (mt + 1)]
                        self.dve(lambda e, d=et, s=bank: e.tensor_copy(d, s),
                                 ins=[bkey], outs=[dkey])
                    return f
                prod16(lambda mt: (self.PS[mt], f"ps{mt}"), kcf, B0,
                       "kcf", "B0", None, foldtm(gk, "gk"))
                prod16(lambda mt: (self.PS[4 + (mt % 2)], f"ps{4 + (mt % 2)}"),
                       kcf, B2, "kcf", "B2", None, foldtm(hbm, "hbm"))
                prod16(lambda mt: (self.PS[6 + (mt % 2)], f"ps{6 + (mt % 2)}"),
                       vcf, B1o, "vcf", "B1o", None, foldtm(dhid, "dhid"))
                self.act(lambda e: e.activation(sg, gk, AF.Sigmoid),
                         ins=["gk"], outs=["sg"])
                self.dve(lambda e: e.tensor_mul(u_, gk, sg), ins=["gk", "sg"],
                         outs=["u_t"])
                self.dve(lambda e: e.tensor_mul(hid, u_, hbm), ins=["u_t", "hbm"],
                         outs=["hid"])
                self.dve(lambda e: e.tensor_mul(dgp, dhid, hbm), ins=["dhid", "hbm"],
                         outs=["dgp"])
                self.dve(lambda e: e.tensor_mul(dgp, dgp, sg), ins=["dgp", "sg"],
                         outs=["dgp"])
                self.dve(lambda e: e.tensor_sub(dhb, gk, u_), ins=["gk", "u_t"],
                         outs=["dhb"])
                self.dve(lambda e: e.scalar_tensor_tensor(
                    dhb, dhb, 1.0, dgp, OP.bypass, OP.mult), ins=["dhb", "dgp"],
                    outs=["dhb"])
                self.dve(lambda e: e.tensor_add(dgp, dgp, dhb), ins=["dgp", "dhb"],
                         outs=["dgp"])
                self.dve(lambda e: e.tensor_mul(dhb, dhid, u_), ins=["dhid", "u_t"],
                         outs=["dhb"])
                if do_upd:
                    for mt in range(4):
                        ti = 12 * (4 * c + mt)
                        for src, dst, lri, sk, dk in (
                                (kct, kl0, 0, "kct", "kl0"),
                                (kct, kl2, 8, "kct", "kl2"),
                                (vct, vl1, 4, "vct", "vl1")):
                            a = src[:, 512 * mt:512 * (mt + 1)]
                            d = dst[:, 512 * mt:512 * (mt + 1)]
                            lap = lrt[:, ti + lri + h:ti + lri + h + 1]
                            self.dve(lambda e, a=a, d=d, lap=lap: e.tensor_scalar(
                                d, a, lap, None, OP.mult),
                                ins=[sk, "lrt"], outs=[dk])
                    def foldupd(mst, bw, km, kb):
                        def f(mt, bank, bkey):
                            ms = mst[:, 512 * mt:512 * (mt + 1)]
                            bs = bw[:, 512 * mt:512 * (mt + 1)]
                            self.dve(lambda e, ms=ms, s=bank: e.tensor_add(ms, ms, s),
                                     ins=[bkey, km], outs=[km])
                            self.dve(lambda e, bs=bs, ms=ms: e.tensor_copy(bs, ms),
                                     ins=[km], outs=[kb])
                        return f
                    prod16(lambda mt: (self.PS[mt], f"ps{mt}"), kl0, dgp,
                           "kl0", "dgp", None, foldupd(M0, B0, "M0", "B0"))
                    prod16(lambda mt: (self.PS[4 + (mt % 2)], f"ps{4 + (mt % 2)}"),
                           kl2, dhb, "kl2", "dhb", None, foldupd(M2, B2, "M2", "B2"))
                    prod16(lambda mt: (self.PS[mt], f"ps{mt}"), vl1, hid,
                           "vl1", "hid", None, foldupd(M1o, B1o, "M1o", "B1o"))
                    prod16(lambda mt: (self.PS[4 + (mt % 2)], f"ps{4 + (mt % 2)}"),
                           hid, vl1, "hid", "vl1", None, foldupd(M1e, B1e, "M1e", "B1e"))
        self.S.barrier()
        self.RBF.reset(); self.RF4.reset()

    def emit_quant(self, x_ap, out_ap, keyx, keyo, W, qt):
        cst = self.P["consts"].ap()
        vt, dqT = qt["vt"], qt["dqT"]
        red, scn, inv = qt["red"], qt["scn"], qt["inv"]
        r_, st_, y_, w_ = qt["r"], qt["st"], qt["y"], qt["w"]
        it = qt["it"]
        B = W // 32
        def b3(ap):
            return ap[:, 0:W].rearrange("p (b x) -> p b x", x=32)
        def rep(small):
            return small[:, 0:B].rearrange("p (b o) -> p b o", o=1).broadcast_to([128, B, 32])
        self.dve(lambda e: e.transpose(vt[:, 0:W], x_ap), ins=[keyx], outs=["vt"])
        self.dve(lambda e: e.tensor_reduce(
            red[:, 0:B], b3(vt), AX, OP.max, apply_absolute_value=True),
            ins=["vt"], outs=["red"])
        self.dve(lambda e: e.tensor_scalar(
            scn[:, 0:B], red[:, 0:B], cst[:, 149:150], None, OP.mult),
            ins=["red", "rcp448"], outs=["scq"])
        self.dve(lambda e: e.tensor_scalar(
            scn[:, 0:B], scn[:, 0:B], -1e-30, None, OP.min),
            ins=["scq"], outs=["scq"])
        self.dve(lambda e: e.reciprocal(inv[:, 0:B], scn[:, 0:B]),
                 ins=["scq"], outs=["invq"])
        self.dve(lambda e: e.tensor_tensor(b3(r_), b3(vt), rep(inv), OP.mult),
                 ins=["vt", "invq"], outs=["r_"])
        self.dve(lambda e: e.tensor_scalar(
            it[:, 0:W], r_[:, 0:W].bitcast(I32), 0x7F800000, None,
            OP.bitwise_and), ins=["r_"], outs=["itq"])
        self.dve(lambda e: e.tensor_scalar(
            it[:, 0:W], it[:, 0:W], 0x3C800000, None, OP.max),
            ins=["itq"], outs=["itq"])
        self.dve(lambda e: e.tensor_scalar(
            it[:, 0:W], it[:, 0:W], 3 << 23, None, OP.subtract),
            ins=["itq"], outs=["itq"])
        self.dve(lambda e: e.tensor_tensor(
            b3(st_), b3(it.bitcast(F32)), rep(scn), OP.mult),
            ins=["itq", "scq"], outs=["stsq"])
        self.dve(lambda e: e.tensor_scalar(
            it[:, 0:W], it[:, 0:W], -1, None, OP.mult),
            ins=["itq"], outs=["itq"])
        self.dve(lambda e: e.tensor_scalar(
            it[:, 0:W], it[:, 0:W], 254 << 23, None, OP.add),
            ins=["itq"], outs=["itq"])
        self.dve(lambda e: e.tensor_tensor(
            y_[:, 0:W], r_[:, 0:W], it[:, 0:W].bitcast(F32), OP.mult),
            ins=["r_", "itq"], outs=["yq"])
        self.dve(lambda e: e.tensor_scalar(
            w_[:, 0:W], y_[:, 0:W], 12582912.0, None, OP.add),
            ins=["yq"], outs=["wq"])
        self.dve(lambda e: e.tensor_scalar(
            w_[:, 0:W], w_[:, 0:W], 12582912.0, None, OP.subtract),
            ins=["wq"], outs=["wq"])
        self.dve(lambda e: e.tensor_mul(dqT[:, 0:W], w_[:, 0:W], st_[:, 0:W]),
                 ins=["wq", "stsq"], outs=["dqT"])
        self.dve(lambda e: e.transpose(out_ap, dqT[:, 0:W]), ins=["dqT"], outs=[keyo])

    def quant_tiles(self, W):
        qt = {}
        for nm in ("scn", "inv", "red"):
            qt[nm] = self.RF4.tile(max(W // 32, 1))
        for nm in ("r", "st", "y", "w"):
            qt[nm] = self.RF4.tile(W)
        qt["vt"] = self.RBF.tile(W)
        qt["dqT"] = self.RBF.tile(W)
        qt["it"] = self.ITI.tile(W)
        return qt

    def f_wquant(self):
        import os as _os
        NT = int(_os.environ.get("KNT", "100000"))
        _done = 0
        D = self.D
        self.RBF.reset(); self.RF4.reset(); self.ITI.reset()
        W = 1024
        qt = self.quant_tiles(W)
        xin = self.RBF.tile(W)
        xout = self.RBF.tile(W)
        for src, dst, kt_n, m_n in (("wgT", "wgq_s", 16, FFI), ("wuT", "wuq_s", 16, FFI),
                                    ("wdT", "wdq_s", 64, H)):
            s_r = D[src].rearrange("(kt kp) m -> kp kt m", kp=128)
            d_r = D[dst].rearrange("(kt kp) m -> kp kt m", kp=128)
            for kt in range(kt_n):
                for m0 in range(0, m_n, W):
                    if _done >= NT:
                        continue
                    _done += 1
                    self.dma("gd", xin, s_r[:, kt, m0:m0 + W], outs=["qxin"])
                    self.emit_quant(xin, xout, "qxin", "qxout", W, qt)
                    self.dma("ad", d_r[:, kt, m0:m0 + W], xout,
                             ins=["qxout"], outs=["qso"])
        self.S.barrier()
        self.RBF.reset(); self.RF4.reset()

    def f5_mlpo(self):
        D = self.D
        cst = self.P["consts"].ap()
        self.RBF.reset(); self.RF4.reset()
        oc_r = D["oc_s"].rearrange("(t p) n -> p t n", p=128)
        ao_r = D["ao_s"].rearrange("(t p) n -> p t n", p=128)
        x0_r = D["x0own"].rearrange("(t p) n -> p t n", p=128)
        x1_r = D["x1_s"].rearrange("(t p) n -> p t n", p=128)
        osel = self.RBF.tile(16 * 512)
        asel = self.RBF.tile(16 * 512)
        bounce = self.RBF.tile(512)
        h2ch = self.RBF.tile(16 * 512)
        wcol = [self.RBF.tile(2048) for _ in range(2)]
        x1b = self.RF4.tile(16 * 512)
        sqt = self.RF4.tile(512)
        x0tile = self.RF4.tile(512)
        w_r = D["wmloT"].rearrange("(kt kp) m -> kp kt m", kp=128)
        mr = self.P["mrow"].ap()
        selc = cst[:, 150:151]
        for j in range(4):
            for (buf, srcr, k1) in ((osel, oc_r, "osel"), (asel, ao_r, "asel")):
                for t in range(16):
                    at = buf[:, 512 * t:512 * (t + 1)]
                    self.dma("sd", at, srcr[:, t, CH * j:CH * (j + 1)], outs=[k1])
                    self.dma("sd", bounce, srcr[:, t, CH * (j + 4):CH * (j + 5)],
                             outs=["bnc"])
                    self.dve(lambda e, at=at: e.tensor_sub(bounce, bounce, at),
                             ins=[k1, "bnc"], outs=["bnc"])
                    self.dve(lambda e: e.tensor_scalar(
                        bounce, bounce, selc, None, OP.mult),
                        ins=["bnc", "c_sel"], outs=["bnc"])
                    self.dve(lambda e, at=at: e.tensor_add(at, at, bounce),
                             ins=[k1, "bnc"], outs=[k1])
            for m in range(16):
                wc = wcol[m % 2]
                wkey = f"wc{m % 2}"
                self.dma("gd", wc.rearrange("p (t x) -> p t x", t=16),
                         w_r[:, :, 128 * m:128 * (m + 1)], outs=[wkey])
                bank = self.PS[m % 4]
                bkey = f"ps{m % 4}"
                for k in range(16):
                    self.mm(bank, wc[:, 128 * k:128 * (k + 1)],
                            osel[:, 512 * k:512 * (k + 1)],
                            start=(k == 0), stop=(k == 15),
                            ins=[wkey, "osel"], outs=[bkey])
                xt = x1b[:, 512 * m:512 * (m + 1)]
                at = asel[:, 512 * m:512 * (m + 1)]
                self.dve(lambda e, xt=xt, s=bank, at=at: e.tensor_add(xt, s, at),
                         ins=[bkey, "asel"], outs=["x1b"])
                self.dve(lambda e, xt=xt, m=m: e.tensor_scalar(
                    xt, xt, cst[:, 84 + m:85 + m], None, OP.mult),
                    ins=["x1b", "c_ga"], outs=["x1b"])
                self.dma("sd", x0tile, x0_r[:, m, CH * j:CH * (j + 1)], outs=["x0t"])
                self.dve(lambda e, xt=xt: e.tensor_add(xt, xt, x0tile),
                         ins=["x1b", "x0t"], outs=["x1b"])
                ps_s = self.PS[6][0:1, :]
                ps_q = self.PS[7][0:1, :]
                self.mm(ps_s, cst[:, 151:152], xt, start=(m == 0), stop=(m == 15),
                        ins=["ones_c", "x1b"], outs=["ps6"])
                self.act(lambda e, xt=xt: e.activation(sqt, xt, AF.Square),
                         ins=["x1b"], outs=["sq2"])
                self.mm(ps_q, cst[:, 151:152], sqt, start=(m == 0), stop=(m == 15),
                        ins=["ones_c", "sq2"], outs=["ps7"])
            self.dma("ad", x1_r[:, :, CH * j:CH * (j + 1)],
                     x1b.rearrange("p (t x) -> p t x", t=16), ins=["x1b"],
                     outs=[f"x1{j}"])
            m_ = mr[:, 0:512]; A_ = mr[:, 512:1024]; B_ = mr[:, 1024:1536]
            v_ = mr[:, 1536:2048]
            self.dve(lambda e: e.tensor_scalar_mul(m_, self.PS[6][0:1, :], 1.0 / 2048.0),
                     ins=["ps6"], outs=["m2_"])
            self.dve(lambda e: e.tensor_scalar_mul(v_, self.PS[7][0:1, :], 1.0 / 2048.0),
                     ins=["ps7"], outs=["v2_"])
            self.dve(lambda e: e.scalar_tensor_tensor(
                A_, m_, 1.0, m_, OP.bypass, OP.mult), ins=["m2_"], outs=["A2_"])
            self.dve(lambda e: e.tensor_sub(v_, v_, A_), ins=["v2_", "A2_"], outs=["v2_"])
            self.act(lambda e: e.activation(A_, v_, AF.Sqrt, bias=1e-5),
                     ins=["v2_", "kconst"], outs=["A2_"])
            self.dve(lambda e: e.reciprocal(A_, A_), ins=["A2_"], outs=["A2_"])
            self.dve(lambda e: e.scalar_tensor_tensor(
                B_, m_, -1.0, A_, OP.mult, OP.mult), ins=["m2_", "A2_"], outs=["B2_"])
            psA = self.PS[4]; psB = self.PS[5]
            self.mm(psA, self.P["onesr"].ap(), A_, True, True,
                    ins=["ones_r", "A2_"], outs=["ps4"])
            self.mm(psB, self.P["onesr"].ap(), B_, True, True,
                    ins=["ones_r", "B2_"], outs=["ps5"])
            for m in range(16):
                xt = x1b[:, 512 * m:512 * (m + 1)]
                ht = h2ch[:, 512 * m:512 * (m + 1)]
                self.dve(lambda e, xt=xt: e.tensor_mul(sqt, xt, psA),
                         ins=["x1b", "ps4"], outs=["sq2"])
                self.dve(lambda e: e.tensor_add(sqt, sqt, psB),
                         ins=["sq2", "ps5"], outs=["sq2"])
                self.dve(lambda e, ht=ht, m=m: e.tensor_scalar(
                    ht, sqt, cst[:, 116 + m:117 + m], cst[:, 100 + m:101 + m],
                    OP.mult, OP.add), ins=["sq2", "c_cb1", "c_sb"], outs=["h2ch"])
            self.dma("ad", D["h2_s"].rearrange("(t p) n -> p t n", p=128)[
                :, :, CH * j:CH * (j + 1)],
                h2ch.rearrange("p (t x) -> p t x", t=16), ins=["h2ch"],
                outs=[f"h2s{j}"])
        self.S.barrier()
        self.RBF.reset(); self.RF4.reset()

    def f6_ffn(self):
        D = self.D
        cst = self.P["consts"].ap()
        self.RF4.reset()
        self.RBF.reset(); self.ITI.reset()
        self.h2res = self.RBF.tile(16 * 2048)
        self.RBF.push()
        qt = self.quant_tiles(1024)
        bounce = self.RBF.tile(1024)
        h2_r = D["h2_s"].rearrange("(t p) n -> p t n", p=128)
        for t in range(16):
            for half in range(2):
                xa = self.h2res[:, 2048 * t + 1024 * half:2048 * t + 1024 * (half + 1)]
                self.dma("sd", bounce, h2_r[:, t, 1024 * half:1024 * (half + 1)],
                         outs=["h2b"])
                self.emit_quant(bounce, xa, "h2b", "h2res", 1024, qt)
        self.S.barrier()
        self.RBF.pop()
        self.RF4.reset(); self.ITI.reset()
        wcol = [self.RBF.tile(2048) for _ in range(3)]
        hkbuf = [self.RBF.tile(1024) for _ in range(2)]
        hrow = self.RBF.tile(512)
        hqt = self.RBF.tile(512)
        qt2 = self.quant_tiles(512)
        wg_r = D["wgq_s"].rearrange("(kt kp) m -> kp kt m", kp=128)
        wu_r = D["wuq_s"].rearrange("(kt kp) m -> kp kt m", kp=128)
        wd_r = D["wdq_s"].rearrange("(kt kp) m -> kp kt m", kp=128)
        hq_r = D["hq_s"].rearrange("(kt kp) m -> kp kt m", kp=128)
        x1_r = D["x1_s"].rearrange("(t p) n -> p t n", p=128)
        out_r = D["out"].rearrange("(t p) n -> p t n", p=128)
        x1t = self.RF4.tile(512)
        hrow2 = self.RF4.tile(512)
        for tg in range(4):
            for ht in range(64):
                for gu, wr in ((0, wg_r), (1, wu_r)):
                    ri = (2 * ht + gu) % 3
                    wc = wcol[ri]
                    wkey = f"wq{ri}"
                    self.dma("sd" if gu == 0 else "gd",
                             wc.rearrange("p (t x) -> p t x", t=16),
                             wr[:, :, 128 * ht:128 * (ht + 1)], outs=[wkey])
                    bank = self.PS[gu]
                    bkey = f"ps{gu}"
                    for k in range(16):
                        self.mm(bank, wc[:, 128 * k:128 * (k + 1)],
                                self.h2res[:, 2048 * k + 512 * tg:
                                           2048 * k + 512 * (tg + 1)],
                                start=(k == 0), stop=(k == 15),
                                ins=[wkey, "h2res"], outs=[bkey])
                self.act(lambda e: e.activation(hrow, self.PS[0], AF.Silu),
                         ins=["ps0"], outs=["hrow"])
                self.dve(lambda e: e.tensor_mul(hrow, hrow, self.PS[1]),
                         ins=["hrow", "ps1"], outs=["hrow"])
                self.emit_quant(hrow, hqt, "hrow", "hqt", 512, qt2)
                self.dma("ad", hq_r[:, ht, 512 * tg:512 * (tg + 1)], hqt,
                         ins=["hqt"], outs=["hq_s"])
            for mg in range(4):
                for kt in range(64):
                    par = kt % 2
                    hk = hkbuf[par][:, 0:512]
                    hkk = f"hk{par}"
                    self.dma("sd", hk[:, 0:512], hq_r[:, kt, 512 * tg:512 * (tg + 1)],
                             ins=["hq_s"], outs=[hkk])
                    wdc = hkbuf[par][:, 512:1024]
                    wdk = f"wd{par}"
                    self.dma("gd", wdc.rearrange("p (t x) -> p t x", t=4),
                             wd_r[:, kt, 512 * mg:512 * (mg + 1)].rearrange(
                                 "p (t x) -> p t x", t=4), outs=[wdk])
                    for mi in range(4):
                        bank = self.PS[4 + mi]
                        bkey = f"ps{4 + mi}"
                        self.mm(bank, wdc[:, 128 * mi:128 * (mi + 1)],
                                hk[:, 0:512], start=(kt == 0), stop=(kt == 63),
                                ins=[hkk, wdk], outs=[bkey])
                for mi in range(4):
                    m = 4 * mg + mi
                    bank = self.PS[4 + mi]
                    bkey = f"ps{4 + mi}"
                    self.dma("sd", x1t, x1_r[:, m, CH * tg:CH * (tg + 1)],
                             outs=["x1t"])
                    self.dve(lambda e, bank=bank, m=m: e.tensor_scalar(
                        hrow2, bank, cst[:, 132 + m:133 + m], None, OP.mult),
                        ins=[bkey, "c_gb"], outs=["x2t"])
                    self.dve(lambda e: e.tensor_add(hrow2, hrow2, x1t),
                             ins=["x2t", "x1t"], outs=["x2t"])
                    self.dma("ad", out_r[:, m, CH * tg:CH * (tg + 1)], hrow2,
                             ins=["x2t"], outs=["outd"])
        self.S.barrier()

    def finish(self):
        self._finish_inner()
        mybir.codegen_inst_isa_subclasses(self.nc)

    def _finish_inner(self):
        nc = self.nc
        sems = {k: self.st.enter_context(nc.semaphore(k)) for k in SEMS}
        blk = self.st.enter_context(nc.Block())
        S = self.S

        def replay(stream):
            def run(eng):
                for waits, emit, sem, inc in S.ops[stream]:
                    for psem, pcnt in waits:
                        eng.wait_ge(sems[psem], pcnt)
                    if emit is not None:
                        ins = emit(eng)
                        if sem is not None:
                            ins.then_inc(sems[sem], inc)
            return run

        blk.tensor(replay("pe"))
        blk.vector(replay("dve"))
        blk.scalar(replay("act"))
        blk.gpsimd(replay("gp"))
        blk.sync(replay("sync"))

    def close(self):
        self.st.close()


def build_nc(upto="all"):
    nc = bass.Bass("TRN2", target_bir_lowering=False)
    kb = KB(nc, upto)
    kb.alloc_all()
    kb.f0()
    if upto in ("f0", "f0a", "f0b", "f0c"):
        kb.finish(); return nc, kb
    kb.f1_ln1()
    if upto in ("f1a",):
        kb.finish(); return nc, kb
    kb.f1_qk()
    if upto in ("f1b",):
        kb.finish(); return nc, kb
    kb.f1_v()
    if upto in ("f1v",):
        kb.finish(); return nc, kb
    kb.f2_rope()
    if upto in ("f2a",):
        kb.finish(); return nc, kb
    kb.f2_attn()
    if upto in ("f2b",):
        kb.finish(); return nc, kb
    kb.f2_oproj()
    if upto in ("f2c",):
        kb.finish(); return nc, kb
    kb.f3_mlpqkv()
    if upto in ("f3",):
        kb.finish(); return nc, kb
    kb.f4_ttt()
    if upto in ("f4",):
        kb.finish(); return nc, kb
    kb.f_wquant()
    if upto in ("f4b",):
        kb.finish(); return nc, kb
    kb.f5_mlpo()
    if upto in ("f5",):
        kb.finish(); return nc, kb
    kb.f6_ffn()
    kb.finish()
    return nc, kb


def kernel(**inputs):
    in_maps = host_inputs(inputs)
    nc, kb = build_nc()
    res = run_bass_kernel_spmd(nc, in_maps, core_ids=list(range(8)))
    outs = []
    for core in range(8):
        outs.append(np.asarray(res.results[core]["out"]).T)
    full = np.empty((4, T, H), dtype=np.float32)
    for core in range(8):
        b, s = core // 2, core % 2
        full[b, 2048 * s:2048 * (s + 1)] = outs[core]
    return full


# version 12 (final)
# speedup vs baseline: 1.2316x; 1.0112x over previous
import math
from contextlib import ExitStack

import numpy as np
import ml_dtypes
import concourse.bass as bass
import concourse.bass_isa as bass_isa
from concourse.library_config import all_libraries
LIB_ATTN = all_libraries[1]
import concourse.mybir as mybir
from concourse.bass_utils import run_bass_kernel_spmd

F32 = mybir.dt.float32
BF16 = mybir.dt.bfloat16
I32 = mybir.dt.int32
AX = mybir.AxisListType.X
OP = mybir.AluOpType
AF = mybir.ActivationFunctionType

H = 2048
T = 4096
CH = 512
NCH = T // CH
NAH = 16
AHD = 128
NFH = 4
FHD = 512
WIN = 1024
FFI = 8192
LN2 = math.log(2.0)
BASE_LR_INV = 0.001 + float(np.log(-np.expm1(-0.001)))
ATT_SCALE = 1.0 / math.sqrt(AHD)

STREAMS = ("pe", "dve", "act", "gp", "sync")
NPOOL = 8
DMA_QUEUES = ("sd", "ad", "gd")
SEMS = ("pe", "dve", "act", "gp") + tuple(
    f"{q}{i}" for q in DMA_QUEUES for i in range(NPOOL))
STREAM_SEM = {"pe": "pe", "dve": "dve", "act": "act", "gp": "gp"}


class Sched:

    def __init__(self):
        self.ops = {s: [] for s in STREAMS}
        self.cnt = {s: 0 for s in SEMS}
        self.water = {s: {p: 0 for p in SEMS} for s in STREAMS}
        self.bufs = {}
        self.rr = {q: 0 for q in DMA_QUEUES}

    def _buf(self, key):
        b = self.bufs.get(key)
        if b is None:
            b = {"w": None, "r": {}}
            self.bufs[key] = b
        return b

    def op(self, stream, emit, ins=(), outs=(), sem=None, inc=None):
        isdma = sem in DMA_QUEUES
        if isdma:
            q = sem
            sem = f"{q}{self.rr[q] % NPOOL}"
            self.rr[q] += 1
        elif sem is None:
            sem = STREAM_SEM[stream]
        if inc is None:
            inc = 16 if isdma else 1
        waits = {}

        def need(psem, pcnt):
            if stream == "pe" and psem == "pe":
                return
            if pcnt > self.water[stream][psem]:
                waits[psem] = max(waits.get(psem, 0), pcnt)

        if isdma and self.cnt[sem] > 0:
            need(sem, self.cnt[sem])
        for k in ins:
            b = self._buf(k)
            if b["w"] is not None:
                need(*b["w"])
        for k in outs:
            b = self._buf(k)
            if b["w"] is not None:
                need(*b["w"])
            for psem, pcnt in b["r"].items():
                need(psem, pcnt)
        for psem, pcnt in waits.items():
            self.water[stream][psem] = pcnt
        self.cnt[sem] += inc
        mycnt = self.cnt[sem]
        self.ops[stream].append((sorted(waits.items()), emit, sem if inc else None, inc))
        for k in ins:
            self._buf(k)["r"][sem] = mycnt
        for k in outs:
            b = self._buf(k)
            b["w"] = (sem, mycnt)
            b["r"] = {}

    def barrier(self):
        snap = dict(self.cnt)
        for s in STREAMS:
            for p in SEMS:
                if p == STREAM_SEM.get(s):
                    continue
                if snap[p] > self.water[s][p]:
                    self.water[s][p] = snap[p]
                    self.ops[s].append(([(p, snap[p])], None, None, 0))


class Arena:

    def __init__(self, ap, cols, name):
        self.ap = ap
        self.cols = cols
        self.off = 0
        self.name = name
        self.marks = []

    def tile(self, n):
        assert self.off + n <= self.cols, f"{self.name} arena overflow: {self.off}+{n}>{self.cols}"
        v = self.ap[:, self.off:self.off + n]
        self.off += n
        return v

    def push(self):
        self.marks.append(self.off)

    def pop(self):
        self.off = self.marks.pop()

    def reset(self):
        self.off = 0
        self.marks = []


def cdiv(a, b):
    return (a + b - 1) // b



_ROPE_CACHE = {}


def rope_tables():
    if "C" in _ROPE_CACHE:
        return _ROPE_CACHE["C"], _ROPE_CACHE["S"]
    import os
    import subprocess
    import sys
    code = (
        "import numpy as np\n"
        "import jax\n"
        "jax.config.update('jax_platforms', 'cpu')\n"
        "import jax.numpy as jnp\n"
        f"AHD={AHD}; T={T}\n"
        "inv_freq = 1.0 / 500000.0 ** (jnp.arange(0, AHD, 2, dtype=jnp.float32) / AHD)\n"
        "t = jnp.arange(T, dtype=jnp.float32)\n"
        "freqs = jnp.outer(t, inv_freq)\n"
        "cos = np.asarray(jnp.cos(freqs))\n"
        "sin = np.asarray(jnp.sin(freqs))\n"
        "import sys\n"
        "sys.stdout.buffer.write(cos.astype(np.float32).tobytes())\n"
        "sys.stdout.buffer.write(sin.astype(np.float32).tobytes())\n"
    )
    env = dict(os.environ)
    env["JAX_PLATFORMS"] = "cpu"
    r = subprocess.run([sys.executable, "-c", code], check=True, env=env,
                       capture_output=True)
    n = T * (AHD // 2) * 4
    cos = np.frombuffer(r.stdout[:n], dtype=np.float32).reshape(T, AHD // 2)
    sin = np.frombuffer(r.stdout[n:2 * n], dtype=np.float32).reshape(T, AHD // 2)
    Ct = np.concatenate([cos.T, cos.T], axis=0).astype(np.float32)
    St = np.concatenate([-sin.T, sin.T], axis=0).astype(np.float32)
    _ROPE_CACHE["C"] = np.ascontiguousarray(Ct)
    _ROPE_CACHE["S"] = np.ascontiguousarray(St)
    return _ROPE_CACHE["C"], _ROPE_CACHE["S"]


def attn_masks():
    kk = np.arange(128)[:, None]
    qq = np.arange(128)[None, :]
    edge = (kk > qq).astype(ml_dtypes.bfloat16)
    diag = (kk <= qq).astype(ml_dtypes.bfloat16)
    return np.ascontiguousarray(edge), np.ascontiguousarray(diag)


def host_inputs(z):
    x0 = np.asarray(z["L0_attn_hidden_states"])
    TR = lambda a: np.ascontiguousarray(np.asarray(a, np.float32).T)
    colv = lambda a: np.ascontiguousarray(
        np.asarray(a, np.float32).reshape(16, 128).T)
    Ct, St = rope_tables()
    edge, diag = attn_masks()
    w0 = np.asarray(z["L0_attn_w0"], np.float32)
    w1 = np.asarray(z["L0_attn_w1"], np.float32)
    w2 = np.asarray(z["L0_attn_w2"], np.float32)
    sel = np.zeros((128, 16, 16), dtype=np.float32)
    for m in range(16):
        sel[:, m, m] = 1.0
    shared = {
        "wqkvT": TR(z["L0_attn_attn_qkv_w"]),
        "woT": TR(z["L0_attn_attn_o_w"]),
        "wmlpT": TR(z["L0_attn_mlp_qkv_w"]),
        "wmloT": TR(z["L0_attn_mlp_o_w"]),
        "wlrT": TR(z["L0_attn_lr_w"]),
        "wgT": TR(z["L0_ffn_gate_w"]),
        "wuT": TR(z["L0_ffn_up_w"]),
        "wdT": TR(z["L0_ffn_down_w"]),
        "w0T": np.ascontiguousarray(w0.transpose(0, 2, 1)),
        "w2T": np.ascontiguousarray(w2.transpose(0, 2, 1)),
        "w1o": np.ascontiguousarray(w1),
        "w1e": np.ascontiguousarray(w1.transpose(0, 2, 1)),
        "qnw": colv(z["L0_attn_q_norm_w"]),
        "knw": colv(z["L0_attn_k_norm_w"]),
        "qnw_s": np.ascontiguousarray(np.roll(colv(z["L0_attn_q_norm_w"]), 64, axis=0)),
        "knw_s": np.ascontiguousarray(np.roll(colv(z["L0_attn_k_norm_w"]), 64, axis=0)),
        "mnw": colv(z["L0_attn_mlp_norm_w"]),
        "tnw": np.ascontiguousarray(
            np.tile(np.asarray(z["L0_attn_ttt_norm_w"], np.float32).reshape(4, 128).T,
                    (1, 1))),
        "lrb": np.ascontiguousarray(np.asarray(z["L0_attn_lr_b"], np.float32).reshape(12, 1)),
        "lrb_row": np.ascontiguousarray(np.asarray(z["L0_attn_lr_b"], np.float32).reshape(1, 12)),
        "wmod": np.ascontiguousarray(np.asarray(z["L0_wmod"], np.float32)),
        "ropeC": Ct, "ropeS": St,
        "mask_edge": edge, "mask_diag": diag,
        "sel16": np.ascontiguousarray(sel.reshape(128, 256)),
    }
    in_maps = []
    for core in range(8):
        b, s = core // 2, core % 2
        m = dict(shared)
        m["x0fm"] = np.ascontiguousarray(x0[b].T.astype(np.float32))
        m["x0own"] = np.ascontiguousarray(x0[b, 2048 * s:2048 * (s + 1)].T.astype(np.float32))
        m["c_col"] = np.ascontiguousarray(
            np.asarray(z["L0_c"], np.float32)[b].reshape(16, 128).T)
        m["selmask"] = np.full((128, 1), float(s), dtype=np.float32)
        in_maps.append(m)
    return in_maps



class KB:

    def __init__(self, nc, upto="all"):
        self.nc = nc
        self.S = Sched()
        self.upto = upto
        self.st = ExitStack()
        self._declare_io()

    def dma(self, q, dst, src, ins=(), outs=(), transpose=False):
        stream = {"sd": "sync", "ad": "act", "gd": "gp"}[q]
        if transpose:
            self.S.op(stream, lambda e: e.dma_start_transpose(dst, src),
                      ins=ins, outs=outs, sem=q)
        else:
            self.S.op(stream, lambda e: e.dma_start(dst, src),
                      ins=ins, outs=outs, sem=q)

    def pe(self, emit, ins=(), outs=()):
        self.S.op("pe", emit, ins=ins, outs=outs)

    def dve(self, emit, ins=(), outs=()):
        self.S.op("dve", emit, ins=ins, outs=outs)

    def act(self, emit, ins=(), outs=()):
        self.S.op("act", emit, ins=ins, outs=outs)

    def gp(self, emit, ins=(), outs=()):
        self.S.op("gp", emit, ins=ins, outs=outs)

    def mm(self, out, lhsT, rhs, start, stop, ins, outs):
        self.pe(lambda e: e.matmul(out, lhsT, rhs, start=start, stop=stop,
                                   skip_group_check=True),
                ins=ins, outs=outs)

    def _declare_io(self):
        nc = self.nc
        D = {}

        def inp(name, shape, dt=F32):
            D[name] = nc.dram_tensor(name, list(shape), dt, kind="ExternalInput").ap()

        inp("x0fm", (H, T)); inp("x0own", (H, 2048))
        inp("wqkvT", (H, 3 * H)); inp("woT", (H, H))
        inp("wmlpT", (H, 3 * H)); inp("wmloT", (H, H)); inp("wlrT", (H, 12))
        inp("wgT", (H, FFI)); inp("wuT", (H, FFI)); inp("wdT", (FFI, H))
        inp("w0T", (NFH, FHD, FHD)); inp("w2T", (NFH, FHD, FHD))
        inp("w1o", (NFH, FHD, FHD)); inp("w1e", (NFH, FHD, FHD))
        inp("qnw", (128, 16)); inp("knw", (128, 16)); inp("mnw", (128, 16))
        inp("qnw_s", (128, 16)); inp("knw_s", (128, 16))
        inp("tnw", (128, 4)); inp("lrb", (12, 1))
        inp("wmod", (H, 6 * H)); inp("c_col", (128, 16))
        inp("lrb_row", (1, 12))
        inp("ropeC", (128, T)); inp("ropeS", (128, T))
        inp("mask_edge", (128, 128), BF16); inp("mask_diag", (128, 128), BF16)
        inp("sel16", (128, 256)); inp("selmask", (128, 1))
        D["out"] = nc.dram_tensor("out", [H, 2048], F32, kind="ExternalOutput").ap()

        import os as _os
        _dbg = "KDBG" in _os.environ

        def scratch(name, shape, dt=BF16):
            kind = "ExternalOutput" if (_dbg and name in (
                "q_s", "ao_s", "xm_s", "fq_s", "oc_s", "x1_s", "h2_s", "wgq_s")) else "Internal"
            D[name] = nc.dram_tensor(name, list(shape), dt, kind=kind).ap()

        scratch("h_s", (H, T))
        scratch("q_s", (H, T)); scratch("k_s", (H, T))
        scratch("qr_s", (H, T)); scratch("kr_s", (H, T))
        scratch("v_s", (T, H))
        scratch("o_s", (H, T)); scratch("ao_s", (H, T)); scratch("xm_s", (H, T))
        scratch("fq_s", (H, T))
        scratch("fk_s", (T, H)); scratch("fv_s", (T, H))
        scratch("oc_s", (H, T))
        scratch("x1_s", (H, 2048), F32)
        scratch("h2_s", (H, 2048))
        scratch("wgq_s", (H, FFI)); scratch("wuq_s", (H, FFI))
        scratch("wdq_s", (FFI, H)); scratch("hq_s", (FFI, 2048))
        scratch("dbg", (128, 4096), F32)
        self.D = D

    def alloc_all(self):
        nc, st = self.nc, self.st
        self.RBF_t = st.enter_context(nc.sbuf_tensor("RBF", [128, 43 * 1024], BF16))
        self.RF4_t = st.enter_context(nc.sbuf_tensor("RF4", [128, 21 * 1024], F32))
        self.RBF = Arena(self.RBF_t.ap(), 43 * 1024, "RBF")
        self.RF4 = Arena(self.RF4_t.ap(), 21 * 1024, "RF4")
        self.ITI_t = st.enter_context(nc.sbuf_tensor("ITI", [128, 2048], mybir.dt.int32))
        self.ITI = Arena(self.ITI_t.ap(), 2048, "ITI")
        P = {}
        P["consts"] = st.enter_context(nc.sbuf_tensor("consts", [128, 768], F32))
        P["onesr"] = st.enter_context(nc.sbuf_tensor("onesr", [1, 128], F32))
        P["onesb"] = st.enter_context(nc.sbuf_tensor("onesb", [128, 1], BF16))
        P["rows"] = st.enter_context(nc.sbuf_tensor("rows", [16, T], BF16))
        P["rows2"] = st.enter_context(nc.sbuf_tensor("rows2", [16, T], BF16))
        P["lr_t"] = st.enter_context(nc.sbuf_tensor("lr_t", [128, 384], F32))
        P["mrow"] = st.enter_context(nc.sbuf_tensor("mrow", [1, 2048], F32))
        P["maskE"] = st.enter_context(nc.sbuf_tensor("maskE", [128, 128], BF16))
        P["maskD"] = st.enter_context(nc.sbuf_tensor("maskD", [128, 128], BF16))
        self.P = P
        cst = P["consts"].ap()
        self.kconst_slots = {}
        vals = [-1.0, 1.0, 0.5, 1e-5, 1e-6, 67.0, 2.0, -2.0, float(BASE_LR_INV)]
        for i, v in enumerate(vals):
            ap = cst[:, 428 + i:429 + i]
            nc.const_aps.register(F32, float(v), ap)
            self.kconst_slots[float(v)] = ap
        self.PS = [st.enter_context(nc.psum_tensor(f"ps{i}", [128, 512], F32)).ap()
                   for i in range(8)]

    def C(self, a, b):
        return self.P["consts"].ap()[:, a:b]

    def f0(self):
        import os
        SKIP = os.environ.get("KSKIP", "")
        D, S = self.D, self.S
        cst = self.P["consts"].ap()
        self.S.op("gp", lambda e: e.load_library(LIB_ATTN), inc=0)
        self.dma("sd", cst[:, 0:16], D["qnw"], outs=["c_qnw"])
        self.dma("sd", cst[:, 16:32], D["knw"], outs=["c_knw"])
        self.dma("sd", cst[:, 32:48], D["mnw"], outs=["c_mnw"])
        self.dma("sd", cst[:, 640:656], D["qnw_s"], outs=["c_qnws"])
        self.dma("sd", cst[:, 656:672], D["knw_s"], outs=["c_knws"])
        self.dma("sd", cst[:, 48:52], D["tnw"], outs=["c_tnw"])
        self.dma("sd", cst[:, 150:151], D["selmask"], outs=["c_sel"])
        self.dma("sd", cst[:, 152:408], D["sel16"], outs=["c_sel16"])
        self.dma("sd", cst[:, 408:424], D["c_col"], outs=["c_ccol"])
        self.dma("sd", self.P["maskE"].ap(), D["mask_edge"], outs=["maskE"])
        self.dma("sd", self.P["maskD"].ap(), D["mask_diag"], outs=["maskD"])
        self.dma("sd", cst[0:12, 424:425], D["lrb"], outs=["lrb"])
        if "lrb" not in SKIP:
            self.dma("sd", cst[0:1, 472:484], D["lrb_row"], outs=["lrbr"])
            self.gp(lambda e: e.partition_broadcast(cst[:, 472:484], cst[0:1, 472:484], 128),
                    ins=["lrbr"], outs=["lrbb"])
        self.dve(lambda e: e.memset(cst[:, 146:147], 2.0), outs=["two_c"])
        self.act(lambda e: e.activation(cst[:, 147:148], cst[:, 146:147], AF.Ln),
                 ins=["two_c"], outs=["ln2col"])
        self.dve(lambda e: e.tensor_scalar(
            cst[:, 145:146], cst[:, 147:148], -1.0, None, OP.mult),
            ins=["ln2col"], outs=["ln2col"])
        self.dve(lambda e: e.tensor_scalar(
            cst[:, 144:145], cst[:, 147:148], 16.0, None, OP.mult),
            ins=["ln2col"], outs=["ln2col"])
        for v, ap in self.kconst_slots.items():
            self.dve(lambda e, a=ap, vv=v: e.memset(a, vv), outs=["kconst"])
        self.dve(lambda e: e.memset(cst[:, 151:152], 1.0), outs=["ones_c"])
        self.dve(lambda e: e.memset(self.P["onesr"].ap(), 1.0), outs=["ones_r"])
        self.dve(lambda e: e.memset(self.P["onesb"].ap(), 1.0), outs=["ones_b"])
        self.dve(lambda e: e.memset(cst[:, 148:149], -448.0), outs=["c448"])
        self.dve(lambda e: e.reciprocal(cst[:, 149:150], cst[:, 148:149]),
                 ins=["c448"], outs=["rcp448"])
        if self.upto == "f0a":
            self.S.barrier()
            return
        self.RBF.push()
        csil = self.RF4.tile(16)
        self.act(lambda e: e.activation(csil, cst[:, 408:424], AF.Silu),
                 ins=["c_ccol"], outs=["csil"])
        wst = self.RF4.tile(2048)
        wstb = self.RF4.tile(2048)
        modv = self.RF4.tile(96)
        self.modv = modv
        wmod_r = D["wmod"].rearrange("(kt kp) m -> kp kt m", kp=128)
        for mc in range(96):
            stg = wst if mc % 2 == 0 else wstb
            key = f"modw{mc % 2}"
            self.dma("sd", stg.rearrange("p (kt x) -> p kt x", kt=16),
                     wmod_r[:, :, 128 * mc:128 * (mc + 1)], outs=[key])
            ps = self.PS[mc % 2][:, 0:1]
            pkey = f"ps{mc % 2}"
            for k in range(16):
                self.mm(ps, stg[:, 128 * k:128 * (k + 1)], csil[:, k:k + 1],
                        start=(k == 0), stop=(k == 15),
                        ins=[key, "csil"], outs=[pkey])
            mcv = mc
            self.dve(lambda e, m=mcv, p=ps: e.tensor_copy(modv[:, m:m + 1], p),
                     ins=[pkey], outs=["modv"])
        if self.upto == "f0b":
            self.RBF.pop()
            self.S.barrier()
            return
        self.dve(lambda e: e.tensor_copy(cst[:, 52:68], modv[:, 0:16]),
                 ins=["modv"], outs=["c_sa"])
        self.act(lambda e: e.activation(cst[:, 68:84], modv[:, 16:32], AF.Identity,
                                        bias=1.0), ins=["modv", "kconst"], outs=["c_ca1"])
        self.dve(lambda e: e.tensor_copy(cst[:, 100:116], modv[:, 48:64]),
                 ins=["modv"], outs=["c_sb"])
        self.act(lambda e: e.activation(cst[:, 116:132], modv[:, 64:80], AF.Identity,
                                        bias=1.0), ins=["modv", "kconst"], outs=["c_cb1"])
        if self.upto == "f0c":
            self.RBF.pop()
            self.S.barrier()
            return
        for src0, dst0, nm in ((32, 84, "c_ga"), (80, 132, "c_gb")):
            sgn = self.RF4.tile(16)
            self.dve(lambda e, s=src0, d=sgn: e.tensor_scalar(
                d, modv[:, s:s + 16], 0.0, None, OP.is_ge),
                ins=["modv"], outs=["sgn_t"])
            self.act(lambda e, d=sgn: e.activation(d, d, AF.Identity, bias=-1.0,
                                                   scale=2.0),
                     ins=["sgn_t", "kconst"], outs=["sgn_t"])
            mag = self.RF4.tile(16)
            self.act(lambda e, s=src0, d=mag: e.activation(
                d, modv[:, s:s + 16], AF.Abs), ins=["modv"], outs=["mag_t"])
            self.dve(lambda e, d=mag: e.tensor_scalar(
                d, d, 0.25, None, OP.max), ins=["mag_t"], outs=["mag_t"])
            self.dve(lambda e, d=dst0, a=sgn, b=mag: e.tensor_mul(
                cst[:, d:d + 16], a, b), ins=["sgn_t", "mag_t"], outs=[nm])
        self.RBF.pop()
        S.barrier()
        self.RF4.reset()

    def f1_ln1(self):
        D = self.D
        cst = self.P["consts"].ap()
        self.RBF.reset(); self.RF4.reset()
        xch = [self.RF4.tile(512) for _ in range(2)]
        self.RF4.reset()
        xbuf = [self.RF4.tile(16 * 512) for _ in range(2)]
        hbuf = self.RBF.tile(16 * 512)
        sqt = self.RF4.tile(512)
        x0r = D["x0fm"].rearrange("(t p) n -> p t n", p=128)
        h_sr = D["h_s"].rearrange("(t p) n -> p t n", p=128)
        for c in range(NCH):
            xb = xbuf[c % 2]
            xk = f"x0c{c % 2}"
            self.dma("sd", xb.rearrange("p (t x) -> p t x", t=16),
                     x0r[:, :, CH * c:CH * (c + 1)], outs=[xk])
            ps_s = self.PS[4][0:1, :]
            ps_q = self.PS[5][0:1, :]
            for t in range(16):
                xt = xb[:, 512 * t:512 * (t + 1)]
                self.mm(ps_s, cst[:, 151:152], xt, start=(t == 0), stop=(t == 15),
                        ins=["ones_c", xk], outs=["ps4"])
                self.act(lambda e, xt=xt: e.activation(sqt, xt, AF.Square),
                         ins=[xk], outs=["sqt"])
                self.mm(ps_q, cst[:, 151:152], sqt, start=(t == 0), stop=(t == 15),
                        ins=["ones_c", "sqt"], outs=["ps5"])
            mr = self.P["mrow"].ap()
            m_ = mr[:, 0:512]; A_ = mr[:, 512:1024]; B_ = mr[:, 1024:1536]
            v_ = mr[:, 1536:2048]
            self.dve(lambda e: e.tensor_scalar_mul(m_, ps_s, 1.0 / 2048.0),
                     ins=["ps4"], outs=["m_"])
            self.dve(lambda e: e.tensor_scalar_mul(v_, ps_q, 1.0 / 2048.0),
                     ins=["ps5"], outs=["v_"])
            self.dve(lambda e: e.scalar_tensor_tensor(
                A_, m_, 1.0, m_, OP.bypass, OP.mult), ins=["m_"], outs=["A_"])
            self.dve(lambda e: e.tensor_sub(v_, v_, A_), ins=["v_", "A_"], outs=["v_"])
            self.act(lambda e: e.activation(A_, v_, AF.Sqrt, bias=1e-5),
                     ins=["v_", "kconst"], outs=["A_"])
            self.dve(lambda e: e.reciprocal(A_, A_), ins=["A_"], outs=["A_"])
            self.dve(lambda e: e.scalar_tensor_tensor(
                B_, m_, -1.0, A_, OP.mult, OP.mult), ins=["m_", "A_"], outs=["B_"])
            psA = self.PS[6]; psB = self.PS[7]
            self.mm(psA, self.P["onesr"].ap(), A_, True, True,
                    ins=["ones_r", "A_"], outs=["ps6"])
            self.mm(psB, self.P["onesr"].ap(), B_, True, True,
                    ins=["ones_r", "B_"], outs=["ps7"])
            for t in range(16):
                xt = xb[:, 512 * t:512 * (t + 1)]
                ht = hbuf[:, 512 * t:512 * (t + 1)]
                tmp = sqt
                self.dve(lambda e, xt=xt: e.tensor_mul(tmp, xt, psA),
                         ins=[xk, "ps6"], outs=["sqt"])
                self.dve(lambda e: e.tensor_add(tmp, tmp, psB),
                         ins=["sqt", "ps7"], outs=["sqt"])
                self.dve(lambda e, t=t, ht=ht: e.tensor_scalar(
                    ht, tmp, cst[:, 68 + t:69 + t], cst[:, 52 + t:53 + t],
                    OP.mult, OP.add), ins=["sqt", "c_ca1", "c_sa"], outs=["hbuf"])
            self.dma("ad", h_sr[:, :, CH * c:CH * (c + 1)],
                     hbuf.rearrange("p (t x) -> p t x", t=16), ins=["hbuf"],
                     outs=[f"h_s{c}"])
        self.S.barrier()
        self.RBF.reset(); self.RF4.reset()

    def f1_qk(self):
        D = self.D
        cst = self.P["consts"].ap()
        self.RBF.reset(); self.RF4.reset()
        wres = self.RBF.tile(8 * 16 * 128)
        hch = [self.RBF.tile(16 * 512) for _ in range(2)]
        evac = self.RBF.tile(8 * 512)
        sqt = self.RF4.tile(512)
        h_sr = D["h_s"].rearrange("(t p) n -> p t n", p=128)
        wq_r = D["wqkvT"].rearrange("(kt kp) m -> kp kt m", kp=128)
        for half, (w_off, out_s, nrm_rows) in enumerate(
                ((0, "q_s", self.P["rows"].ap()), (2048, "k_s", self.P["rows2"].ap()))):
            out_r = D[out_s].rearrange("(t p) n -> p t n", p=128)
            for quart in range(2):
                mo = 8 * quart
                for mc in range(8):
                    co = w_off + 128 * (mo + mc)
                    dst = wres[:, 2048 * mc:2048 * (mc + 1)]
                    self.dma("gd", dst.rearrange("p (t x) -> p t x", t=16),
                             wq_r[:, :, co:co + 128], outs=[f"wres{mc}"])
                for c in range(NCH):
                    hb = hch[c % 2]
                    hk = f"hch{c % 2}"
                    self.dma("sd", hb.rearrange("p (t x) -> p t x", t=16),
                             h_sr[:, :, CH * c:CH * (c + 1)], outs=[hk])
                    ps_r = self.PS[4][0:16, :]
                    for m in range(8):
                        bank = self.PS[m % 4]
                        bkey = f"ps{m % 4}"
                        for k in range(16):
                            lhsT = wres[:, 2048 * m + 128 * k:2048 * m + 128 * (k + 1)]
                            self.mm(bank, lhsT, hb[:, 512 * k:512 * (k + 1)],
                                    start=(k == 0), stop=(k == 15),
                                    ins=[f"wres{m}", hk], outs=[bkey])
                        et = evac[:, 512 * m:512 * (m + 1)]
                        self.act(lambda e, d=et, s=bank: e.activation(d, s, AF.Copy),
                                 ins=[bkey], outs=["evac"])
                        self.act(lambda e, s=et: e.activation(sqt, s, AF.Square),
                                 ins=["evac"], outs=["sqt"])
                        gm = mo + m
                        self.mm(ps_r, cst[:, 152 + 16 * gm:152 + 16 * (gm + 1)], sqt,
                                start=(m == 0), stop=(m == 7),
                                ins=["c_sel16", "sqt"], outs=["ps4"])
                    self.dma("ad", out_r[:, mo:mo + 8, CH * c:CH * (c + 1)],
                             evac.rearrange("p (t x) -> p t x", t=8), ins=["evac"],
                             outs=[f"{out_s}{c}_{quart}"])
                    nr = nrm_rows[:, CH * c:CH * (c + 1)]
                    if quart == 0:
                        self.dve(lambda e, nr=nr: e.tensor_copy(nr, ps_r),
                                 ins=["ps4"], outs=[f"nrm{half}"])
                    else:
                        self.dve(lambda e, nr=nr: e.tensor_add(nr, nr, ps_r),
                                 ins=["ps4", f"nrm{half}"], outs=[f"nrm{half}"])
        self.S.barrier()
        self.RBF.reset(); self.RF4.reset()

    def f1_v(self):
        D = self.D
        self.RBF.reset(); self.RF4.reset()
        wres = self.RBF.tile(8 * 16 * 128)
        hch = [self.RBF.tile(16 * 512) for _ in range(2)]
        evac = self.RBF.tile(2 * 512)
        h_sr = D["h_s"].rearrange("(t p) n -> p t n", p=128)
        wv_r = D["wqkvT"].rearrange("(kt kp) m -> kp kt m", kp=128)
        v_r = D["v_s"].rearrange("(n p) t -> p n t", p=128)
        for vh in range(2):
            for mc in range(8):
                co = 4096 + 1024 * vh + 128 * mc
                dst = wres[:, 2048 * mc:2048 * (mc + 1)]
                self.dma("gd", dst.rearrange("p (t x) -> p t x", t=16),
                         wv_r[:, :, co:co + 128], outs=["wres"])
            wres_r = wres.rearrange("p (m x) -> p m x", m=8)
            for c in range(NCH):
                hb = hch[c % 2]
                hk = f"hch{c % 2}"
                self.dma("sd", hb.rearrange("p (t x) -> p t x", t=16),
                         h_sr[:, :, CH * c:CH * (c + 1)], outs=[hk])
                for mt in range(4):
                    for nc2 in range(2):
                        bank = self.PS[nc2]
                        bkey = f"ps{nc2}"
                        for k in range(16):
                            lhsT = hb[:, 512 * k + 128 * mt:512 * k + 128 * (mt + 1)]
                            rhs = wres_r[:, 4 * nc2:4 * nc2 + 4,
                                         128 * k:128 * k + 128]
                            self.mm(bank, lhsT, rhs, start=(k == 0), stop=(k == 15),
                                    ins=["wres", hk], outs=[bkey])
                        et = evac[:, 512 * nc2:512 * (nc2 + 1)]
                        self.act(lambda e, d=et, s=bank: e.activation(d, s, AF.Copy),
                                 ins=[bkey], outs=["evac"])
                    self.dma("ad", v_r[:, 4 * c + mt, 1024 * vh:1024 * (vh + 1)],
                             evac.rearrange("p (n x) -> p n x", n=2), ins=["evac"],
                             outs=[f"v_s{c}_{mt}_{vh}"])
        self.S.barrier()
        self.RBF.reset(); self.RF4.reset()

    def f2_rope(self):
        D = self.D
        cst = self.P["consts"].ap()
        for half, (in_s, out_s, rows_t, w_a, w_b, bnd_dst) in enumerate((
                ("q_s", "qr_s", self.P["rows"].ap(), 0, 640, 0),
                ("k_s", "kr_s", self.P["rows2"].ap(), 16, 656, 8))):
            wkeys = ["c_qnw", "c_qnws"] if half == 0 else ["c_knw", "c_knws"]
            self.RBF.reset(); self.RF4.reset()
            tot = self.RF4.tile(T)[0:16, :]
            self.dve(lambda e, r=rows_t, t=tot: e.tensor_copy(t, r),
                    ins=[f"nrm{half}"], outs=["tot"])
            self.gp(lambda e, t=tot: e.partition_all_reduce(
                t, t, 16, bass.bass_isa.ReduceOp.add), ins=["tot"], outs=["tot"])
            inv = self.RF4.tile(T)[0:1, :]
            self.act(lambda e, t=tot, i=inv: e.activation(
                i, t[0:1, :], AF.Sqrt, bias=1e-6, scale=1.0 / 2048.0),
                ins=["tot", "kconst"], outs=["invr"])
            self.dve(lambda e, i=inv: e.reciprocal(i, i), ins=["invr"], outs=["invr"])
            inv2 = self.RF4.tile(T)[0:16, :]
            self.dve(lambda e, i=inv, o=inv2: e.scalar_tensor_tensor(
                o[0:1, :], i, 1.0, i, OP.bypass, OP.mult), ins=["invr"], outs=["inv2"])
            self.gp(lambda e, o=inv2: e.partition_broadcast(o, o[0:1, :], 16),
                    ins=["inv2"], outs=["inv2"])
            u = self.RF4.tile(T)[0:16, :]
            self.dve(lambda e, r=rows_t, i=inv2, u=u: e.tensor_mul(u, r, i),
                     ins=[f"nrm{half}", "inv2"], outs=["u_"])
            bnd = cst[0:16, 672 + bnd_dst:672 + bnd_dst + 8]
            self.dve(lambda e, u=u, b=bnd: e.tensor_reduce(
                b, u.rearrange("p (c x) -> p c x", c=8), AX, OP.max),
                ins=["u_"], outs=[f"bnd{half}"])
            combC = self.RBF.tile(T)
            combS = self.RBF.tile(T)
            rtile = self.RF4.tile(512)
            for c in range(NCH):
                ps = self.PS[6]
                self.mm(ps, self.P["onesr"].ap(), inv[:, CH * c:CH * (c + 1)],
                        True, True, ins=["ones_r", "invr"], outs=["ps6"])
                for tbl, comb, keyo in (("ropeC", combC, "combC"), ("ropeS", combS, "combS")):
                    self.dma("sd", rtile, D[tbl][:, CH * c:CH * (c + 1)], outs=["rtile"])
                    self.dve(lambda e, co=comb, c=c, rt=rtile, ps=ps: e.tensor_mul(
                        co[:, CH * c:CH * (c + 1)], rt, ps),
                        ins=["rtile", "ps6"], outs=[keyo])
            in_r = D[in_s].rearrange("(t p) n -> p t n", p=128)
            out_r = D[out_s].rearrange("(t p) n -> p t n", p=128)
            qa = self.RBF.tile(T)
            qb = self.RBF.tile(T)
            t1 = self.RBF.tile(T)
            t2 = self.RBF.tile(T)
            for t in range(16):
                self.dma("sd", qa, in_r[:, t, :], outs=["qa"])
                self.dma("sd", qb[0:64, :], D[in_s].rearrange(
                    "(t p) n -> t p n", t=16)[t, 64:128, :], outs=["qb"])
                self.dma("sd", qb[64:128, :], D[in_s].rearrange(
                    "(t p) n -> t p n", t=16)[t, 0:64, :], outs=["qb"])
                wa = cst[:, w_a + t:w_a + t + 1]
                wb = cst[:, w_b + t:w_b + t + 1]
                self.dve(lambda e, wa=wa: e.scalar_tensor_tensor(
                    t1, qa, wa, combC, OP.mult, OP.mult),
                    ins=["qa", "combC", wkeys[0]], outs=["t1", "t1f"])
                self.dve(lambda e, wb=wb: e.scalar_tensor_tensor(
                    t2, qb, wb, combS, OP.mult, OP.mult),
                    ins=["qb", "combS", wkeys[1]], outs=["t2"])
                self.dve(lambda e: e.tensor_add(t1, t1, t2),
                         ins=["t1", "t2"], outs=["t1f"])
                self.dma("ad", out_r[:, t, :], t1, ins=["t1f"], outs=[f"{out_s}{t}"])
        self.S.barrier()
        self.RBF.reset(); self.RF4.reset()

    def f2_attn(self):
        D = self.D
        cst = self.P["consts"].ap()
        for hg in range(2):
            self.RBF.reset(); self.RF4.reset()
            kwin = [self.RBF.tile(8 * 512) for _ in range(3)]
            vwin = [self.RBF.tile(8 * 512) for _ in range(3)]
            qch = self.RBF.tile(8 * 512)
            invb = self.RBF.tile(128)
            pt = self.RBF.tile(9 * 128)
            och = self.RBF.tile(8 * 512)
            sinv = self.RF4.tile(128)[0:1, :]
            mbias = self.RF4.tile(128)
            kr_r = D["kr_s"].rearrange("(t p) n -> p t n", p=128)
            v_r = D["v_s"].rearrange("(n p) t -> p n t", p=128)
            qr_r = D["qr_s"].rearrange("(t p) n -> p t n", p=128)
            bq = cst[0:16, 672:680]
            bk = cst[0:16, 680:688]
            wq2 = self.RF4.tile(8)[0:16, 0:1]
            wk2 = self.RF4.tile(8)[0:16, 0:1]
            if hg == 0:
                for wsl, dst, key in ((0, wq2, "wq2"), (16, wk2, "wk2")):
                    tmp = self.RF4.tile(16)
                    self.gp(lambda e, a=wsl, t=tmp: e.partition_all_reduce(
                        t, cst[:, a:a + 16], 128, bass.bass_isa.ReduceOp.absmax),
                        ins=["c_qnw", "c_knw"], outs=["tmpw"])
                    ps = self.PS[6][0:16, 0:1]
                    self.mm(ps, tmp[0:1, :], self.P["onesr"].ap()[0:1, 0:1],
                            True, True, ins=["tmpw", "ones_r"], outs=["ps6"])
                    self.act(lambda e, d=dst, s=ps: e.activation(d, s, AF.Square),
                             ins=["ps6"], outs=[key])
                bk3 = self.RF4.tile(8)[0:16, :]
                self.dve(lambda e, b=bk3: e.tensor_copy(b, bk), ins=["bnd1"], outs=["bk3"])
                for sh in (1, 2):
                    self.dve(lambda e, b=bk3, sh=sh: e.tensor_tensor(
                        b[:, sh:8], b[:, sh:8], bk[:, 0:8 - sh], OP.max),
                        ins=["bk3", "bnd1"], outs=["bk3"])
                prod = self.RF4.tile(8)[0:16, :]
                self.dve(lambda e, p=prod, b=bk3: e.tensor_mul(p, bq, b),
                         ins=["bnd0", "bk3"], outs=["prod"])
                self.dve(lambda e, p=prod: e.tensor_scalar(
                    p, p, wq2, None, OP.mult), ins=["prod", "wq2"], outs=["prod"])
                self.dve(lambda e, p=prod: e.tensor_scalar(
                    p, p, wk2, None, OP.mult), ins=["prod", "wk2"], outs=["prod"])
                self.act(lambda e, p=prod: e.activation(
                    p, p, AF.Sqrt, scale=ATT_SCALE * ATT_SCALE),
                    ins=["prod"], outs=["prod"])
                self.dve(lambda e, p=prod: e.tensor_scalar(
                    p, p, 100.0, None, OP.min), ins=["prod"], outs=["prod"])
                self.dve(lambda e, p=prod: e.tensor_scalar(
                    p, p, -1.0, 44.0, OP.mult, OP.add), ins=["prod"], outs=["prod"])
                mb = cst[0:16, 688:696]
                self.dve(lambda e, p=prod, m=mb: e.tensor_copy(m, p),
                         ins=["prod"], outs=["mb"])
                self.dma("sd", self.P["mrow"].ap()[0:1, 1024:1152],
                         mb, ins=["mb"], outs=["mbrow"])
            mbr = self.P["mrow"].ap()
            for h8 in range(8):
                h = 8 * hg + h8
                for c in range(NCH):
                    self.gp(lambda e, h=h, c=c, h8=h8: e.partition_broadcast(
                        mbias[:, 8 * h8 + c:8 * h8 + c + 1],
                        mbr[0:1, 1024 + 8 * h + c:1025 + 8 * h + c], 128),
                        ins=["mbrow"], outs=["mbias"])
            for c in range(NCH):
                sl = c % 3
                self.dma("sd", kwin[sl].rearrange("p (t x) -> p t x", t=8),
                         kr_r[:, 8 * hg:8 * hg + 8, CH * c:CH * (c + 1)],
                         outs=[f"kw{sl}"])
                self.dma("sd", vwin[sl].rearrange("p (n x) -> p n x", n=4),
                         v_r[:, 4 * c:4 * c + 4, 1024 * hg:1024 * (hg + 1)],
                         outs=[f"vw{sl}"])
                self.dma("sd", qch.rearrange("p (t x) -> p t x", t=8),
                         qr_r[:, 8 * hg:8 * hg + 8, CH * c:CH * (c + 1)],
                         outs=["qch"])
                for h8 in range(8):
                    h = 8 * hg + h8
                    for qt in range(4):
                        q0 = CH * c + 128 * qt
                        j0 = max(0, (WIN - q0) // 128)
                        njt = 9 - j0
                        qap = qch[:, 512 * h8 + 128 * qt:512 * h8 + 128 * (qt + 1)]
                        for jj in range(j0, 9):
                            kpos = q0 - WIN + 128 * jj
                            ksl = (kpos // CH) % 3
                            koff = kpos % CH
                            kap = kwin[ksl][:, 512 * h8 + koff:512 * h8 + koff + 128]
                            bi = (jj - j0) // 4
                            bo = ((jj - j0) % 4) * 128
                            bank = self.PS[bi]
                            self.mm(bank[:, bo:bo + 128], kap, qap,
                                    start=(bo == 0), stop=True,
                                    ins=[f"kw{ksl}", "qch"], outs=[f"ps{bi}"])
                        nb = cdiv(njt, 4)
                        for bi in range(nb):
                            w = min(4, njt - 4 * bi) * 128
                            bank = self.PS[bi]
                            self.act(lambda e, bi=bi, w=w, h8=h8, c=c, bank=bank:
                                     e.activation(
                                         pt[:, 512 * bi:512 * bi + w],
                                         bank[:, 0:w], AF.Exp,
                                         bias=mbias[:, 8 * h8 + c:8 * h8 + c + 1],
                                         scale=ATT_SCALE),
                                     ins=[f"ps{bi}", "mbias"], outs=["pt"])
                        if q0 >= WIN:
                            self.dve(lambda e: e.tensor_mul(
                                pt[:, 0:128], pt[:, 0:128], self.P["maskE"].ap()),
                                ins=["pt", "maskE"], outs=["pt"])
                        dof = 128 * (njt - 1)
                        self.dve(lambda e, dof=dof: e.tensor_mul(
                            pt[:, dof:dof + 128], pt[:, dof:dof + 128],
                            self.P["maskD"].ap()), ins=["pt", "maskD"], outs=["pt"])
                        srow_ps = self.PS[3][0:1, 0:128]
                        for jj in range(j0, 9):
                            self.mm(srow_ps, self.P["onesb"].ap(),
                                    pt[:, 128 * (jj - j0):128 * (jj - j0 + 1)],
                                    start=(jj == j0), stop=(jj == 8),
                                    ins=["ones_b", "pt"], outs=["ps3r"])
                        self.dve(lambda e, s=srow_ps: e.reciprocal(sinv, s),
                                 ins=["ps3r"], outs=["sinv"])
                        ob = self.PS[4][:, 0:128]
                        for jj in range(j0, 9):
                            kpos = q0 - WIN + 128 * jj
                            ksl = (kpos // CH) % 3
                            ktt = (kpos % CH) // 128
                            vap = vwin[ksl][:, 1024 * ktt + 128 * h8:
                                            1024 * ktt + 128 * (h8 + 1)]
                            self.mm(ob, vap, pt[:, 128 * (jj - j0):128 * (jj - j0 + 1)],
                                    start=(jj == j0), stop=(jj == 8),
                                    ins=[f"vw{ksl}", "pt"], outs=["ps4"])
                        bb = self.PS[5][:, 0:128]
                        self.mm(bb, self.P["onesr"].ap(), sinv, True, True,
                                ins=["ones_r", "sinv"], outs=["ps5"])
                        self.act(lambda e: e.activation(invb, bb, AF.Copy),
                                 ins=["ps5"], outs=["invb128"])
                        oap = och[:, 512 * h8 + 128 * qt:512 * h8 + 128 * (qt + 1)]
                        self.dve(lambda e, o=oap: e.tensor_mul(o, ob, invb),
                                 ins=["ps4", "invb128"], outs=["och"])
                self.dma("ad", D["o_s"].rearrange("(t p) n -> p t n", p=128)[
                    :, 8 * hg:8 * hg + 8, CH * c:CH * (c + 1)],
                    och.rearrange("p (t x) -> p t x", t=8),
                    ins=["och"], outs=[f"o_s{hg}_{c}"])
            self.S.barrier()
        self.S.barrier()
        self.RBF.reset(); self.RF4.reset()

    def f2_oproj(self):
        D = self.D
        cst = self.P["consts"].ap()
        self.RBF.reset(); self.RF4.reset()
        och = [self.RBF.tile(16 * 512) for _ in range(2)]
        wocol = [self.RBF.tile(2048) for _ in range(2)]
        aoch = self.RBF.tile(16 * 512)
        xmch = self.RBF.tile(16 * 512)
        aof = self.RF4.tile(16 * 512)
        sqt = self.RF4.tile(512)
        o_r = D["o_s"].rearrange("(t p) n -> p t n", p=128)
        wo_r = D["woT"].rearrange("(kt kp) m -> kp kt m", kp=128)
        ao_r = D["ao_s"].rearrange("(t p) n -> p t n", p=128)
        xm_r = D["xm_s"].rearrange("(t p) n -> p t n", p=128)
        mr = self.P["mrow"].ap()
        for c in range(NCH):
            ob = och[c % 2]
            okey = f"och{c % 2}"
            self.dma("sd", ob.rearrange("p (t x) -> p t x", t=16),
                     o_r[:, :, CH * c:CH * (c + 1)], outs=[okey])
            ps_r = self.PS[5][0:1, :]
            for m in range(16):
                wc = wocol[m % 2]
                wkey = f"woc{m % 2}"
                self.dma("gd", wc.rearrange("p (t x) -> p t x", t=16),
                         wo_r[:, :, 128 * m:128 * (m + 1)], outs=[wkey])
                bank = self.PS[m % 4]
                bkey = f"ps{m % 4}"
                for k in range(16):
                    self.mm(bank, wc[:, 128 * k:128 * (k + 1)],
                            ob[:, 512 * k:512 * (k + 1)],
                            start=(k == 0), stop=(k == 15),
                            ins=[wkey, okey], outs=[bkey])
                at = aof[:, 512 * m:512 * (m + 1)]
                self.dve(lambda e, d=at, s=bank: e.tensor_copy(d, s),
                         ins=[bkey], outs=["aof"])
                self.act(lambda e, s=at: e.activation(sqt, s, AF.Square),
                         ins=["aof"], outs=["sqt"])
                self.mm(ps_r, cst[:, 151:152], sqt, start=(m == 0), stop=(m == 15),
                        ins=["ones_c", "sqt"], outs=["ps5"])
            iv = mr[:, 0:512]
            self.act(lambda e: e.activation(iv, ps_r, AF.Sqrt, bias=1e-6,
                                            scale=1.0 / 2048.0),
                     ins=["ps5", "kconst"], outs=["iv"])
            self.dve(lambda e: e.reciprocal(iv, iv), ins=["iv"], outs=["iv"])
            bb = self.PS[6]
            self.mm(bb, self.P["onesr"].ap(), iv, True, True,
                    ins=["ones_r", "iv"], outs=["ps6"])
            for m in range(16):
                at = aof[:, 512 * m:512 * (m + 1)]
                ac = aoch[:, 512 * m:512 * (m + 1)]
                xc = xmch[:, 512 * m:512 * (m + 1)]
                self.dve(lambda e, d=ac, s=at: e.tensor_copy(d, s),
                         ins=["aof"], outs=["aoch"])
                self.dve(lambda e, x=xc, s=at, m=m: e.scalar_tensor_tensor(
                    x, s, cst[:, 32 + m:33 + m], bb, OP.mult, OP.mult),
                    ins=["aof", "c_mnw", "ps6"], outs=["xmch"])
            self.dma("ad", ao_r[:, :, CH * c:CH * (c + 1)],
                     aoch.rearrange("p (t x) -> p t x", t=16),
                     ins=["aoch"], outs=[f"ao{c}"])
            self.dma("ad", xm_r[:, :, CH * c:CH * (c + 1)],
                     xmch.rearrange("p (t x) -> p t x", t=16),
                     ins=["xmch"], outs=[f"xm{c}"])
        self.S.barrier()
        self.RBF.reset(); self.RF4.reset()

    def f3_mlpqkv(self):
        D = self.D
        cst = self.P["consts"].ap()
        self.RBF.reset(); self.RF4.reset()
        wres = self.RBF.tile(8 * 16 * 128)
        xch = [self.RBF.tile(16 * 512) for _ in range(2)]
        evac = self.RBF.tile(8 * 512)
        sqt = self.RF4.tile(512)
        nrm = self.RF4.tile(512)
        nrm2 = self.RF4.tile(512)
        nrk = self.RF4.tile(8)
        xm_r = D["xm_s"].rearrange("(t p) n -> p t n", p=128)
        w_r = D["wmlpT"].rearrange("(kt kp) m -> kp kt m", kp=128)
        fq_r = D["fq_s"].rearrange("(t p) n -> p t n", p=128)
        for quart in range(2):
            mo = 8 * quart
            for mc in range(8):
                co = 128 * (mo + mc)
                dst = wres[:, 2048 * mc:2048 * (mc + 1)]
                self.dma("gd", dst.rearrange("p (t x) -> p t x", t=16),
                         w_r[:, :, co:co + 128], outs=[f"wres{mc}"])
            for c in range(NCH):
                xb = xch[c % 2]
                xk = f"xch{c % 2}"
                self.dma("sd", xb.rearrange("p (t x) -> p t x", t=16),
                         xm_r[:, :, CH * c:CH * (c + 1)], outs=[xk])
                for hl in range(2):
                    ps_r = self.PS[4][0:16, :]
                    for mt in range(4):
                        m = 4 * hl + mt
                        bank = self.PS[mt % 4]
                        bkey = f"ps{mt % 4}"
                        for k in range(16):
                            lhsT = wres[:, 2048 * m + 128 * k:2048 * m + 128 * (k + 1)]
                            self.mm(bank, lhsT, xb[:, 512 * k:512 * (k + 1)],
                                    start=(k == 0), stop=(k == 15),
                                    ins=[f"wres{m}", xk], outs=[bkey])
                        et = evac[:, 512 * m:512 * (m + 1)]
                        self.act(lambda e, d=et, s=bank: e.activation(d, s, AF.Silu),
                                 ins=[bkey], outs=["evac"])
                        self.act(lambda e, s=et: e.activation(sqt, s, AF.Square),
                                 ins=["evac"], outs=["sqt"])
                        self.mm(ps_r, cst[:, 152 + 16 * mt:152 + 16 * (mt + 1)], sqt,
                                start=(mt == 0), stop=(mt == 3),
                                ins=["c_sel16", "sqt"], outs=["ps4"])
                    nr = nrm[0:4, :]
                    self.dve(lambda e, nr=nr: e.tensor_copy(nr, self.PS[4][0:4, :]),
                             ins=["ps4"], outs=["nr"])
                    self.gp(lambda e, nr=nr: e.partition_all_reduce(
                        nr, nr, 4, bass_isa.ReduceOp.add), ins=["nr"], outs=["nr"])
                    iv = nrm2[0:1, :]
                    self.act(lambda e, nr=nr, iv=iv: e.activation(
                        iv, nr[0:1, :], AF.Sqrt), ins=["nr"], outs=["iv2"])
                    self.dve(lambda e, iv=iv: e.tensor_scalar_max(iv, iv, 1e-6),
                             ins=["iv2"], outs=["iv2"])
                    self.dve(lambda e, iv=iv: e.reciprocal(iv, iv),
                             ins=["iv2"], outs=["iv2"])
                    bb = self.PS[5]
                    self.mm(bb, self.P["onesr"].ap(), iv, True, True,
                            ins=["ones_r", "iv2"], outs=["ps5"])
                    for mt in range(4):
                        m = 4 * hl + mt
                        et = evac[:, 512 * m:512 * (m + 1)]
                        self.dve(lambda e, et=et: e.tensor_mul(et, et, bb),
                                 ins=["evac", "ps5"], outs=["evac"])
                self.dma("ad", fq_r[:, mo:mo + 8, CH * c:CH * (c + 1)],
                         evac.rearrange("p (t x) -> p t x", t=8), ins=["evac"],
                         outs=[f"fq{c}_{quart}"])
        wres2 = wres
        fk_r = D["fk_s"].rearrange("(n p) t -> p n t", p=128)
        fv_r = D["fv_s"].rearrange("(n p) t -> p n t", p=128)
        evac2 = evac
        for out_s, woff, outr in (("fk_s", 2048, fk_r), ("fv_s", 4096, fv_r)):
            for vh in range(2):
                for mc in range(8):
                    co = woff + 1024 * vh + 128 * mc
                    dst = wres2[:, 2048 * mc:2048 * (mc + 1)]
                    self.dma("gd", dst.rearrange("p (t x) -> p t x", t=16),
                             w_r[:, :, co:co + 128], outs=[f"wres{mc}"])
                wrr = wres2.rearrange("p (m x) -> p m x", m=8)
                for c in range(NCH):
                    xb = xch[c % 2]
                    xk = f"xch{c % 2}"
                    self.dma("sd", xb.rearrange("p (t x) -> p t x", t=16),
                             xm_r[:, :, CH * c:CH * (c + 1)], outs=[xk])
                    for mt in range(4):
                        for nc2 in range(2):
                            bank = self.PS[nc2]
                            bkey = f"ps{nc2}"
                            for k in range(16):
                                lhsT = xb[:, 512 * k + 128 * mt:512 * k + 128 * (mt + 1)]
                                rhs = wrr[:, 4 * nc2:4 * nc2 + 4, 128 * k:128 * k + 128]
                                self.mm(bank, lhsT, rhs, start=(k == 0), stop=(k == 15),
                                        ins=[f"wres{4 * nc2}", f"wres{4 * nc2 + 1}",
                                             f"wres{4 * nc2 + 2}", f"wres{4 * nc2 + 3}",
                                             xk], outs=[bkey])
                            et = evac2[:, 512 * nc2:512 * (nc2 + 1)]
                            self.act(lambda e, d=et, s=bank: e.activation(d, s, AF.Silu),
                                     ins=[bkey], outs=["evac"])
                        if out_s == "fk_s":
                            sq2 = nrm2
                            self.act(lambda e: e.activation(nrm, evac2[:, 0:512],
                                                            AF.Square),
                                     ins=["evac"], outs=["sqa"])
                            self.act(lambda e: e.activation(nrm2, evac2[:, 512:1024],
                                                            AF.Square),
                                     ins=["evac"], outs=["sqb"])
                            nr = nrk[:, 0:2]
                            self.dve(lambda e, nr=nr: e.tensor_reduce(
                                nr[:, 0:1], nrm, AX, OP.add),
                                ins=["sqa"], outs=["nr2"])
                            self.dve(lambda e, nr=nr: e.tensor_reduce(
                                nr[:, 1:2], nrm2, AX, OP.add),
                                ins=["sqb"], outs=["nr2"])
                            self.act(lambda e, nr=nr: e.activation(nr, nr, AF.Sqrt),
                                     ins=["nr2"], outs=["nr2"])
                            self.dve(lambda e, nr=nr: e.tensor_scalar_max(
                                nr, nr, 1e-6), ins=["nr2"], outs=["nr2"])
                            self.dve(lambda e, nr=nr: e.reciprocal(nr, nr),
                                     ins=["nr2"], outs=["nr2"])
                            for bq2 in range(2):
                                self.dve(lambda e, b=bq2, nr=nr: e.tensor_scalar(
                                    evac2[:, 512 * b:512 * (b + 1)],
                                    evac2[:, 512 * b:512 * (b + 1)],
                                    nr[:, b:b + 1], None, OP.mult),
                                    ins=["evac", "nr2"], outs=["evac"])
                        self.dma("ad", outr[:, 4 * c + mt, 1024 * vh:1024 * (vh + 1)],
                                 evac2[:, 0:1024].rearrange("p (n x) -> p n x", n=2),
                                 ins=["evac"], outs=[f"{out_s}{c}_{mt}_{vh}"])
        lrt = self.P["lr_t"].ap()
        wlr = self.RBF.tile(16 * 12)
        self.dma("gd", wlr.rearrange("p (t x) -> p t x", t=16),
                 D["wlrT"].rearrange("(kt kp) m -> kp kt m", kp=128),
                 outs=["wlr"])
        for c in range(NCH):
            xb = xch[c % 2]
            xk = f"xch{c % 2}"
            self.dma("sd", xb.rearrange("p (t x) -> p t x", t=16),
                     xm_r[:, :, CH * c:CH * (c + 1)], outs=[xk])
            for mt in range(4):
                bank = self.PS[6][:, 0:12]
                for k in range(16):
                    lhsT = xb[:, 512 * k + 128 * mt:512 * k + 128 * (mt + 1)]
                    self.mm(bank, lhsT, wlr[:, 12 * k:12 * (k + 1)],
                            start=(k == 0), stop=(k == 15),
                            ins=[xk, "wlr"], outs=["ps6"])
                ti = 12 * (4 * c + mt)
                lap = lrt[:, ti:ti + 12]
                self.dve(lambda e, lap=lap, bank=bank: e.tensor_scalar(
                    lap, bank, float(BASE_LR_INV), None, OP.add),
                    ins=["ps6", "kconst"], outs=["lrt"])
                self.dve(lambda e, lap=lap: e.tensor_add(
                    lap, lap, cst[:, 472:484]), ins=["lrt", "lrbb"], outs=["lrt"])
                self.act(lambda e, lap=lap: e.activation(lap, lap, AF.Exp),
                         ins=["lrt"], outs=["lrt"])
                self.act(lambda e, lap=lap: e.activation(lap, lap, AF.Ln, bias=1.0),
                         ins=["lrt", "kconst"], outs=["lrt"])
        self.S.barrier()
        self.RBF.reset(); self.RF4.reset()

    def f4_ttt(self):
        D = self.D
        self.RBF.reset(); self.RF4.reset()
        M0 = self.RF4.tile(2048); M2 = self.RF4.tile(2048)
        M1o = self.RF4.tile(2048); M1e = self.RF4.tile(2048)
        B0 = self.RBF.tile(2048); B2 = self.RBF.tile(2048)
        B1o = self.RBF.tile(2048); B1e = self.RBF.tile(2048)
        qcf = self.RBF.tile(2048)
        kcf = self.RBF.tile(2048); kct = self.RBF.tile(2048)
        vcf = self.RBF.tile(2048); vct = self.RBF.tile(2048)
        gk = self.RBF.tile(2048); hbm = self.RBF.tile(2048)
        dhid = self.RBF.tile(2048); sg = self.RBF.tile(2048)
        u_ = self.RBF.tile(2048); hid = self.RBF.tile(2048)
        dgp = self.RBF.tile(2048); dhb = self.RBF.tile(2048)
        kl0 = self.RBF.tile(2048); kl2 = self.RBF.tile(2048)
        vl1 = self.RBF.tile(2048)
        gq = gk; t2q = hbm
        hq = dhid
        ocb = self.RBF.tile(2048)
        sqt = self.RF4.tile(512)
        nrm = self.RF4.tile(512)
        lrt = self.P["lr_t"].ap()
        fq_r = D["fq_s"].rearrange("(t p) n -> p t n", p=128)
        fk_r = D["fk_s"].rearrange("(n p) t -> p n t", p=128)
        fv_r = D["fv_s"].rearrange("(n p) t -> p n t", p=128)
        oc_r = D["oc_s"].rearrange("(t p) n -> p t n", p=128)
        cst = self.P["consts"].ap()

        def load_state(h):
            for nm, dst, key in (("w0T", M0, "M0"), ("w2T", M2, "M2"),
                                 ("w1o", M1o, "M1o"), ("w1e", M1e, "M1e")):
                self.dma("sd", dst.rearrange("p (t x) -> p t x", t=4),
                         D[nm].rearrange("h (t p) x -> h p t x", p=128)[h],
                         outs=[key])

        def prod16(outbank_fn, lhsT_tile, rhs_tile, lkey, rkey, okey, fold=None):
            for mt in range(4):
                bank, bkey = outbank_fn(mt)
                for kt in range(4):
                    lhsT = lhsT_tile[:, 512 * kt + 128 * mt:512 * kt + 128 * (mt + 1)]
                    rhs = rhs_tile[:, 512 * kt:512 * (kt + 1)]
                    self.mm(bank, lhsT, rhs, start=(kt == 0), stop=(kt == 3),
                            ins=[lkey, rkey], outs=[bkey])
                if fold is not None:
                    fold(mt, bank, bkey)

        for h in range(NFH):
            load_state(h)
            for nm, mst, bw, km, kb in (("M0", M0, B0, "M0", "B0"),
                                        ("M2", M2, B2, "M2", "B2"),
                                        ("M1o", M1o, B1o, "M1o", "B1o"),
                                        ("M1e", M1e, B1e, "M1e", "B1e")):
                self.dve(lambda e, d=bw, s=mst: e.tensor_copy(d, s),
                         ins=[km], outs=[kb])
            for c in range(NCH):
                self.dma("sd", qcf.rearrange("p (t x) -> p t x", t=4),
                         fq_r[:, 4 * h:4 * h + 4, CH * c:CH * (c + 1)], outs=["qcf"])
                self.dma("sd", kct.rearrange("p (n x) -> p n x", n=4),
                         fk_r[:, 4 * c:4 * c + 4, FHD * h:FHD * (h + 1)], outs=["kct"])
                self.dma("sd", vct.rearrange("p (n x) -> p n x", n=4),
                         fv_r[:, 4 * c:4 * c + 4, FHD * h:FHD * (h + 1)], outs=["vct"])
                import os as _os
                if "tdma" in _os.environ.get("KSKIP", ""):
                    self.dma("sd", kcf.rearrange("p (t x) -> p t x", t=4),
                             fq_r[:, 4 * h:4 * h + 4, CH * c:CH * (c + 1)], outs=["kcf"])
                    self.dma("sd", vcf.rearrange("p (t x) -> p t x", t=4),
                             fq_r[:, 4 * h:4 * h + 4, CH * c:CH * (c + 1)], outs=["vcf"])
                else:
                 for dt4 in range(4):
                    self.dma("sd", kcf[:, 512 * dt4:512 * (dt4 + 1)],
                             D["fk_s"][CH * c:CH * (c + 1),
                                       FHD * h + 128 * dt4:FHD * h + 128 * (dt4 + 1)],
                             outs=["kcf"], transpose=True)
                    self.dma("sd", vcf[:, 512 * dt4:512 * (dt4 + 1)],
                             D["fv_s"][CH * c:CH * (c + 1),
                                       FHD * h + 128 * dt4:FHD * h + 128 * (dt4 + 1)],
                             outs=["vcf"], transpose=True)

                import os as _os
                _SK = _os.environ.get("KSKIP", "")
                do_upd = (c < NCH - 1) and ("upd" not in _SK)
                if "qp" in _SK and "keyp" in _SK:
                    continue
                def foldq(dst, dkey):
                    def f(mt, bank, bkey):
                        et = dst[:, 512 * mt:512 * (mt + 1)]
                        self.dve(lambda e, d=et, s=bank: e.tensor_copy(d, s),
                                 ins=[bkey], outs=[dkey])
                    return f
                prod16(lambda mt: (self.PS[mt], f"ps{mt}"), B0, qcf,
                       "B0", "qcf", None, foldq(gq, "gk"))
                if "t2" not in _SK:
                    prod16(lambda mt: (self.PS[4 + (mt % 2)], f"ps{4 + (mt % 2)}"),
                           B2, qcf, "B2", "qcf", None, foldq(t2q, "hbm"))
                for mt in range(4) if "t2" not in _SK else []:
                    a = gq[:, 512 * mt:512 * (mt + 1)]
                    b = t2q[:, 512 * mt:512 * (mt + 1)]
                    d = hq[:, 512 * mt:512 * (mt + 1)]
                    self.act(lambda e, a=a, d=d: e.activation(d, a, AF.Silu),
                             ins=["gk"], outs=["dhid"])
                    self.dve(lambda e, d=d, b=b: e.tensor_mul(d, d, b),
                             ins=["dhid", "hbm"], outs=["dhid"])
                if "oc" in _SK:
                    continue
                def foldoc(mt, bank, bkey):
                    et = ocb[:, 512 * mt:512 * (mt + 1)]
                    self.dve(lambda e, d=et, s=bank: e.tensor_copy(d, s),
                             ins=[bkey], outs=["ocb"])
                    if ("rms" not in _SK) and ("rmsA" not in _SK):
                        self.act(lambda e, s=et: e.activation(sqt, s, AF.Square),
                                 ins=["ocb"], outs=["sqt"])
                        self.mm(self.PS[6][0:16, :], cst[:, 152 + 16 * mt:152 + 16 * (mt + 1)],
                                sqt, start=(mt == 0), stop=(mt == 3),
                                ins=["sqt", "c_sel16"], outs=["ps6"])
                prod16(lambda mt: (self.PS[mt], f"ps{mt}"), B1e, hq,
                       "B1e", "dhid", None, foldoc)
                if "rms" in _SK:
                    self.dma("ad", oc_r[:, 4 * h:4 * h + 4, CH * c:CH * (c + 1)],
                             ocb.rearrange("p (t x) -> p t x", t=4), ins=["ocb"],
                             outs=[f"oc{h}_{c}"])
                    continue
                nr = nrm[0:16, :]
                self.dve(lambda e, nr=nr: e.tensor_copy(nr, self.PS[6][0:16, :]),
                         ins=["ps6"], outs=["nrf"])
                self.gp(lambda e, nr=nr: e.partition_all_reduce(
                    nr[0:4, :], nr[0:4, :], 4, bass_isa.ReduceOp.add),
                    ins=["nrf"], outs=["nrf"])
                iv = nrm[0:1, :]
                self.act(lambda e, nr=nr, iv=iv: e.activation(
                    iv, nr[0:1, :], AF.Sqrt, bias=1e-6, scale=1.0 / 512.0),
                    ins=["nrf", "kconst"], outs=["ivt"])
                self.dve(lambda e, iv=iv: e.reciprocal(iv, iv),
                         ins=["ivt"], outs=["ivt"])
                bb = self.PS[7]
                self.mm(bb, self.P["onesr"].ap(), iv, True, True,
                        ins=["ones_r", "ivt"], outs=["ps7"])
                for mt in range(4):
                    et = ocb[:, 512 * mt:512 * (mt + 1)]
                    self.dve(lambda e, et=et, mt=mt: e.scalar_tensor_tensor(
                        et, et, cst[:, 48 + mt:49 + mt], bb, OP.mult, OP.mult),
                        ins=["ocb", "c_tnw", "ps7"], outs=["ocb"])
                self.dma("ad", oc_r[:, 4 * h:4 * h + 4, CH * c:CH * (c + 1)],
                         ocb.rearrange("p (t x) -> p t x", t=4), ins=["ocb"],
                         outs=[f"oc{h}_{c}"])
                if "keyp" in _SK:
                    continue
                def foldtm(dst, dkey, actf=None):
                    def f(mt, bank, bkey):
                        et = dst[:, 512 * mt:512 * (mt + 1)]
                        self.dve(lambda e, d=et, s=bank: e.tensor_copy(d, s),
                                 ins=[bkey], outs=[dkey])
                    return f
                prod16(lambda mt: (self.PS[mt], f"ps{mt}"), kcf, B0,
                       "kcf", "B0", None, foldtm(gk, "gk"))
                prod16(lambda mt: (self.PS[4 + (mt % 2)], f"ps{4 + (mt % 2)}"),
                       kcf, B2, "kcf", "B2", None, foldtm(hbm, "hbm"))
                prod16(lambda mt: (self.PS[6 + (mt % 2)], f"ps{6 + (mt % 2)}"),
                       vcf, B1o, "vcf", "B1o", None, foldtm(dhid, "dhid"))
                self.act(lambda e: e.activation(sg, gk, AF.Sigmoid),
                         ins=["gk"], outs=["sg"])
                self.dve(lambda e: e.tensor_mul(u_, gk, sg), ins=["gk", "sg"],
                         outs=["u_t"])
                self.dve(lambda e: e.tensor_mul(hid, u_, hbm), ins=["u_t", "hbm"],
                         outs=["hid"])
                self.dve(lambda e: e.tensor_mul(dgp, dhid, hbm), ins=["dhid", "hbm"],
                         outs=["dgp"])
                self.dve(lambda e: e.tensor_mul(dgp, dgp, sg), ins=["dgp", "sg"],
                         outs=["dgp"])
                self.dve(lambda e: e.tensor_sub(dhb, gk, u_), ins=["gk", "u_t"],
                         outs=["dhb"])
                self.dve(lambda e: e.scalar_tensor_tensor(
                    dhb, dhb, 1.0, dgp, OP.bypass, OP.mult), ins=["dhb", "dgp"],
                    outs=["dhb"])
                self.dve(lambda e: e.tensor_add(dgp, dgp, dhb), ins=["dgp", "dhb"],
                         outs=["dgp"])
                self.dve(lambda e: e.tensor_mul(dhb, dhid, u_), ins=["dhid", "u_t"],
                         outs=["dhb"])
                if do_upd:
                    for mt in range(4):
                        ti = 12 * (4 * c + mt)
                        for src, dst, lri, sk, dk in (
                                (kct, kl0, 0, "kct", "kl0"),
                                (kct, kl2, 8, "kct", "kl2"),
                                (vct, vl1, 4, "vct", "vl1")):
                            a = src[:, 512 * mt:512 * (mt + 1)]
                            d = dst[:, 512 * mt:512 * (mt + 1)]
                            lap = lrt[:, ti + lri + h:ti + lri + h + 1]
                            self.dve(lambda e, a=a, d=d, lap=lap: e.tensor_scalar(
                                d, a, lap, None, OP.mult),
                                ins=[sk, "lrt"], outs=[dk])
                    def foldupd(mst, bw, km, kb):
                        def f(mt, bank, bkey):
                            ms = mst[:, 512 * mt:512 * (mt + 1)]
                            bs = bw[:, 512 * mt:512 * (mt + 1)]
                            self.dve(lambda e, ms=ms, s=bank: e.tensor_add(ms, ms, s),
                                     ins=[bkey, km], outs=[km])
                            self.dve(lambda e, bs=bs, ms=ms: e.tensor_copy(bs, ms),
                                     ins=[km], outs=[kb])
                        return f
                    prod16(lambda mt: (self.PS[mt], f"ps{mt}"), kl0, dgp,
                           "kl0", "dgp", None, foldupd(M0, B0, "M0", "B0"))
                    prod16(lambda mt: (self.PS[4 + (mt % 2)], f"ps{4 + (mt % 2)}"),
                           kl2, dhb, "kl2", "dhb", None, foldupd(M2, B2, "M2", "B2"))
                    prod16(lambda mt: (self.PS[mt], f"ps{mt}"), vl1, hid,
                           "vl1", "hid", None, foldupd(M1o, B1o, "M1o", "B1o"))
                    prod16(lambda mt: (self.PS[4 + (mt % 2)], f"ps{4 + (mt % 2)}"),
                           hid, vl1, "hid", "vl1", None, foldupd(M1e, B1e, "M1e", "B1e"))
        self.S.barrier()
        self.RBF.reset(); self.RF4.reset()

    def emit_quant(self, x_ap, out_ap, keyx, keyo, W, qt):
        cst = self.P["consts"].ap()
        vt, dqT = qt["vt"], qt["dqT"]
        red, scn, inv = qt["red"], qt["scn"], qt["inv"]
        r_, st_, y_, w_ = qt["r"], qt["st"], qt["y"], qt["w"]
        it = qt["it"]
        B = W // 32
        def b3(ap):
            return ap[:, 0:W].rearrange("p (b x) -> p b x", x=32)
        def rep(small):
            return small[:, 0:B].rearrange("p (b o) -> p b o", o=1).broadcast_to([128, B, 32])
        self.dve(lambda e: e.transpose(vt[:, 0:W], x_ap), ins=[keyx], outs=["vt"])
        self.dve(lambda e: e.tensor_reduce(
            red[:, 0:B], b3(vt), AX, OP.max, apply_absolute_value=True),
            ins=["vt"], outs=["red"])
        self.dve(lambda e: e.tensor_scalar(
            scn[:, 0:B], red[:, 0:B], cst[:, 149:150], None, OP.mult),
            ins=["red", "rcp448"], outs=["scq"])
        self.dve(lambda e: e.tensor_scalar(
            scn[:, 0:B], scn[:, 0:B], -1e-30, None, OP.min),
            ins=["scq"], outs=["scq"])
        self.dve(lambda e: e.reciprocal(inv[:, 0:B], scn[:, 0:B]),
                 ins=["scq"], outs=["invq"])
        self.dve(lambda e: e.tensor_tensor(b3(r_), b3(vt), rep(inv), OP.mult),
                 ins=["vt", "invq"], outs=["r_"])
        self.dve(lambda e: e.tensor_scalar(
            it[:, 0:W], r_[:, 0:W].bitcast(I32), 0x7F800000, None,
            OP.bitwise_and), ins=["r_"], outs=["itq"])
        self.dve(lambda e: e.tensor_scalar(
            it[:, 0:W], it[:, 0:W], 0x3C800000, None, OP.max),
            ins=["itq"], outs=["itq"])
        self.dve(lambda e: e.tensor_scalar(
            it[:, 0:W], it[:, 0:W], 3 << 23, None, OP.subtract),
            ins=["itq"], outs=["itq"])
        self.dve(lambda e: e.tensor_tensor(
            b3(st_), b3(it.bitcast(F32)), rep(scn), OP.mult),
            ins=["itq", "scq"], outs=["stsq"])
        self.dve(lambda e: e.tensor_scalar(
            it[:, 0:W], it[:, 0:W], -1, None, OP.mult),
            ins=["itq"], outs=["itq"])
        self.dve(lambda e: e.tensor_scalar(
            it[:, 0:W], it[:, 0:W], 254 << 23, None, OP.add),
            ins=["itq"], outs=["itq"])
        self.dve(lambda e: e.tensor_tensor(
            y_[:, 0:W], r_[:, 0:W], it[:, 0:W].bitcast(F32), OP.mult),
            ins=["r_", "itq"], outs=["yq"])
        self.dve(lambda e: e.tensor_scalar(
            w_[:, 0:W], y_[:, 0:W], 12582912.0, None, OP.add),
            ins=["yq"], outs=["wq"])
        self.dve(lambda e: e.tensor_scalar(
            w_[:, 0:W], w_[:, 0:W], 12582912.0, None, OP.subtract),
            ins=["wq"], outs=["wq"])
        self.dve(lambda e: e.tensor_mul(dqT[:, 0:W], w_[:, 0:W], st_[:, 0:W]),
                 ins=["wq", "stsq"], outs=["dqT"])
        self.dve(lambda e: e.transpose(out_ap, dqT[:, 0:W]), ins=["dqT"], outs=[keyo])

    def quant_tiles(self, W):
        qt = {}
        for nm in ("scn", "inv", "red"):
            qt[nm] = self.RF4.tile(max(W // 32, 1))
        for nm in ("r", "st", "y", "w"):
            qt[nm] = self.RF4.tile(W)
        qt["vt"] = self.RBF.tile(W)
        qt["dqT"] = self.RBF.tile(W)
        qt["it"] = self.ITI.tile(W)
        return qt

    def f_wquant(self):
        import os as _os
        NT = int(_os.environ.get("KNT", "100000"))
        _done = 0
        D = self.D
        self.RBF.reset(); self.RF4.reset(); self.ITI.reset()
        W = 1024
        qt = self.quant_tiles(W)
        xin = self.RBF.tile(W)
        xout = self.RBF.tile(W)
        for src, dst, kt_n, m_n in (("wgT", "wgq_s", 16, FFI), ("wuT", "wuq_s", 16, FFI),
                                    ("wdT", "wdq_s", 64, H)):
            s_r = D[src].rearrange("(kt kp) m -> kp kt m", kp=128)
            d_r = D[dst].rearrange("(kt kp) m -> kp kt m", kp=128)
            for kt in range(kt_n):
                for m0 in range(0, m_n, W):
                    if _done >= NT:
                        continue
                    _done += 1
                    self.dma("gd", xin, s_r[:, kt, m0:m0 + W], outs=["qxin"])
                    self.emit_quant(xin, xout, "qxin", "qxout", W, qt)
                    self.dma("ad", d_r[:, kt, m0:m0 + W], xout,
                             ins=["qxout"], outs=["qso"])
        self.S.barrier()
        self.RBF.reset(); self.RF4.reset()

    def f5_mlpo(self):
        D = self.D
        cst = self.P["consts"].ap()
        self.RBF.reset(); self.RF4.reset()
        oc_r = D["oc_s"].rearrange("(t p) n -> p t n", p=128)
        ao_r = D["ao_s"].rearrange("(t p) n -> p t n", p=128)
        x0_r = D["x0own"].rearrange("(t p) n -> p t n", p=128)
        x1_r = D["x1_s"].rearrange("(t p) n -> p t n", p=128)
        osel = self.RBF.tile(16 * 512)
        asel = self.RBF.tile(16 * 512)
        bounce = self.RBF.tile(512)
        h2ch = self.RBF.tile(16 * 512)
        wcol = [self.RBF.tile(2048) for _ in range(2)]
        x1b = self.RF4.tile(16 * 512)
        sqt = self.RF4.tile(512)
        x0tile = self.RF4.tile(512)
        w_r = D["wmloT"].rearrange("(kt kp) m -> kp kt m", kp=128)
        mr = self.P["mrow"].ap()
        selc = cst[:, 150:151]
        for j in range(4):
            for (buf, srcr, k1) in ((osel, oc_r, "osel"), (asel, ao_r, "asel")):
                for t in range(16):
                    at = buf[:, 512 * t:512 * (t + 1)]
                    self.dma("sd", at, srcr[:, t, CH * j:CH * (j + 1)], outs=[k1])
                    self.dma("sd", bounce, srcr[:, t, CH * (j + 4):CH * (j + 5)],
                             outs=["bnc"])
                    self.dve(lambda e, at=at: e.tensor_sub(bounce, bounce, at),
                             ins=[k1, "bnc"], outs=["bnc"])
                    self.dve(lambda e: e.tensor_scalar(
                        bounce, bounce, selc, None, OP.mult),
                        ins=["bnc", "c_sel"], outs=["bnc"])
                    self.dve(lambda e, at=at: e.tensor_add(at, at, bounce),
                             ins=[k1, "bnc"], outs=[k1])
            for m in range(16):
                wc = wcol[m % 2]
                wkey = f"wc{m % 2}"
                self.dma("gd", wc.rearrange("p (t x) -> p t x", t=16),
                         w_r[:, :, 128 * m:128 * (m + 1)], outs=[wkey])
                bank = self.PS[m % 4]
                bkey = f"ps{m % 4}"
                for k in range(16):
                    self.mm(bank, wc[:, 128 * k:128 * (k + 1)],
                            osel[:, 512 * k:512 * (k + 1)],
                            start=(k == 0), stop=(k == 15),
                            ins=[wkey, "osel"], outs=[bkey])
                xt = x1b[:, 512 * m:512 * (m + 1)]
                at = asel[:, 512 * m:512 * (m + 1)]
                self.dve(lambda e, xt=xt, s=bank, at=at: e.tensor_add(xt, s, at),
                         ins=[bkey, "asel"], outs=["x1b"])
                self.dve(lambda e, xt=xt, m=m: e.tensor_scalar(
                    xt, xt, cst[:, 84 + m:85 + m], None, OP.mult),
                    ins=["x1b", "c_ga"], outs=["x1b"])
                self.dma("sd", x0tile, x0_r[:, m, CH * j:CH * (j + 1)], outs=["x0t"])
                self.dve(lambda e, xt=xt: e.tensor_add(xt, xt, x0tile),
                         ins=["x1b", "x0t"], outs=["x1b"])
                ps_s = self.PS[6][0:1, :]
                ps_q = self.PS[7][0:1, :]
                self.mm(ps_s, cst[:, 151:152], xt, start=(m == 0), stop=(m == 15),
                        ins=["ones_c", "x1b"], outs=["ps6"])
                self.act(lambda e, xt=xt: e.activation(sqt, xt, AF.Square),
                         ins=["x1b"], outs=["sq2"])
                self.mm(ps_q, cst[:, 151:152], sqt, start=(m == 0), stop=(m == 15),
                        ins=["ones_c", "sq2"], outs=["ps7"])
            self.dma("ad", x1_r[:, :, CH * j:CH * (j + 1)],
                     x1b.rearrange("p (t x) -> p t x", t=16), ins=["x1b"],
                     outs=[f"x1{j}"])
            m_ = mr[:, 0:512]; A_ = mr[:, 512:1024]; B_ = mr[:, 1024:1536]
            v_ = mr[:, 1536:2048]
            self.dve(lambda e: e.tensor_scalar_mul(m_, self.PS[6][0:1, :], 1.0 / 2048.0),
                     ins=["ps6"], outs=["m2_"])
            self.dve(lambda e: e.tensor_scalar_mul(v_, self.PS[7][0:1, :], 1.0 / 2048.0),
                     ins=["ps7"], outs=["v2_"])
            self.dve(lambda e: e.scalar_tensor_tensor(
                A_, m_, 1.0, m_, OP.bypass, OP.mult), ins=["m2_"], outs=["A2_"])
            self.dve(lambda e: e.tensor_sub(v_, v_, A_), ins=["v2_", "A2_"], outs=["v2_"])
            self.act(lambda e: e.activation(A_, v_, AF.Sqrt, bias=1e-5),
                     ins=["v2_", "kconst"], outs=["A2_"])
            self.dve(lambda e: e.reciprocal(A_, A_), ins=["A2_"], outs=["A2_"])
            self.dve(lambda e: e.scalar_tensor_tensor(
                B_, m_, -1.0, A_, OP.mult, OP.mult), ins=["m2_", "A2_"], outs=["B2_"])
            psA = self.PS[4]; psB = self.PS[5]
            self.mm(psA, self.P["onesr"].ap(), A_, True, True,
                    ins=["ones_r", "A2_"], outs=["ps4"])
            self.mm(psB, self.P["onesr"].ap(), B_, True, True,
                    ins=["ones_r", "B2_"], outs=["ps5"])
            for m in range(16):
                xt = x1b[:, 512 * m:512 * (m + 1)]
                ht = h2ch[:, 512 * m:512 * (m + 1)]
                self.dve(lambda e, xt=xt: e.tensor_mul(sqt, xt, psA),
                         ins=["x1b", "ps4"], outs=["sq2"])
                self.dve(lambda e: e.tensor_add(sqt, sqt, psB),
                         ins=["sq2", "ps5"], outs=["sq2"])
                self.dve(lambda e, ht=ht, m=m: e.tensor_scalar(
                    ht, sqt, cst[:, 116 + m:117 + m], cst[:, 100 + m:101 + m],
                    OP.mult, OP.add), ins=["sq2", "c_cb1", "c_sb"], outs=["h2ch"])
            self.dma("ad", D["h2_s"].rearrange("(t p) n -> p t n", p=128)[
                :, :, CH * j:CH * (j + 1)],
                h2ch.rearrange("p (t x) -> p t x", t=16), ins=["h2ch"],
                outs=[f"h2s{j}"])
        self.S.barrier()
        self.RBF.reset(); self.RF4.reset()

    def f6_ffn(self):
        D = self.D
        cst = self.P["consts"].ap()
        self.RF4.reset()
        self.RBF.reset(); self.ITI.reset()
        self.h2res = self.RBF.tile(16 * 2048)
        self.RBF.push()
        qt = self.quant_tiles(1024)
        bounce = self.RBF.tile(1024)
        h2_r = D["h2_s"].rearrange("(t p) n -> p t n", p=128)
        for t in range(16):
            for half in range(2):
                xa = self.h2res[:, 2048 * t + 1024 * half:2048 * t + 1024 * (half + 1)]
                self.dma("sd", bounce, h2_r[:, t, 1024 * half:1024 * (half + 1)],
                         outs=["h2b"])
                self.emit_quant(bounce, xa, "h2b", "h2res", 1024, qt)
        self.S.barrier()
        self.RBF.pop()
        self.RF4.reset(); self.ITI.reset()
        wcol = [self.RBF.tile(2048) for _ in range(3)]
        hkbuf = [self.RBF.tile(1024) for _ in range(2)]
        hrow = self.RBF.tile(512)
        hqt = self.RBF.tile(512)
        qt2 = self.quant_tiles(512)
        wg_r = D["wgq_s"].rearrange("(kt kp) m -> kp kt m", kp=128)
        wu_r = D["wuq_s"].rearrange("(kt kp) m -> kp kt m", kp=128)
        wd_r = D["wdq_s"].rearrange("(kt kp) m -> kp kt m", kp=128)
        hq_r = D["hq_s"].rearrange("(kt kp) m -> kp kt m", kp=128)
        x1_r = D["x1_s"].rearrange("(t p) n -> p t n", p=128)
        out_r = D["out"].rearrange("(t p) n -> p t n", p=128)
        x1t = self.RF4.tile(512)
        hrow2 = self.RF4.tile(512)
        for tg in range(4):
            for ht in range(64):
                for gu, wr in ((0, wg_r), (1, wu_r)):
                    ri = (2 * ht + gu) % 3
                    wc = wcol[ri]
                    wkey = f"wq{ri}"
                    self.dma("sd" if gu == 0 else "gd",
                             wc.rearrange("p (t x) -> p t x", t=16),
                             wr[:, :, 128 * ht:128 * (ht + 1)], outs=[wkey])
                    bank = self.PS[gu]
                    bkey = f"ps{gu}"
                    for k in range(16):
                        self.mm(bank, wc[:, 128 * k:128 * (k + 1)],
                                self.h2res[:, 2048 * k + 512 * tg:
                                           2048 * k + 512 * (tg + 1)],
                                start=(k == 0), stop=(k == 15),
                                ins=[wkey, "h2res"], outs=[bkey])
                self.act(lambda e: e.activation(hrow, self.PS[0], AF.Silu),
                         ins=["ps0"], outs=["hrow"])
                self.dve(lambda e: e.tensor_mul(hrow, hrow, self.PS[1]),
                         ins=["hrow", "ps1"], outs=["hrow"])
                self.emit_quant(hrow, hqt, "hrow", "hqt", 512, qt2)
                self.dma("ad", hq_r[:, ht, 512 * tg:512 * (tg + 1)], hqt,
                         ins=["hqt"], outs=["hq_s"])
            for mg in range(4):
                for kt in range(64):
                    par = kt % 2
                    hk = hkbuf[par][:, 0:512]
                    hkk = f"hk{par}"
                    self.dma("sd", hk[:, 0:512], hq_r[:, kt, 512 * tg:512 * (tg + 1)],
                             ins=["hq_s"], outs=[hkk])
                    wdc = hkbuf[par][:, 512:1024]
                    wdk = f"wd{par}"
                    self.dma("gd", wdc.rearrange("p (t x) -> p t x", t=4),
                             wd_r[:, kt, 512 * mg:512 * (mg + 1)].rearrange(
                                 "p (t x) -> p t x", t=4), outs=[wdk])
                    for mi in range(4):
                        bank = self.PS[4 + mi]
                        bkey = f"ps{4 + mi}"
                        self.mm(bank, wdc[:, 128 * mi:128 * (mi + 1)],
                                hk[:, 0:512], start=(kt == 0), stop=(kt == 63),
                                ins=[hkk, wdk], outs=[bkey])
                for mi in range(4):
                    m = 4 * mg + mi
                    bank = self.PS[4 + mi]
                    bkey = f"ps{4 + mi}"
                    self.dma("sd", x1t, x1_r[:, m, CH * tg:CH * (tg + 1)],
                             outs=["x1t"])
                    self.dve(lambda e, bank=bank, m=m: e.tensor_scalar(
                        hrow2, bank, cst[:, 132 + m:133 + m], None, OP.mult),
                        ins=[bkey, "c_gb"], outs=["x2t"])
                    self.dve(lambda e: e.tensor_add(hrow2, hrow2, x1t),
                             ins=["x2t", "x1t"], outs=["x2t"])
                    self.dma("ad", out_r[:, m, CH * tg:CH * (tg + 1)], hrow2,
                             ins=["x2t"], outs=["outd"])
        self.S.barrier()

    def finish(self):
        self._finish_inner()
        mybir.codegen_inst_isa_subclasses(self.nc)

    def _finish_inner(self):
        nc = self.nc
        sems = {k: self.st.enter_context(nc.semaphore(k)) for k in SEMS}
        blk = self.st.enter_context(nc.Block())
        S = self.S

        def replay(stream):
            def run(eng):
                for waits, emit, sem, inc in S.ops[stream]:
                    for psem, pcnt in waits:
                        eng.wait_ge(sems[psem], pcnt)
                    if emit is not None:
                        ins = emit(eng)
                        if sem is not None:
                            ins.then_inc(sems[sem], inc)
            return run

        blk.tensor(replay("pe"))
        blk.vector(replay("dve"))
        blk.scalar(replay("act"))
        blk.gpsimd(replay("gp"))
        blk.sync(replay("sync"))

    def close(self):
        self.st.close()


def build_nc(upto="all"):
    nc = bass.Bass("TRN2", target_bir_lowering=False)
    kb = KB(nc, upto)
    kb.alloc_all()
    kb.f0()
    if upto in ("f0", "f0a", "f0b", "f0c"):
        kb.finish(); return nc, kb
    kb.f1_ln1()
    if upto in ("f1a",):
        kb.finish(); return nc, kb
    kb.f1_qk()
    if upto in ("f1b",):
        kb.finish(); return nc, kb
    kb.f1_v()
    if upto in ("f1v",):
        kb.finish(); return nc, kb
    kb.f2_rope()
    if upto in ("f2a",):
        kb.finish(); return nc, kb
    kb.f2_attn()
    if upto in ("f2b",):
        kb.finish(); return nc, kb
    kb.f2_oproj()
    if upto in ("f2c",):
        kb.finish(); return nc, kb
    kb.f3_mlpqkv()
    if upto in ("f3",):
        kb.finish(); return nc, kb
    kb.f4_ttt()
    if upto in ("f4",):
        kb.finish(); return nc, kb
    kb.f_wquant()
    if upto in ("f4b",):
        kb.finish(); return nc, kb
    kb.f5_mlpo()
    if upto in ("f5",):
        kb.finish(); return nc, kb
    kb.f6_ffn()
    kb.finish()
    return nc, kb


def kernel(**inputs):
    in_maps = host_inputs(inputs)
    nc, kb = build_nc()
    res = run_bass_kernel_spmd(nc, in_maps, core_ids=list(range(8)))
    outs = []
    for core in range(8):
        outs.append(np.asarray(res.results[core]["out"]).T)
    full = np.empty((4, T, H), dtype=np.float32)
    for core in range(8):
        b, s = core // 2, core % 2
        full[b, 2048 * s:2048 * (s + 1)] = outs[core]
    return full
